# version 02
# baseline (speedup 1.0000x reference)
import numpy as np
import ml_dtypes
from contextlib import ExitStack
import concourse.bass as bass
import concourse.mybir as mybir
from concourse.bass_utils import run_bass_kernel_spmd

F32 = mybir.dt.float32
BF16 = mybir.dt.bfloat16
I32 = mybir.dt.int32
I8 = mybir.dt.int8
AF = mybir.ActivationFunctionType
ALU = mybir.AluOpType
AXX = mybir.AxisListType.X
BF = ml_dtypes.bfloat16

NCORES = 8
D = 4096
KT = D // 128
NLOC = 1024
TH = 512
NTH = NLOC // TH



def shard_x(x):
    out = []
    for c in range(NCORES):
        b, q = divmod(c, 4)
        out.append(np.ascontiguousarray(x[b, q * NLOC:(q + 1) * NLOC, :].T))
    return out


def unshard_x(parts):
    x = np.empty((2, 4096, 4096), np.float32)
    for c in range(NCORES):
        b, q = divmod(c, 4)
        x[b, q * NLOC:(q + 1) * NLOC, :] = parts[c].T
    return x


def wblocks_km(w_km, mc=512):
    K, M = w_km.shape
    return np.ascontiguousarray(w_km.reshape(K, M // mc, mc).transpose(1, 0, 2))


def bias_cols(b):
    return np.ascontiguousarray(b.reshape(-1, 128).T)


def eye_groups():
    e = np.zeros((128, KT * 32), np.float32)
    for kt in range(KT):
        e[:, kt * 32 + kt] = 1.0
    return e.astype(BF)


def threefry_mask_sel():
    def rotl(x, d):
        return ((x << np.uint32(d)) | (x >> np.uint32(32 - d))).astype(np.uint32)

    def block(k0, k1, x0, x1):
        rots = [[13, 15, 26, 6], [17, 29, 16, 24]]
        ks2 = np.uint32(0x1BD11BDA) ^ k0 ^ k1
        x0 = (x0 + k0).astype(np.uint32)
        x1 = (x1 + k1).astype(np.uint32)
        ks = [k0, k1, ks2]
        for i in range(5):
            for dd in rots[i % 2]:
                x0 = (x0 + x1).astype(np.uint32)
                x1 = rotl(x1, dd) ^ x0
            x0 = (x0 + ks[(i + 1) % 3]).astype(np.uint32)
            x1 = (x1 + ks[(i + 2) % 3] + np.uint32(i + 1)).astype(np.uint32)
        return x0, x1

    n = 2 * 4096 * 4096
    idx = np.arange(n, dtype=np.uint64)
    c1 = (idx >> np.uint64(32)).astype(np.uint32)
    c2 = (idx & np.uint64(0xFFFFFFFF)).astype(np.uint32)
    del idx
    o0, o1 = block(np.uint32(0), np.uint32(42), c1, c2)
    bits = o0 ^ o1
    del c1, c2, o0, o1
    u = ((bits >> np.uint32(9)) | np.uint32(0x3F800000)).view(np.float32) - np.float32(1.0)
    sel = np.minimum((u * 4).astype(np.int32), 3).astype(np.int8)
    return sel.reshape(2, 4096, 4096)



_NP2DT = {np.dtype(np.float32): F32, np.dtype(np.int32): I32,
          np.dtype(BF): BF16, np.dtype(np.int8): I8}


class Env:
    def __init__(self):
        self.nc = bass.Bass("TRN2", target_bir_lowering=False, debug=False)
        self.stack = ExitStack()
        self.sems = {}
        self.cnt = {}
        self.in_maps = [dict() for _ in range(NCORES)]
        self.out_names = []
        self.block = None
        self._uid = 0

    def uid(self):
        self._uid += 1
        return self._uid

    def din(self, name, arrs):
        if not isinstance(arrs, (list, tuple)):
            arrs = [arrs] * NCORES
        a0 = np.ascontiguousarray(arrs[0])
        t = self.nc.dram_tensor(name, list(a0.shape), _NP2DT[a0.dtype], kind="ExternalInput").ap()
        for c in range(NCORES):
            self.in_maps[c][name] = np.ascontiguousarray(arrs[c])
        return t

    def dout(self, name, shape, dtype=F32):
        self.out_names.append(name)
        return self.nc.dram_tensor(name, list(shape), dtype, kind="ExternalOutput").ap()

    def dram(self, name, shape, dtype=F32):
        return self.nc.dram_tensor(name, list(shape), dtype, kind="Internal").ap()

    def sbuf(self, name, shape, dtype, stack=None):
        return (stack or self.stack).enter_context(self.nc.sbuf_tensor(name, list(shape), dtype))

    def psum(self, name, shape=(128, 512)):
        return self.stack.enter_context(self.nc.psum_tensor(name, list(shape), F32))

    def sem(self, name):
        if name not in self.sems:
            self.sems[name] = self.stack.enter_context(self.nc.semaphore(name))
            self.cnt[name] = 0
        return self.sems[name]

    def inc(self, inst, name, k=1):
        inst.then_inc(self.sem(name), k)
        self.cnt[name] += k
        return self.cnt[name]

    def run(self, trace=False):
        res = run_bass_kernel_spmd(self.nc, self.in_maps, core_ids=list(range(NCORES)), trace=trace)
        return res


class DmaRing:

    def __init__(self, E, name, nslots):
        self.E = E
        self.name = name
        self.n = nslots
        self.fills = [0] * nslots
        for i in range(nslots):
            E.sem(name + str(i))

    def fill(self, eng_inst, slot):
        self.E.inc(eng_inst, self.name + str(slot), 16)
        self.fills[slot] += 1

    def wait_filled(self, eng, slot):
        eng.wait_ge(self.E.sem(self.name + str(slot)), 16 * self.fills[slot])


class PsumRing:

    def __init__(self, E, banks, mm_sem, ev_sem):
        self.E = E
        self.banks = banks
        self.mm = mm_sem
        self.ev = ev_sem
        self.tile = 0

    def acquire(self, t):
        b = self.banks[self.tile % len(self.banks)]
        need = self.tile - len(self.banks) + 1
        if need > 0:
            t.wait_ge(self.E.sem(self.ev), need)
        return b

    def finish_tile(self, last_mm_inst):
        self.E.inc(last_mm_inst, self.mm, 1)
        self.tile += 1
        return self.tile



def emit_ln(E, p, x_hbm, xl, sq, stat, rstd_bc, mbc_sb, mid, rbf_row, ones_col, ones_row,
            psA, psB, psBC, eps_col):
    nc = E.nc
    ld = p + "ld"; smm = p + "smm"; sev = p + "sev"; sqd = p + "sqd"; cen = p + "cen"

    @E.block.gpsimd
    def _(g):
        E.inc(g.dma_start(out=xl[:, :],
                          in_=x_hbm.rearrange("(kt p) n -> p kt n", p=128)), ld, 16)

    base_ld = E.cnt[ld]

    @E.block.tensor
    def _(t):
        t.wait_ge(E.sem(ld), base_ld)
        for th in range(NTH):
            ps = psA if th == 0 else psB
            for kt in range(KT):
                inst = t.matmul(ps[0:1, :], ones_col[:, 0:1], xl[:, kt * NLOC + th * TH: kt * NLOC + th * TH + TH],
                                start=(kt == 0), stop=(kt == KT - 1))
            E.inc(inst, smm, 1)

    @E.block.vector
    def _(v):
        v.wait_ge(E.sem(smm), E.cnt[smm])
        v.tensor_scalar(stat[0:1, 0:TH], psA[0:1, :], 1.0 / D, None, ALU.mult)
        inst = v.tensor_scalar(stat[0:1, TH:NLOC], psB[0:1, :], 1.0 / D, None, ALU.mult)
        E.inc(inst, sev, 1)

    E.sem(sqd)

    @E.block.scalar
    def _(s):
        s.wait_ge(E.sem(ld), base_ld)
        for kt in range(KT):
            if kt >= 2:
                s.wait_ge(E.sem(sqd), kt - 1)
            inst = s.activation(sq[:, (kt % 2) * NLOC:(kt % 2 + 1) * NLOC],
                                xl[:, kt * NLOC:(kt + 1) * NLOC], AF.Square)
            E.inc(inst, p + "sqa", 1)

    @E.block.tensor
    def _(t):
        t.wait_ge(E.sem(sev), E.cnt[sev])
        for kt in range(KT):
            t.wait_ge(E.sem(p + "sqa"), kt + 1)
            for th in range(NTH):
                ps = psA if th == 0 else psB
                inst = t.matmul(ps[0:1, :], ones_col[:, 0:1],
                                sq[:, (kt % 2) * NLOC + th * TH: (kt % 2) * NLOC + th * TH + TH],
                                start=(kt == 0), stop=(kt == KT - 1))
            if kt < KT - 1:
                E.inc(inst, sqd, 1)
            else:
                E.inc(inst, smm, 1)

    @E.block.vector
    def _(v):
        v.wait_ge(E.sem(smm), E.cnt[smm])
        v.tensor_tensor(stat[0:1, 3 * NLOC: 3 * NLOC + NLOC], stat[0:1, 0:NLOC], stat[0:1, 0:NLOC], ALU.mult)
        v.tensor_scalar(stat[0:1, NLOC:NLOC + TH], psA[0:1, :], 1.0 / D, None, ALU.mult)
        v.tensor_scalar(stat[0:1, NLOC + TH:2 * NLOC], psB[0:1, :], 1.0 / D, None, ALU.mult)
        inst = v.tensor_tensor(stat[0:1, NLOC:2 * NLOC], stat[0:1, NLOC:2 * NLOC],
                               stat[0:1, 3 * NLOC:4 * NLOC], ALU.subtract)
        E.inc(inst, sev, 1)

    @E.block.scalar
    def _(s):
        s.wait_ge(E.sem(sev), E.cnt[sev])
        s.activation(stat[0:1, 2 * NLOC:3 * NLOC], stat[0:1, NLOC:2 * NLOC], AF.Ln, bias=eps_col[0:1, 0:1])
        inst = s.activation(stat[0:1, 2 * NLOC:3 * NLOC], stat[0:1, 2 * NLOC:3 * NLOC], AF.Exp, scale=-0.5)
        E.inc(inst, p + "rst", 1)

    @E.block.vector
    def _(v):
        v.wait_ge(E.sem(p + "rst"), E.cnt[p + "rst"])
        v.tensor_copy(rbf_row[0:1, :], stat[0:1, 2 * NLOC:3 * NLOC])
        v.tensor_copy(stat[0:1, NLOC:2 * NLOC], rbf_row[0:1, :])
        inst = v.tensor_tensor(stat[0:1, 3 * NLOC:4 * NLOC], stat[0:1, 0:NLOC],
                               stat[0:1, NLOC:2 * NLOC], ALU.mult)
        E.inc(inst, p + "mr", 1)

    E.sem(cen)

    @E.block.tensor
    def _(t):
        t.wait_ge(E.sem(p + "mr"), E.cnt[p + "mr"])
        for j, src_off in enumerate([NLOC, 3 * NLOC]):
            for th in range(NTH):
                idx = 2 * j + th
                if idx > 0:
                    t.wait_ge(E.sem(cen), idx)
                inst = t.matmul(psBC[:, :], ones_row[0:1, :], stat[0:1, src_off + th * TH: src_off + th * TH + TH],
                                start=True, stop=True)
                E.inc(inst, p + "bcm", 1)

    @E.block.vector
    def _(v):
        for j, dst in enumerate([rstd_bc, mbc_sb]):
            for th in range(NTH):
                v.wait_ge(E.sem(p + "bcm"), 2 * j + th + 1)
                inst = v.tensor_copy(dst[:, th * TH:(th + 1) * TH], psBC[:, :])
                E.inc(inst, cen, 1)

    @E.block.vector
    def _(v):
        v.wait_ge(E.sem(cen), E.cnt[cen])
        for kt in range(KT):
            for th in range(NTH):
                sl = slice(kt * NLOC + th * TH, kt * NLOC + th * TH + TH)
                v.tensor_tensor(mid[:, :], xl[:, sl], rstd_bc[:, th * TH:(th + 1) * TH], ALU.mult)
                inst = v.tensor_tensor(xl[:, sl], mid[:, :], mbc_sb[:, th * TH:(th + 1) * TH], ALU.subtract)
        E.inc(inst, p + "xln", 1)
    return p + "xln"


def emit_wstream_proj(E, p, xin, xout, wdram, rhs_sb, rhs_off, nkt, NMB, common,
                      bias_sb=None, extra_mm=None, ready=(), wait_dq=None,
                      dq_hook=None, out_scale=None):
    MJ = 4
    mm_banks = common["mm_banks"]
    wb = E.sbuf(p + "wb", [128, 2 * nkt * 512], BF16)
    xres = E.sbuf(p + "xres", [128, 4 * TH], F32)
    outsb = E.sbuf(p + "out", [128, 4 * TH], F32)
    wfree = p + "wfree"; mm = p + "mm"; ev = p + "ev"
    for i in range(2):
        E.sem(p + "wld%d" % i)
    for i in range(4):
        E.sem(p + "rld%d" % i)
        E.sem(p + "st%d" % i)
    E.sem(wfree)
    ring = PsumRing(E, mm_banks, mm, ev)
    total_tiles = NMB * 4 * NTH
    xin_t = xin.rearrange("(mt pp) n -> mt pp n", pp=128)
    xout_t = xout.rearrange("(mt pp) n -> mt pp n", pp=128)

    def tile_coords(j):
        mb, r = divmod(j, 4 * NTH)
        mj, th = divmod(r, NTH)
        return mb, mj, th

    @E.block.sync
    def _(s):
        for j in range(4):
            mb, mj, th = tile_coords(j)
            m = mb * 4 + mj
            E.inc(s.dma_start(out=xres[:, (j % 4) * TH:(j % 4 + 1) * TH],
                              in_=xin_t[m, :, th * TH:(th + 1) * TH]), p + "rld%d" % (j % 4), 16)

    @E.block.gpsimd
    def _(g):
        for mb in range(NMB):
            if mb >= 2:
                g.wait_ge(E.sem(ev), (mb - 1) * 4 * NTH)
            E.inc(g.dma_start(out=wb.ap().rearrange("pp (b k) -> pp b k", b=2)[:, mb % 2, :],
                              in_=wdram[mb, :, :].rearrange("(kt p) c -> p kt c", p=128)),
                  p + "wld%d" % (mb % 2), 16)

    def emit_main(t, mb):
        t.wait_ge(E.sem(p + "wld%d" % (mb % 2)), 16 * (mb // 2 + 1))
        if wait_dq is not None:
            wait_dq(t, mb)
        if mb == 0:
            for sname, val in ready:
                t.wait_ge(E.sem(sname), val)
        for mj in range(4):
            for th in range(NTH):
                ps = ring.acquire(t)
                woff = (mb % 2) * nkt * 512 + mj * 128
                for kt in range(nkt):
                    minst = t.matmul(ps[:, :], wb[:, woff + kt * 512: woff + kt * 512 + 128],
                                     rhs_sb[:, rhs_off(kt, th): rhs_off(kt, th) + TH],
                                     start=(kt == 0), stop=(extra_mm is None and kt == nkt - 1))
                if extra_mm is not None:
                    inst = extra_mm(t, ps, mb, mj, th)
                else:
                    inst = minst
                ring.finish_tile(inst)

    def emit_evac(v, mb):
        if mb == 0:
            v.wait_ge(E.sem("preld"), E.cnt["preld"])
        for mj in range(4):
            for th in range(NTH):
                j = mb * 8 + mj * NTH + th
                v.wait_ge(E.sem(mm), j + 1)
                v.wait_ge(E.sem(p + "rld%d" % (j % 4)), 16 * (j // 4 + 1))
                if j >= 4:
                    v.wait_ge(E.sem(p + "st%d" % (j % 4)), 16 * (j // 4))
                ps = ring.banks[j % len(ring.banks)]
                m = mb * 4 + mj
                bias_ap = bias_sb[:, m:m + 1] if bias_sb is not None else 0.0
                inst = v.scalar_tensor_tensor(outsb[:, (j % 4) * TH:(j % 4 + 1) * TH],
                                              ps[:, :], bias_ap,
                                              xres[:, (j % 4) * TH:(j % 4 + 1) * TH],
                                              ALU.add, ALU.add)
                E.inc(inst, ev, 1)

    def emit_store(s, mb):
        for mj in range(MJ):
            for th in range(NTH):
                j = mb * MJ * NTH + mj * NTH + th
                s.wait_ge(E.sem(ev), j + 1)
                mb2, mj2, th2 = tile_coords(j)
                m = mb2 * MJ + mj2
                E.inc(s.dma_start(out=xout_t[m, :, th2 * TH:(th2 + 1) * TH],
                                  in_=outsb[:, (j % 4) * TH:(j % 4 + 1) * TH]), p + "st%d" % (j % 4), 16)
                jn = j + 4
                if jn < total_tiles:
                    mbn, mjn, thn = tile_coords(jn)
                    mn = mbn * MJ + mjn
                    E.inc(s.dma_start(out=xres[:, (jn % 4) * TH:(jn % 4 + 1) * TH],
                                      in_=xin_t[mn, :, thn * TH:(thn + 1) * TH]), p + "rld%d" % (jn % 4), 16)

    for mb in range(NMB):
        @E.block.vector
        def _(v, mb=mb):
            if mb > 0:
                emit_evac(v, mb - 1)
            if dq_hook is not None:
                dq_hook(v, mb)

        @E.block.tensor
        def _(t, mb=mb):
            emit_main(t, mb)

        @E.block.sync
        def _(s, mb=mb):
            if mb > 0:
                emit_store(s, mb - 1)

    @E.block.vector
    def _(v):
        emit_evac(v, NMB - 1)

    @E.block.sync
    def _(s):
        emit_store(s, NMB - 1)

    @E.block.sync
    def _(s):
        for i in range(4):
            s.wait_ge(E.sem(p + "st%d" % i), E.cnt[p + "st%d" % i])


def _l0ffn_host(inp):
    Wb = wblocks_km(np.ascontiguousarray(inp["L0_ffn_W_base"].T))
    routerT = np.ascontiguousarray(inp["L0_ffn_W_router"].T)
    acat = np.ascontiguousarray(inp["L0_ffn_A"].transpose(2, 0, 1).reshape(4096, 128))
    bcat = np.ascontiguousarray(inp["L0_ffn_B"].transpose(0, 2, 1).reshape(128, 4096))
    return Wb, routerT, acat, bcat, bias_cols(inp["L0_ffn_b_base"])


def emit_l0ffn(E, p, xin, xout, wdram, router_d, acat_d, bcat_d, bias_sb_name,
               xl_ready, xl, common):
    (psA, psB, psBC, ones_col) = (common["psA"], common["psB"], common["psBC"], common["ones_col"])
    eye8 = common["eye8"]; eye128 = common["eye128"]; e16 = common["e16"]

    router_sb = E.sbuf(p + "rt", [128, KT * 8], BF16)
    acat_sb = E.sbuf(p + "ac", [128, KT * 128], BF16)
    bcat_sb = E.sbuf(p + "bc", [128, D], BF16)
    lgb = E.sbuf(p + "lgb", [8, NLOC], F32)
    lgT = E.sbuf(p + "lgT", [128, 64], F32)
    msk = E.sbuf(p + "msk", [128, 64], F32)
    eqm = E.sbuf(p + "eqm", [128, 64], F32)
    itp = E.sbuf(p + "itp", [128, 64], F32)
    et = E.sbuf(p + "et", [128, 64], F32)
    cun = E.sbuf(p + "cun", [128, 64], F32)
    m1s = E.sbuf(p + "m1s", [128, 8], F32)
    m1g = E.sbuf(p + "m1g", [128, 8], F32)
    m2g = E.sbuf(p + "m2g", [128, 8], F32)
    densg = E.sbuf(p + "densg", [128, 8], F32)
    recsg = E.sbuf(p + "recsg", [128, 8], F32)
    nm1s = E.sbuf(p + "nm1s", [128, 8], F32)
    m2s = E.sbuf(p + "m2s", [128, 8], F32)
    dens = E.sbuf(p + "dens", [128, 8], F32)
    recs = E.sbuf(p + "recs", [128, 8], F32)
    cmb = E.sbuf(p + "cmb", [128, 64], F32)
    cem = E.sbuf(p + "cem", [8, NLOC], BF16)
    cexp = E.sbuf(p + "cexp", [128, NLOC], BF16)
    hw = E.sbuf(p + "hw", [128, NLOC], BF16)
    dbg0 = E.sbuf(p + "dbg0", [128, 32], F32)

    ld = p + "ld"; rmm = p + "rmm"; lgv = p + "lgv"; tmm = p + "tmm"; tcv = p + "tcv"
    v1 = p + "v1"; a1 = p + "a1"; btm = p + "btm"; btv = p + "btv"
    exm = p + "exm"; exv = p + "exv"; hmm_ = p + "hmm"; hwv = p + "hwv"

    @E.block.gpsimd
    def _(g):
        E.inc(g.dma_start(out=router_sb[:, :], in_=router_d.rearrange("(kt pp) e -> pp kt e", pp=128)), ld, 16)
        E.inc(g.dma_start(out=acat_sb[:, :], in_=acat_d.rearrange("(kt pp) m -> pp kt m", pp=128)), ld, 16)
        E.inc(g.dma_start(out=bcat_sb[:, :], in_=bcat_d[:, :]), ld, 16)

    ld_total = E.cnt[ld]

    @E.block.tensor
    def _(t):
        t.wait_ge(E.sem(xl_ready), E.cnt[xl_ready])
        t.wait_ge(E.sem(ld), ld_total)
        t.wait_ge(E.sem("preld"), E.cnt["preld"])
        for th in range(NTH):
            ps = psA if th == 0 else psB
            for kt in range(KT):
                inst = t.matmul(ps[0:8, :], router_sb[:, kt * 8:(kt + 1) * 8],
                                xl[:, kt * NLOC + th * TH: kt * NLOC + th * TH + TH],
                                start=(kt == 0), stop=(kt == KT - 1))
            E.inc(inst, rmm, 1)

    @E.block.vector
    def _(v):
        v.wait_ge(E.sem(rmm), 2)
        v.tensor_copy(lgb[:, 0:TH], psA[0:8, :])
        inst = v.tensor_copy(lgb[:, TH:NLOC], psB[0:8, :])
        E.inc(inst, lgv, 1)

    for tt in range(8):
        @E.block.tensor
        def _(t, tt=tt):
            if tt == 0:
                t.wait_ge(E.sem(lgv), 1)
            if tt >= 2:
                t.wait_ge(E.sem(tcv), tt - 1)
            ps = psA if tt % 2 == 0 else psB
            E.inc(t.transpose(ps[0:128, 0:8],
                              lgb[0:8, tt * 128:(tt + 1) * 128], eye8[0:8, 0:8]), tmm, 1)

        @E.block.vector
        def _(v, tt=tt):
            v.wait_ge(E.sem(tmm), tt + 1)
            ps = psA if tt % 2 == 0 else psB
            E.inc(v.tensor_copy(lgT[:, tt * 8:(tt + 1) * 8], ps[0:128, 0:8]), tcv, 1)

    tv = p + "tv"; tg = p + "tg"

    @E.block.vector
    def _(v):
        for tt in range(8):
            inst = v.reduce_max(m1s[:, tt:tt + 1], lgT[:, tt * 8:(tt + 1) * 8], axis=AXX)
        E.inc(inst, tv, 1)

    @E.block.gpsimd
    def _(g):
        g.wait_ge(E.sem(tv), 1)
        E.inc(g.tensor_copy(m1g[:, :], m1s[:, :]), tg, 1)

    @E.block.vector
    def _(v):
        v.wait_ge(E.sem(tg), 1)
        v.tensor_scalar(nm1s[:, :], m1g[:, :], -1.0, None, ALU.mult)
        for tt in range(8):
            inst = v.tensor_scalar(eqm[:, tt * 8:(tt + 1) * 8], lgT[:, tt * 8:(tt + 1) * 8],
                                   m1g[:, tt:tt + 1], None, ALU.is_equal)
        E.inc(inst, tv, 1)

    @E.block.gpsimd
    def _(g):
        g.wait_ge(E.sem(tv), 2)
        for tt in range(8):
            g.tensor_scalar(msk[:, tt * 8:(tt + 1) * 8], eqm[:, tt * 8:(tt + 1) * 8], -1e30, None, ALU.mult)
        for tt in range(8):
            inst = g.tensor_tensor(msk[:, tt * 8:(tt + 1) * 8], msk[:, tt * 8:(tt + 1) * 8],
                                   lgT[:, tt * 8:(tt + 1) * 8], ALU.add)
        E.inc(inst, tg, 1)

    @E.block.vector
    def _(v):
        v.wait_ge(E.sem(tg), 2)
        for tt in range(8):
            inst = v.reduce_max(m2s[:, tt:tt + 1], msk[:, tt * 8:(tt + 1) * 8], axis=AXX)
        E.inc(inst, tv, 1)

    @E.block.gpsimd
    def _(g):
        g.wait_ge(E.sem(tv), 3)
        E.inc(g.tensor_copy(m2g[:, :], m2s[:, :]), tg, 1)

    @E.block.scalar
    def _(s):
        s.wait_ge(E.sem(tv), 2)
        for tt in range(8):
            inst = s.activation(et[:, tt * 8:(tt + 1) * 8], lgT[:, tt * 8:(tt + 1) * 8],
                                AF.Exp, bias=nm1s[:, tt:tt + 1])
        E.inc(inst, a1, 1)

    @E.block.vector
    def _(v):
        v.wait_ge(E.sem(tg), 3)
        for tt in range(8):
            inst = v.tensor_scalar(itp[:, tt * 8:(tt + 1) * 8], lgT[:, tt * 8:(tt + 1) * 8],
                                   m2g[:, tt:tt + 1], None, ALU.is_ge)
        E.inc(inst, tv, 1)

    @E.block.gpsimd
    def _(g):
        g.wait_ge(E.sem(tv), 4)
        g.wait_ge(E.sem(a1), 1)
        E.inc(g.tensor_tensor(cun[:, :], et[:, :], itp[:, :], ALU.mult), tg, 1)

    @E.block.vector
    def _(v):
        v.wait_ge(E.sem(tg), 4)
        for tt in range(8):
            inst = v.reduce_sum(dens[:, tt:tt + 1], cun[:, tt * 8:(tt + 1) * 8], axis=AXX)
        E.inc(inst, tv, 1)

    @E.block.gpsimd
    def _(g):
        g.wait_ge(E.sem(tv), 5)
        E.inc(g.tensor_copy(densg[:, :], dens[:, :]), tg, 1)

    @E.block.vector
    def _(v):
        v.wait_ge(E.sem(tg), 5)
        E.inc(v.reciprocal(recs[:, :], densg[:, :]), tv, 1)

    @E.block.gpsimd
    def _(g):
        g.wait_ge(E.sem(tv), 6)
        E.inc(g.tensor_copy(recsg[:, :], recs[:, :]), tg, 1)

    @E.block.vector
    def _(v):
        v.wait_ge(E.sem(tg), 6)
        for tt in range(8):
            inst = v.tensor_scalar(cmb[:, tt * 8:(tt + 1) * 8], cun[:, tt * 8:(tt + 1) * 8],
                                   recsg[:, tt:tt + 1], 2.0, ALU.mult, ALU.mult)
        if hasattr(E, "_dbg"):
            v.tensor_copy(dbg0[:, 0:8], m1s[:, :])
            v.tensor_copy(dbg0[:, 8:16], itp[:, 0:8])
            v.tensor_copy(dbg0[:, 16:24], cun[:, 0:8])
            inst = v.tensor_copy(dbg0[:, 24:32], et[:, 0:8])
        E.inc(inst, btv, 8)

    for tt in range(8):
        @E.block.tensor
        def _(t, tt=tt):
            t.wait_ge(E.sem(btv), tt + 1)
            if tt == 0:
                t.wait_ge(E.sem(tcv), 8)
            if tt >= 2:
                t.wait_ge(E.sem(exv), tt - 1)
            ps = psA if tt % 2 == 0 else psB
            E.inc(t.transpose(ps[0:8, 0:128],
                              cmb[:, tt * 8:(tt + 1) * 8], eye128[0:128, 0:128]), btm, 1)

        @E.block.vector
        def _(v, tt=tt):
            v.wait_ge(E.sem(btm), tt + 1)
            ps = psA if tt % 2 == 0 else psB
            E.inc(v.tensor_copy(cem[0:8, tt * 128:(tt + 1) * 128], ps[0:8, 0:128]), exv, 1)

    for th in range(NTH):
        @E.block.tensor
        def _(t, th=th):
            t.wait_ge(E.sem(exv), 8)
            if th == 1:
                t.wait_ge(E.sem(hwv), 1)
            E.inc(t.matmul(psBC[:, :], e16[0:8, 0:128], cem[0:8, th * TH:(th + 1) * TH],
                           start=True, stop=True), exm, 1)

        @E.block.vector
        def _(v, th=th):
            v.wait_ge(E.sem(exm), th + 1)
            E.inc(v.tensor_copy(cexp[:, th * TH:(th + 1) * TH], psBC[:, :]), hwv, 1)

    for th in range(NTH):
        @E.block.tensor
        def _(t, th=th):
            if th == 0:
                t.wait_ge(E.sem(exv), 8)
            ps = psA if th == 0 else psB
            for kt in range(KT):
                inst = t.matmul(ps[:, :], acat_sb[:, kt * 128:(kt + 1) * 128],
                                xl[:, kt * NLOC + th * TH: kt * NLOC + th * TH + TH],
                                start=(kt == 0), stop=(kt == KT - 1))
            E.inc(inst, hmm_, 1)

        @E.block.vector
        def _(v, th=th):
            v.wait_ge(E.sem(hmm_), th + 1)
            v.wait_ge(E.sem(hwv), th + 2)
            E.inc(v.tensor_tensor(hw[:, th * TH:(th + 1) * TH],
                                  (psA if th == 0 else psB)[:, :], cexp[:, th * TH:(th + 1) * TH], ALU.mult), hwv, 1)

    hw_total = E.cnt[hwv]

    if hasattr(E, "_dbg"):
        d_lgT, d_cmb, d_cexp, d_hw, d_d0 = E._dbg

        @E.block.gpsimd
        def _(s):
            s.wait_ge(E.sem(hwv), hw_total)
            E.inc(s.dma_start(out=d_lgT[:, :], in_=lgT[:, :]), p + "dbgst", 16)
            E.inc(s.dma_start(out=d_cmb[:, :], in_=cmb[:, :]), p + "dbgst", 16)
            E.inc(s.dma_start(out=d_cexp[:, :], in_=cexp[:, :]), p + "dbgst", 16)
            E.inc(s.dma_start(out=d_hw[:, :], in_=hw[:, :]), p + "dbgst", 16)
            E.inc(s.dma_start(out=d_d0[:, :], in_=dbg0[:, :]), p + "dbgst", 16)
            s.wait_ge(E.sem(p + "dbgst"), 80)

    def extra_mm(t, ps, mb, mj, th):
        m = mb * 4 + mj
        return t.matmul(ps[:, :], bcat_sb[:, m * 128:(m + 1) * 128],
                        hw[:, th * TH:(th + 1) * TH], start=False, stop=True)

    emit_wstream_proj(E, p + "w_", xin, xout, wdram, xl,
                      lambda kt, th: kt * NLOC + th * TH, KT, 8, common,
                      bias_sb=common[bias_sb_name], extra_mm=extra_mm,
                      ready=((hwv, hw_total),))


def build_l0ffn(x_arrs, inp):
    E = Env()
    Wb, routerT, acat, bcat, bias = _l0ffn_host(inp)
    with E.stack:
        xin = E.din("xin", x_arrs)
        wb_d = E.din("wb_d", Wb)
        rt_d = E.din("rt_d", routerT)
        ac_d = E.din("ac_d", acat)
        bc_d = E.din("bc_d", bcat)
        eye_d = E.din("eye_d", eye_groups())
        bias_d = E.din("bias_d", bias)
        e8 = np.eye(8, dtype=np.float32)
        e128 = np.eye(128, dtype=np.float32)
        e16m = np.zeros((8, 128), np.float32)
        for e in range(8):
            e16m[e, e * 16:(e + 1) * 16] = 1.0
        eye8_d = E.din("eye8_d", e8)
        eye128_d = E.din("eye128_d", e128)
        e16_d = E.din("e16_d", e16m.astype(BF))
        xout = E.dout("xout", [D, NLOC], F32)

        common = _common_alloc(E)
        common["bias_sb"] = E.sbuf("bias_sb", [128, 32], F32)
        common["eye8"] = E.sbuf("eye8", [8, 8], F32)
        common["eye128"] = E.sbuf("eye128", [128, 128], F32)
        common["e16"] = E.sbuf("e16", [8, 128], BF16)
        xl = E.sbuf("xl", [128, KT * NLOC], BF16)
        sq = E.sbuf("sq", [128, 2 * NLOC], BF16)
        stat = E.sbuf("stat", [1, 4 * NLOC], F32)
        rstd_bc = E.sbuf("rstd_bc", [128, NLOC], BF16)
        mbc_sb = E.sbuf("mbc_sb", [128, NLOC], F32)
        mid = E.sbuf("mid", [128, TH], F32)
        rbf_row = E.sbuf("rbf_row", [1, NLOC], BF16)

        with E.nc.Block() as block:
            E.block = block
            emit_preamble(E, common, eye_d)

            @E.block.sync
            def _(s):
                E.inc(s.dma_start(out=common["bias_sb"][:, :], in_=bias_d[:, :]), "preld", 16)
                E.inc(s.dma_start(out=common["eye8"][:, :], in_=eye8_d[:, :]), "preld", 16)
                E.inc(s.dma_start(out=common["eye128"][:, :], in_=eye128_d[:, :]), "preld", 16)
                E.inc(s.dma_start(out=common["e16"][:, :], in_=e16_d[:, :]), "preld", 16)

            xl_ready = emit_ln(E, "ln_", xin, xl, sq, stat, rstd_bc, mbc_sb, mid, rbf_row,
                               common["ones_col"], common["ones_row"],
                               common["psA"], common["psB"], common["psBC"], common["eps_col"])
            emit_l0ffn(E, "f0_", xin, xout, wb_d, rt_d, ac_d, bc_d, "bias_sb",
                       xl_ready, xl, common)
        return E


def run_l0ffn(x, inp):
    x_arrs = shard_x(x)
    E = build_l0ffn(x_arrs, inp)
    res = E.run()
    return unshard_x([res.results[c]["xout"] for c in range(NCORES)])


def _l2ffn_host(inp):
    Wb = wblocks_km(np.ascontiguousarray(inp["L2_ffn_W0"].T), 256)
    A = inp["L2_ffn_A"]
    Bm = inp["L2_ffn_B"]
    acat = np.zeros((4096, 64), np.float32)
    bpad = np.zeros((16, 4 * 4096), np.float32)
    for l in range(4):
        acat[:, 16 * l:16 * l + 16] = A[l].T
        bpad[0:16, l * 4096:(l + 1) * 4096] = Bm[l].T
    return Wb, acat, bpad, bias_cols(inp["L2_ffn_b0"])


def emit_l2ffn(E, p, xin, xout, wdram, acat_d, bpad_d, sel_d, bias_sb_name,
               xl_ready, xl, common):
    (psA, psB, psBC, mm_banks, ones_col) = (
        common["psA"], common["psB"], common["psBC"], common["mm_banks"], common["ones_col"])

    acat_sb = E.sbuf(p + "ac", [128, KT * 64], BF16)
    bpad_sb = E.sbuf(p + "bp", [16, 4 * D], BF16)
    hsb = E.sbuf(p + "h", [16, 4 * NLOC], BF16)
    selsb = E.sbuf(p + "sel", [128, 4 * TH], I8)
    eqt = E.sbuf(p + "eq", [128, TH], BF16)
    yt = E.sbuf(p + "yt", [128, TH], F32)
    xres = E.sbuf(p + "xres", [128, 4 * TH], F32)
    outsb = E.sbuf(p + "out", [128, 4 * TH], F32)
    wb = E.sbuf(p + "wb", [128, 2 * KT * 256], BF16)

    ld = p + "ld"; hmm_ = p + "hmm"; hev = p + "hev"
    wfree = p + "wfree"; mm = p + "mm"; ev = p + "ev"; ymm = p + "ymm"; yev = p + "yev"
    for i in range(2):
        E.sem(p + "wld%d" % i)
    for i in range(4):
        E.sem(p + "rld%d" % i)
        E.sem(p + "st%d" % i)
        E.sem(p + "sld%d" % i)

    ring = PsumRing(E, mm_banks, mm, ev)
    NMB = 16
    MJ = 2
    total_tiles = NMB * MJ * NTH
    xin_t = xin.rearrange("(mt pp) n -> mt pp n", pp=128)
    xout_t = xout.rearrange("(mt pp) n -> mt pp n", pp=128)

    def tile_coords(j):
        mb, r = divmod(j, MJ * NTH)
        mj, th = divmod(r, NTH)
        return mb, mj, th

    @E.block.gpsimd
    def _(g):
        E.inc(g.dma_start(out=acat_sb[:, :], in_=acat_d.rearrange("(kt pp) m -> pp kt m", pp=128)), ld, 16)
        E.inc(g.dma_start(out=bpad_sb[:, :], in_=bpad_d[:, :]), ld, 16)

    sel_t = sel_d.rearrange("(mt pp) n -> mt pp n", pp=128)

    @E.block.sync
    def _(s):
        for j in range(4):
            mb, mj, th = tile_coords(j)
            m = mb * MJ + mj
            E.inc(s.dma_start(out=xres[:, (j % 4) * TH:(j % 4 + 1) * TH],
                              in_=xin_t[m, :, th * TH:(th + 1) * TH]), p + "rld%d" % (j % 4), 16)
            E.inc(s.dma_start(out=selsb[:, (j % 4) * TH:(j % 4 + 1) * TH],
                              in_=sel_t[m, :, th * TH:(th + 1) * TH]), p + "sld%d" % (j % 4), 16)

    ld_total = E.cnt[ld]

    for l in range(4):
        for th in range(NTH):
            @E.block.tensor
            def _(t, l=l, th=th):
                idx = l * NTH + th
                if idx == 0:
                    t.wait_ge(E.sem(xl_ready), E.cnt[xl_ready])
                    t.wait_ge(E.sem(ld), ld_total)
                    t.wait_ge(E.sem("preld"), E.cnt["preld"])
                if idx >= 2:
                    t.wait_ge(E.sem(hev), idx - 1)
                ps = psA if idx % 2 == 0 else psB
                for kt in range(KT):
                    inst = t.matmul(ps[0:16, :], acat_sb[:, kt * 64 + 16 * l: kt * 64 + 16 * l + 16],
                                    xl[:, kt * NLOC + th * TH: kt * NLOC + th * TH + TH],
                                    start=(kt == 0), stop=(kt == KT - 1))
                E.inc(inst, hmm_, 1)

            @E.block.vector
            def _(v, l=l, th=th):
                idx = l * NTH + th
                v.wait_ge(E.sem(hmm_), idx + 1)
                ps = psA if idx % 2 == 0 else psB
                E.inc(v.tensor_scalar(hsb[0:16, l * NLOC + th * TH: l * NLOC + th * TH + TH],
                                      ps[0:16, :], 8.0, None, ALU.mult), hev, 1)

    hev_total = E.cnt[hev]

    @E.block.gpsimd
    def _(g):
        for mb in range(NMB):
            if mb >= 2:
                g.wait_ge(E.sem(ev), (mb - 1) * MJ * NTH)
            E.inc(g.dma_start(out=wb.ap().rearrange("pp (b k) -> pp b k", b=2)[:, mb % 2, :],
                              in_=wdram[mb, :, :].rearrange("(kt p) c -> p kt c", p=128)),
                  p + "wld%d" % (mb % 2), 16)

    def emit_main(t, mb):
        t.wait_ge(E.sem(p + "wld%d" % (mb % 2)), 16 * (mb // 2 + 1))
        if mb == 0:
            t.wait_ge(E.sem(hev), hev_total)
        for mj in range(MJ):
            for th in range(NTH):
                ps = ring.acquire(t)
                woff = (mb % 2) * KT * 256 + mj * 128
                for kt in range(KT):
                    minst = t.matmul(ps[:, :], wb[:, woff + kt * 256: woff + kt * 256 + 128],
                                     xl[:, kt * NLOC + th * TH: kt * NLOC + th * TH + TH],
                                     start=(kt == 0), stop=(kt == KT - 1))
                ring.finish_tile(minst)
                j = mb * MJ * NTH + mj * NTH + th
                m = mb * MJ + mj
                for l in range(4):
                    yi = j * 4 + l
                    if yi >= 2:
                        t.wait_ge(E.sem(yev), yi - 1)
                    yps = psA if yi % 2 == 0 else psB
                    E.inc(t.matmul(yps[:, :], bpad_sb[0:16, l * D + m * 128:(l * D + m * 128) + 128],
                                   hsb[0:16, l * NLOC + th * TH: l * NLOC + th * TH + TH],
                                   start=True, stop=True), ymm, 1)

    def emit_evac(v, mb):
        if mb == 0:
            v.wait_ge(E.sem("preld"), E.cnt["preld"])
        for mj in range(MJ):
            for th in range(NTH):
                j = mb * MJ * NTH + mj * NTH + th
                v.wait_ge(E.sem(mm), j + 1)
                v.wait_ge(E.sem(p + "rld%d" % (j % 4)), 16 * (j // 4 + 1))
                if j >= 4:
                    v.wait_ge(E.sem(p + "st%d" % (j % 4)), 16 * (j // 4))
                ps = ring.banks[j % len(ring.banks)]
                mb2, mj2, th = tile_coords(j)
                m = mb2 * MJ + mj2
                osl = outsb[:, (j % 4) * TH:(j % 4 + 1) * TH]
                inst = v.scalar_tensor_tensor(osl, ps[:, :], common[bias_sb_name][:, m:m + 1],
                                              xres[:, (j % 4) * TH:(j % 4 + 1) * TH],
                                              ALU.add, ALU.add)
                v.wait_ge(E.sem(p + "sld%d" % (j % 4)), 16 * (j // 4 + 1))
                for l in range(4):
                    yi = j * 4 + l
                    v.wait_ge(E.sem(ymm), yi + 1)
                    yps = psA if yi % 2 == 0 else psB
                    v.tensor_scalar(eqt[:, :], selsb[:, (j % 4) * TH:(j % 4 + 1) * TH],
                                    float(l), None, ALU.is_equal)
                    E.inc(v.tensor_tensor(yt[:, :], eqt[:, :], yps[:, :], ALU.mult), yev, 1)
                    inst = v.tensor_tensor(osl, osl, yt[:, :], ALU.add)
                E.inc(inst, ev, 1)

    def emit_store(s, mb):
        for mj in range(MJ):
            for th in range(NTH):
                j = mb * MJ * NTH + mj * NTH + th
                s.wait_ge(E.sem(ev), j + 1)
                mb2, mj2, th2 = tile_coords(j)
                m = mb2 * MJ + mj2
                E.inc(s.dma_start(out=xout_t[m, :, th2 * TH:(th2 + 1) * TH],
                                  in_=outsb[:, (j % 4) * TH:(j % 4 + 1) * TH]), p + "st%d" % (j % 4), 16)
                jn = j + 4
                if jn < total_tiles:
                    mbn, mjn, thn = tile_coords(jn)
                    mn = mbn * MJ + mjn
                    E.inc(s.dma_start(out=xres[:, (jn % 4) * TH:(jn % 4 + 1) * TH],
                                      in_=xin_t[mn, :, thn * TH:(thn + 1) * TH]), p + "rld%d" % (jn % 4), 16)
                    E.inc(s.dma_start(out=selsb[:, (jn % 4) * TH:(jn % 4 + 1) * TH],
                                      in_=sel_t[mn, :, thn * TH:(thn + 1) * TH]), p + "sld%d" % (jn % 4), 16)

    for mb in range(NMB):
        @E.block.vector
        def _(v, mb=mb):
            if mb > 0:
                emit_evac(v, mb - 1)

        @E.block.tensor
        def _(t, mb=mb):
            emit_main(t, mb)

        @E.block.sync
        def _(s, mb=mb):
            if mb > 0:
                emit_store(s, mb - 1)

    @E.block.vector
    def _(v):
        emit_evac(v, NMB - 1)

    @E.block.sync
    def _(s):
        emit_store(s, NMB - 1)

    @E.block.sync
    def _(s):
        for i in range(4):
            s.wait_ge(E.sem(p + "st%d" % i), E.cnt[p + "st%d" % i])


def build_l2ffn(x_arrs, inp, sel_arrs):
    E = Env()
    Wb, acat, bpad, bias = _l2ffn_host(inp)
    with E.stack:
        xin = E.din("xin", x_arrs)
        wb_d = E.din("wb_d", Wb)
        ac_d = E.din("ac_d", acat)
        bp_d = E.din("bp_d", bpad)
        sel_d = E.din("sel_d", sel_arrs)
        eye_d = E.din("eye_d", eye_groups())
        bias_d = E.din("bias_d", bias)
        xout = E.dout("xout", [D, NLOC], F32)

        common = _common_alloc(E)
        common["bias_sb"] = E.sbuf("bias_sb", [128, 32], F32)
        xl = E.sbuf("xl", [128, KT * NLOC], BF16)
        sq = E.sbuf("sq", [128, 2 * NLOC], BF16)
        stat = E.sbuf("stat", [1, 4 * NLOC], F32)
        rstd_bc = E.sbuf("rstd_bc", [128, NLOC], BF16)
        mbc_sb = E.sbuf("mbc_sb", [128, NLOC], F32)
        mid = E.sbuf("mid", [128, TH], F32)
        rbf_row = E.sbuf("rbf_row", [1, NLOC], BF16)

        with E.nc.Block() as block:
            E.block = block
            emit_preamble(E, common, eye_d)

            @E.block.sync
            def _(s):
                E.inc(s.dma_start(out=common["bias_sb"][:, :], in_=bias_d[:, :]), "preld", 16)

            xl_ready = emit_ln(E, "ln_", xin, xl, sq, stat, rstd_bc, mbc_sb, mid, rbf_row,
                               common["ones_col"], common["ones_row"],
                               common["psA"], common["psB"], common["psBC"], common["eps_col"])
            emit_l2ffn(E, "f2_", xin, xout, wb_d, ac_d, bp_d, sel_d, "bias_sb",
                       xl_ready, xl, common)
        return E


def shard_sel(sel):
    out = []
    for c in range(NCORES):
        b, q = divmod(c, 4)
        out.append(np.ascontiguousarray(sel[b, q * NLOC:(q + 1) * NLOC, :].T))
    return out


def run_l2ffn(x, inp):
    sel = threefry_mask_sel()
    E = build_l2ffn(shard_x(x), inp, shard_sel(sel))
    res = E.run()
    return unshard_x([res.results[c]["xout"] for c in range(NCORES)])


def emit_zs_precompute(E, p, qz_d, s_d, zs_dram, G, N, done_sem, scratch, after=None):
    CH = 2752 if N % 2752 == 0 else 2048
    nch = N // CH
    qz_sb, zi, zf, ssb, zo = scratch
    ld = p + "ld"; sl = p + "sl"; st = p + "st"; vv = p + "vv"

    @E.block.sync
    def _(s):
        if after is not None:
            s.wait_ge(E.sem(after), 1)
        E.inc(s.dma_start(out=qz_sb[0:G, 0:N // 8], in_=qz_d[:, :]), ld, 16)

    for c in range(nch):
        @E.block.sync
        def _(s, c=c):
            if c >= 1:
                s.wait_ge(E.sem(vv), c)
            E.inc(s.dma_start(out=ssb[0:G, 0:CH], in_=s_d[:, c * CH:(c + 1) * CH]), sl, 16)

        @E.block.vector
        def _(v, c=c):
            v.wait_ge(E.sem(ld), 16)
            v.wait_ge(E.sem(sl), 16 * (c + 1))
            if c >= 1:
                v.wait_ge(E.sem(st), 16 * c)
            for i in range(8):
                v.tensor_scalar(zi[0:G, i:CH:8], qz_sb[0:G, c * (CH // 8):(c + 1) * (CH // 8)],
                                4 * i, 15, ALU.logical_shift_right, ALU.bitwise_and)
            v.tensor_scalar(zf[0:G, 0:CH], zi[0:G, 0:CH], 1.0, None, ALU.add)
            v.tensor_scalar(zf[0:G, 0:CH], zf[0:G, 0:CH], -1.0, None, ALU.mult)
            inst = v.tensor_tensor(zo[0:G, 0:CH], zf[0:G, 0:CH], ssb[0:G, 0:CH], ALU.mult)
            E.inc(inst, vv, 1)

        @E.block.gpsimd
        def _(g, c=c):
            g.wait_ge(E.sem(vv), c + 1)
            E.inc(g.dma_start(out=zs_dram[:, c * CH:(c + 1) * CH], in_=zo[0:G, 0:CH]), st, 16)

    @E.block.vector
    def _(v):
        v.wait_ge(E.sem(st), 16 * nch)
        E.inc(v.engine_nop(), done_sem, 1)


def emit_l1ffn(E, p, xin, xout, W, xl_ready, xl, common, bstack):
    (psA, psB, psBC, mm_banks) = (common["psA"], common["psB"], common["psBC"], common["mm_banks"])
    eye = common["eye"]
    shiftv = common["shiftv"]
    INTER = 11008
    IKT = INTER // 128
    NMC = INTER // 256
    NMC = getattr(E, "_l1_nmc", NMC)

    zs_g = E.dram(p + "zs_g", [32, INTER], BF16)
    zs_u = E.dram(p + "zs_u", [32, INTER], BF16)
    zs_d = E.dram(p + "zs_d", [86, D], BF16)
    h_dram = E.dram(p + "h", [IKT, 128, NLOC], BF16)
    wd_dram = E.dram(p + "wd", [16, 128, IKT * 256], BF16)

    zgd = p + "zgd"; zud = p + "zud"; zdd = p + "zdd"
    with ExitStack() as zstack:
        zscr = (E.sbuf(p + "z_qz", [128, 1376], I32, zstack),
                E.sbuf(p + "z_zi", [128, 2752], I32, zstack),
                E.sbuf(p + "z_zf", [128, 2752], F32, zstack),
                E.sbuf(p + "z_ss", [128, 2752], F32, zstack),
                E.sbuf(p + "z_zo", [128, 2752], BF16, zstack))
        emit_zs_precompute(E, p + "zg_", W["gq_z"], W["gq_s"], zs_g, 32, INTER, zgd, zscr)
        emit_zs_precompute(E, p + "zu_", W["uq_z"], W["uq_s"], zs_u, 32, INTER, zud, zscr, after=zgd)
        emit_zs_precompute(E, p + "zd_", W["dq_z"], W["dq_s"], zs_d, 86, D, zdd, zscr, after=zud)

    if getattr(E, "_l1_debug", None) == "A":
        bstack.close()
        zdump = E.dout("zdump", [32, INTER], BF16)
        zdump2 = E.dout("zdump2", [86, D], BF16)

        @E.block.gpsimd
        def _(g):
            g.wait_ge(E.sem(zgd), 1)
            g.wait_ge(E.sem(zud), 1)
            g.wait_ge(E.sem(zdd), 1)
            E.inc(g.dma_start(out=zdump[:, :], in_=zs_g[:, :]), p + "zd1", 16)
            E.inc(g.dma_start(out=zdump2[:, :], in_=zs_d[:, :]), p + "zd1", 16)
            g.wait_ge(E.sem(p + "zd1"), 32)
        return

    colsum = common["l1_colsum"]
    csd = p + "csd"

    @E.block.tensor
    def _(t):
        t.wait_ge(E.sem(xl_ready), E.cnt[xl_ready])
        t.wait_ge(E.sem("preld"), E.cnt["preld"])
        for th in range(NTH):
            ps = psA if th == 0 else psB
            for kt in range(KT):
                inst = t.matmul(ps[0:32, :], eye[:, kt * 32:(kt + 1) * 32],
                                xl[:, kt * NLOC + th * TH: kt * NLOC + th * TH + TH],
                                start=(kt == 0), stop=(kt == KT - 1))
            E.inc(inst, csd, 1)

    @E.block.vector
    def _(v):
        v.wait_ge(E.sem(csd), E.cnt[csd])
        v.tensor_copy(colsum[:, 0:TH], psA[0:32, :])
        inst = v.tensor_copy(colsum[:, TH:NLOC], psB[0:32, :])
        E.inc(inst, csd, 1)

    qrep = E.sbuf(p + "qrepB", [128, 4 * 8 * 256], I32, bstack)
    wqg = E.sbuf(p + "wqg", [128, KT * 256], BF16, bstack)
    wqu = E.sbuf(p + "wqu", [128, KT * 256], BF16, bstack)
    sexpf = E.sbuf(p + "sexpf", [128, KT * 256], BF16, bstack)
    zrow = E.sbuf(p + "zrow", [32, 2 * 512], BF16, bstack)
    hout = E.sbuf(p + "hout", [128, 2 * TH], BF16, bstack)
    gsb = E.sbuf(p + "gsb", [128, 2 * TH], BF16, bstack)
    wsh = common["l1_wsh"]
    wf4 = common["l1_wf4"]

    qld = [p + "qld%d" % i for i in range(4)]
    hst = [p + "hst%d" % i for i in range(2)]
    zld = [p + "zld%d" % i for i in range(2)]
    for n in qld + hst + zld + [p + "sxl"]:
        E.sem(n)
    dqv = p + "dqv"; bmm = p + "bmm"; bev = p + "bev"; silu = p + "silu"; sxw = p + "sxw"

    def q_src(qd, mc, ktg, nkt2):
        return qd[mc, :, None, ktg * 2048:ktg * 2048 + nkt2 * 256].broadcast_to([16, 8, nkt2 * 256])

    def emit_gw_B(g, mc):
        if mc == 0:
            g.wait_ge(E.sem(zgd), 1)
            g.wait_ge(E.sem(zud), 1)
            g.wait_ge(E.sem(zdd), 1)
        for m, qd in enumerate([W["gq"], W["uq"]]):
            si = mc * 2 + m
            sd = W["gq_s"] if m == 0 else W["uq_s"]
            if si >= 1:
                g.wait_ge(E.sem(dqv), 32 * si)
            E.inc(g.dma_start(out=sexpf[:, :],
                              in_=sd[None, :, mc * 256:(mc + 1) * 256].broadcast_to([128, KT, 256])),
                  p + "sxl", 16)
            for ktg in range(4):
                gi = mc * 8 + m * 4 + ktg
                slot = gi % 4
                if gi >= 4:
                    g.wait_ge(E.sem(dqv), (gi - 3) * 8)
                E.inc(g.dma_start(out=qrep[:, slot * 2048:(slot + 1) * 2048],
                                  in_=q_src(qd, mc, ktg, 8)), qld[slot], 16)

    def emit_zl_B(s, mc):
        slot = mc % 2
        if mc >= 2:
            s.wait_ge(E.sem(bmm), (mc - 1) * 8)
        E.inc(s.dma_start(out=zrow[0:32, slot * 512:slot * 512 + 256], in_=zs_g[:, mc * 256:(mc + 1) * 256]),
              zld[slot], 16)
        E.inc(s.dma_start(out=zrow[0:32, slot * 512 + 256:slot * 512 + 512], in_=zs_u[:, mc * 256:(mc + 1) * 256]),
              zld[slot], 16)

    def emit_dq_B(v, mc):
        for m in range(2):
            wq = wqg if m == 0 else wqu
            si = mc * 2 + m
            v.wait_ge(E.sem(p + "sxl"), 16 * (si + 1))
            if mc > 0:
                v.wait_ge(E.sem(bmm), 8 * mc)
            for ktg in range(4):
                gi = mc * 8 + m * 4 + ktg
                slot = gi % 4
                v.wait_ge(E.sem(qld[slot]), 16 * (gi // 4 + 1))
                for hh in range(2):
                    srcq = qrep[:, slot * 2048 + hh * 1024: slot * 2048 + (hh + 1) * 1024]
                    v.tensor_scalar(wsh[:, :], srcq, shiftv[:, 0:1], 15, ALU.logical_shift_right, ALU.bitwise_and)
                    v.tensor_scalar(wf4[:, :], wsh[:, :], 1.0, None, ALU.mult)
                    inst = v.tensor_tensor(wq[:, ktg * 2048 + hh * 1024: ktg * 2048 + (hh + 1) * 1024], wf4[:, :],
                                           sexpf[:, ktg * 2048 + hh * 1024: ktg * 2048 + (hh + 1) * 1024], ALU.mult)
                    E.inc(inst, dqv, 4)

    _noz = getattr(E, "_l1_debug", None) == "B2"

    def emit_mm_B(t, mc):
        t.wait_ge(E.sem(dqv), (mc + 1) * 2 * KT)
        if not _noz:
            t.wait_ge(E.sem(zld[mc % 2]), 32 * (mc // 2 + 1))
        if mc == 0:
            t.wait_ge(E.sem(csd), 2)
        for mj in range(2):
            for th in range(NTH):
                ti = (mc * 2 + mj) * NTH + th
                psg = mm_banks[(ti % 2) * 2]
                psu = mm_banks[(ti % 2) * 2 + 1]
                if ti >= 1:
                    t.wait_ge(E.sem(silu), ti)
                    t.wait_ge(E.sem(bev), ti)
                for kt in range(KT):
                    inst = t.matmul(psg[:, :], wqg[:, kt * 256 + mj * 128: kt * 256 + mj * 128 + 128],
                                    xl[:, kt * NLOC + th * TH: kt * NLOC + th * TH + TH],
                                    start=(kt == 0), stop=(_noz and kt == KT - 1))
                if not _noz:
                    inst = t.matmul(psg[:, :], zrow[0:32, (mc % 2) * 512 + mj * 128:(mc % 2) * 512 + mj * 128 + 128],
                                    colsum[:, th * TH:(th + 1) * TH], start=False, stop=True)
                E.inc(inst, bmm, 1)
                for kt in range(KT):
                    inst = t.matmul(psu[:, :], wqu[:, kt * 256 + mj * 128: kt * 256 + mj * 128 + 128],
                                    xl[:, kt * NLOC + th * TH: kt * NLOC + th * TH + TH],
                                    start=(kt == 0), stop=(_noz and kt == KT - 1))
                if not _noz:
                    inst = t.matmul(psu[:, :], zrow[0:32, (mc % 2) * 512 + 256 + mj * 128:(mc % 2) * 512 + 256 + mj * 128 + 128],
                                    colsum[:, th * TH:(th + 1) * TH], start=False, stop=True)
                E.inc(inst, bmm, 1)

    def emit_silu_B(s, mc):
        for mj in range(2):
            for th in range(NTH):
                ti = (mc * 2 + mj) * NTH + th
                psg = mm_banks[(ti % 2) * 2]
                s.wait_ge(E.sem(bmm), 2 * ti + 1)
                if getattr(E, "_l1_nosilu", False):
                    E.inc(s.copy(gsb[:, (ti % 2) * TH:(ti % 2 + 1) * TH], psg[:, :]), silu, 1)
                else:
                    E.inc(s.activation(gsb[:, (ti % 2) * TH:(ti % 2 + 1) * TH], psg[:, :], AF.Silu), silu, 1)

    def emit_hev_B(v, mc):
        for mj in range(2):
            for th in range(NTH):
                ti = (mc * 2 + mj) * NTH + th
                psu = mm_banks[(ti % 2) * 2 + 1]
                v.wait_ge(E.sem(silu), ti + 1)
                v.wait_ge(E.sem(bmm), 2 * ti + 2)
                if ti >= 2:
                    v.wait_ge(E.sem(hst[ti % 2]), 16 * (ti // 2))
                inst = v.tensor_tensor(hout[:, (ti % 2) * TH:(ti % 2 + 1) * TH], psu[:, :],
                                       gsb[:, (ti % 2) * TH:(ti % 2 + 1) * TH], ALU.mult)
                E.inc(inst, bev, 1)

    def emit_st_B(s, mc):
        for mj in range(2):
            for th in range(NTH):
                ti = (mc * 2 + mj) * NTH + th
                s.wait_ge(E.sem(bev), ti + 1)
                kt_i = mc * 2 + mj
                E.inc(s.dma_start(out=h_dram[kt_i, :, th * TH:(th + 1) * TH],
                                  in_=hout[:, (ti % 2) * TH:(ti % 2 + 1) * TH]), hst[ti % 2], 16)

    if getattr(E, "_l1_debug", None) == "B0":
        @E.block.gpsimd
        def _(g):
            emit_gw_B(g, 0)

        @E.block.vector
        def _(v):
            emit_dq_B(v, 0)
        bstack.close()
        wdump = E.dout("wdump", [128, KT * 256], BF16)

        @E.block.gpsimd
        def _(g):
            g.wait_ge(E.sem(dqv), 32)
            E.inc(g.dma_start(out=wdump[:, :], in_=wqg[:, :]), p + "wdp", 16)
            g.wait_ge(E.sem(p + "wdp"), 16)
        return

    noz_dbg = getattr(E, "_l1_debug", None) == "B2"
    for mc in range(NMC):
        @E.block.gpsimd
        def _(g, mc=mc):
            emit_gw_B(g, mc)

        @E.block.sync
        def _(s, mc=mc):
            if not noz_dbg:
                if mc == 0:
                    s.wait_ge(E.sem(zgd), 1)
                    s.wait_ge(E.sem(zud), 1)
                emit_zl_B(s, mc)

        @E.block.vector
        def _(v, mc=mc):
            emit_dq_B(v, mc)

        @E.block.tensor
        def _(t, mc=mc):
            emit_mm_B(t, mc)

        @E.block.scalar
        def _(s, mc=mc):
            emit_silu_B(s, mc)

        @E.block.vector
        def _(v, mc=mc):
            emit_hev_B(v, mc)

        @E.block.sync
        def _(s, mc=mc):
            emit_st_B(s, mc)

    if getattr(E, "_l1_debug", None) in ("B", "B2"):
        bstack.close()
        hdump = E.dout("hdump", [IKT, 128, NLOC], BF16)

        @E.block.gpsimd
        def _(g):
            for i in range(2):
                g.wait_ge(E.sem(hst[i]), E.cnt[hst[i]])
            E.inc(g.dma_start(out=hdump[:, :, :], in_=h_dram[:, :, :]), p + "hd", 16)
            g.wait_ge(E.sem(p + "hd"), 16)
        return

    bstack.close()
    qrep = E.sbuf(p + "qrepC", [128, 2 * 8 * 256], I32)
    hsb = E.sbuf(p + "hsb", [128, IKT * TH], BF16)
    wqd = E.sbuf(p + "wqd", [128, IKT * 256], BF16)
    sexpd = E.sbuf(p + "sexpd", [128, 2 * 8 * 256], BF16)
    csh = E.sbuf(p + "csh", [86, TH], BF16)
    zrowd = E.sbuf(p + "zrowd", [86, 2 * 256], BF16)
    e86 = common["e86"]
    xres = E.sbuf(p + "xres", [128, 4 * TH], F32)
    outsb = E.sbuf(p + "out", [128, 4 * TH], F32)

    hld = [p + "hld%d" % i for i in range(2)]
    sxd = [p + "sxd%d" % i for i in range(2)]
    zldd = [p + "zldd%d" % i for i in range(2)]
    rld = [p + "rld%d" % i for i in range(4)]
    stt = [p + "stt%d" % i for i in range(4)]
    qldd = [p + "qldd%d" % i for i in range(2)]
    for n in hld + sxd + zldd + rld + stt + qldd + [p + "wcb", p + "wcl"]:
        E.sem(n)
    dqd = p + "dqd"; cmm = p + "cmm"; cev = p + "cev"; chs = p + "chs"
    E.sem(dqd)

    NMB_D = 16
    KTGS = [(ktg, 8 if ktg < 10 else 6) for ktg in range(11)]
    _sizes = [nk for _, nk in KTGS]
    _cum = []
    _tot = 0
    for _mb in range(NMB_D):
        for _nk in _sizes:
            _tot += _nk
            _cum.append(_tot)
    PHASEB_DQV = NMC * 2 * KT
    xin_t = xin.rearrange("(mt pp) n -> mt pp n", pp=128)
    xout_t = xout.rearrange("(mt pp) n -> mt pp n", pp=128)
    ring = PsumRing(E, mm_banks, cmm, cev)

    def q_src_d(mb, ktg, nk):
        return W["dq"][mb, :, None, ktg * 2048:ktg * 2048 + nk * 256].broadcast_to([16, 8, nk * 256])

    @E.block.sync
    def _(s):
        for i in range(2):
            s.wait_ge(E.sem(hst[i]), E.cnt[hst[i]])
        for jj in range(4):
            E.inc(s.dma_start(out=xres[:, jj * TH:(jj + 1) * TH],
                              in_=xin_t[jj, :, 0:TH]), rld[jj], 16)

    dq_counts = {}

    for th in range(NTH):
        @E.block.sync
        def _(s, th=th):
            if th == 0:
                for i in range(2):
                    s.wait_ge(E.sem(hst[i]), E.cnt[hst[i]])
            else:
                s.wait_ge(E.sem(cev), NMB_D * 2)
            half = h_dram[:, :, th * TH:(th + 1) * TH].rearrange("kt pp n -> pp kt n")
            bounds = [0, 11, 22, 33, 44, 55, 66, 76, IKT]
            for bi in range(8):
                a, b = bounds[bi], bounds[bi + 1]
                E.inc(s.dma_start(out=hsb[:, a * TH:b * TH], in_=half[:, a:b, :]), hld[0], 16)

        @E.block.tensor
        def _(t, th=th):
            t.wait_ge(E.sem(hld[0]), 16 * 8 * (th + 1))
            for kt in range(IKT):
                inst = t.matmul(psA[0:86, :], e86[:, kt * 86:(kt + 1) * 86],
                                hsb[:, kt * TH:(kt + 1) * TH], start=(kt == 0), stop=(kt == IKT - 1))
            E.inc(inst, chs, 1)

        @E.block.vector
        def _(v, th=th):
            v.wait_ge(E.sem(chs), 2 * th + 1)
            E.inc(v.tensor_copy(csh[0:86, :], psA[0:86, :]), chs, 1)

        for mb in range(NMB_D):
            if th == 0:
                @E.block.gpsimd
                def _(g, mb=mb):
                    if mb == 0 and th == 0:
                        g.wait_ge(E.sem(dqv), PHASEB_DQV)
                        for i in range(2):
                            g.wait_ge(E.sem(hst[i]), E.cnt[hst[i]])
                    for ktg, nk in KTGS:
                        gi = mb * 11 + ktg
                        slot = gi % 2
                        if gi >= 2:
                            g.wait_ge(E.sem(dqd), _cum[gi - 2])
                        E.inc(g.dma_start(
                            out=qrep[:, slot * 2048:slot * 2048 + nk * 256],
                            in_=q_src_d(mb, ktg, nk)), qldd[slot], 16)
                        E.inc(g.dma_start(out=sexpd[:, slot * 2048:slot * 2048 + nk * 256],
                                          in_=W["dq_s"][None, ktg * 8:ktg * 8 + nk, mb * 256:(mb + 1) * 256]
                                          .broadcast_to([128, nk, 256])), sxd[slot], 16)

            @E.block.sync
            def _(s, th=th, mb=mb):
                gmb = th * NMB_D + mb
                slotz = mb % 2
                if gmb >= 2:
                    s.wait_ge(E.sem(cmm), 2 * (gmb - 1))
                E.inc(s.dma_start(out=zrowd[0:86, slotz * 256:(slotz + 1) * 256],
                                  in_=zs_d[:, mb * 256:(mb + 1) * 256]), zldd[slotz], 16)
                if th == 1:
                    if mb > 0:
                        s.wait_ge(E.sem(cmm), NMB_D * 2 + mb * 2)
                    s.wait_ge(E.sem(p + "wcb"), 16 * (mb + 1))
                    E.inc(s.dma_start(out=wqd[:, :], in_=wd_dram[mb, :, :]), p + "wcl", 16)

            if th == 0:
                @E.block.vector
                def _(v, mb=mb):
                    if mb > 0:
                        v.wait_ge(E.sem(cmm), mb * 2)
                        v.wait_ge(E.sem(p + "wcb"), 16 * mb)
                    for ktg, nk in KTGS:
                        gi = mb * 11 + ktg
                        slot = gi % 2
                        v.wait_ge(E.sem(qldd[slot]), 16 * (gi // 2 + 1))
                        v.wait_ge(E.sem(sxd[slot]), 16 * (gi // 2 + 1))
                        for off in range(0, nk * 256, 1024):
                            w = min(1024, nk * 256 - off)
                            srcq = qrep[:, slot * 2048 + off: slot * 2048 + off + w]
                            v.tensor_scalar(wsh[:, 0:w], srcq, shiftv[:, 0:1], 15,
                                            ALU.logical_shift_right, ALU.bitwise_and)
                            v.tensor_scalar(wf4[:, 0:w], wsh[:, 0:w], 1.0, None, ALU.mult)
                            inst = v.tensor_tensor(wqd[:, ktg * 2048 + off: ktg * 2048 + off + w], wf4[:, 0:w],
                                                   sexpd[:, slot * 2048 + off: slot * 2048 + off + w],
                                                   ALU.mult)
                            E.inc(inst, dqd, w // 256)
                    dq_counts[(0, mb)] = E.cnt[dqd]

                @E.block.gpsimd
                def _(g, mb=mb):
                    g.wait_ge(E.sem(dqd), dq_counts[(0, mb)])
                    E.inc(g.dma_start(out=wd_dram[mb, :, :], in_=wqd[:, :]), p + "wcb", 16)

            if th == 0 and mb == 0 and getattr(E, "_l1_debug", None) == "C0":
                wdump2 = E.dout("wdump2", [128, IKT * 256], BF16)
                hdump2 = E.dout("hdump2", [128, IKT * TH], BF16)
                cdump = E.dout("cdump", [86, TH], BF16)

                @E.block.gpsimd
                def _(g):
                    g.wait_ge(E.sem(dqd), dq_counts[(0, 0)])
                    g.wait_ge(E.sem(chs), 2)
                    E.inc(g.dma_start(out=wdump2[:, :], in_=wqd[:, :]), p + "cd0", 16)
                    E.inc(g.dma_start(out=hdump2[:, :], in_=hsb[:, :]), p + "cd0", 16)
                    E.inc(g.dma_start(out=cdump[:, :], in_=csh[0:86, :]), p + "cd0", 16)
                    g.wait_ge(E.sem(p + "cd0"), 48)

            @E.block.tensor
            def _(t, th=th, mb=mb):
                if th == 0:
                    t.wait_ge(E.sem(dqd), dq_counts[(0, mb)])
                    if mb == 0:
                        t.wait_ge(E.sem(chs), 2)
                else:
                    t.wait_ge(E.sem(p + "wcl"), 16 * (mb + 1))
                    if mb == 0:
                        t.wait_ge(E.sem(chs), 4)
                t.wait_ge(E.sem(zldd[mb % 2]), 16 * ((th * NMB_D + mb) // 2 + 1))
                for mj in range(2):
                    ps = ring.acquire(t)
                    for kt in range(IKT):
                        t.matmul(ps[:, :], wqd[:, kt * 256 + mj * 128: kt * 256 + mj * 128 + 128],
                                 hsb[:, kt * TH:(kt + 1) * TH], start=(kt == 0), stop=False)
                    inst = t.matmul(ps[:, :], zrowd[0:86, (mb % 2) * 256 + mj * 128:(mb % 2) * 256 + mj * 128 + 128],
                                    csh[0:86, :], start=False, stop=True)
                    ring.finish_tile(inst)

            @E.block.vector
            def _(v, th=th, mb=mb):
                for mj in range(2):
                    j = th * NMB_D * 2 + mb * 2 + mj
                    v.wait_ge(E.sem(cmm), j + 1)
                    v.wait_ge(E.sem(rld[j % 4]), 16 * (j // 4 + 1))
                    if j >= 4:
                        v.wait_ge(E.sem(stt[j % 4]), 16 * (j // 4))
                    ps = ring.banks[j % len(ring.banks)]
                    inst = v.scalar_tensor_tensor(outsb[:, (j % 4) * TH:(j % 4 + 1) * TH],
                                                  ps[:, :], 0.0,
                                                  xres[:, (j % 4) * TH:(j % 4 + 1) * TH],
                                                  ALU.add, ALU.add)
                    E.inc(inst, cev, 1)

            @E.block.sync
            def _(s, th=th, mb=mb):
                for mj in range(2):
                    j = th * NMB_D * 2 + mb * 2 + mj
                    s.wait_ge(E.sem(cev), j + 1)
                    m = j % 32
                    E.inc(s.dma_start(out=xout_t[m, :, th * TH:(th + 1) * TH],
                                      in_=outsb[:, (j % 4) * TH:(j % 4 + 1) * TH]), stt[j % 4], 16)
                    jn = j + 4
                    if jn < 64:
                        thn = jn // 32
                        mn = jn % 32
                        E.inc(s.dma_start(out=xres[:, (jn % 4) * TH:(jn % 4 + 1) * TH],
                                          in_=xin_t[mn, :, thn * TH:(thn + 1) * TH]), rld[jn % 4], 16)

    @E.block.sync
    def _(s):
        for i in range(4):
            s.wait_ge(E.sem(stt[i]), E.cnt[stt[i]])


def _l1ffn_host(inp):
    def gu_tiles(q):
        a = q.reshape(32, 16, 43, 256).transpose(2, 1, 0, 3)
        return np.ascontiguousarray(a.reshape(43, 16, 32 * 256))

    def d_tiles(q):
        a = q.reshape(86, 16, 16, 256).transpose(2, 1, 0, 3)
        return np.ascontiguousarray(a.reshape(16, 16, 86 * 256))

    return {
        "gq": gu_tiles(inp["L1_ffn_gate_qweight"]), "gq_s": inp["L1_ffn_gate_scales"], "gq_z": inp["L1_ffn_gate_qzeros"],
        "uq": gu_tiles(inp["L1_ffn_up_qweight"]), "uq_s": inp["L1_ffn_up_scales"], "uq_z": inp["L1_ffn_up_qzeros"],
        "dq": d_tiles(inp["L1_ffn_down_qweight"]), "dq_s": inp["L1_ffn_down_scales"], "dq_z": inp["L1_ffn_down_qzeros"],
    }


def eye86():
    e = np.zeros((128, 86 * 86), np.float32)
    for kt in range(86):
        e[:, kt * 86 + kt] = 1.0
    return e.astype(BF)


def build_l1ffn(x_arrs, inp):
    E = Env()
    W = _l1ffn_host(inp)
    with E.stack:
        xin = E.din("xin", x_arrs)
        Wd = {k: E.din("w_" + k, v) for k, v in W.items()}
        eye_d = E.din("eye_d", eye_groups())
        e86_d = E.din("e86_d", eye86())
        xout = E.dout("xout", [D, NLOC], F32)

        common = _common_alloc(E)
        common["e86"] = E.sbuf("e86", [128, 86 * 86], BF16)
        common["shiftv"] = E.sbuf("shiftv", [128, 1], I32)
        common["l1_colsum"] = E.sbuf("l1_cs", [32, NLOC], BF16)
        common["l1_wsh"] = E.sbuf("l1_wsh", [128, 1024], I32)
        common["l1_wf4"] = E.sbuf("l1_wf4", [128, 1024], BF16)
        bstack = ExitStack()
        xl = E.sbuf("xl", [128, KT * NLOC], BF16, bstack)
        sq = E.sbuf("sq", [128, 2 * NLOC], BF16, bstack)
        stat = E.sbuf("stat", [1, 4 * NLOC], F32, bstack)
        rstd_bc = E.sbuf("rstd_bc", [128, NLOC], BF16, bstack)
        mbc_sb = E.sbuf("mbc_sb", [128, NLOC], F32, bstack)
        mid = E.sbuf("mid", [128, TH], F32, bstack)
        rbf_row = E.sbuf("rbf_row", [1, NLOC], BF16, bstack)

        with E.nc.Block() as block:
            E.block = block
            emit_preamble(E, common, eye_d)

            @E.block.sync
            def _(s):
                E.inc(s.dma_start(out=common["e86"][:, :], in_=e86_d[:, :]), "preld", 16)

            @E.block.gpsimd
            def _(g):
                E.inc(g.iota(common["shiftv"][:, 0:1], [[0, 1]], channel_multiplier=4), "pre", 1)

            @E.block.vector
            def _(v):
                v.wait_ge(E.sem("pre"), E.cnt["pre"])
                E.inc(v.tensor_scalar(common["shiftv"][:, 0:1], common["shiftv"][:, 0:1], 31, None,
                                      ALU.bitwise_and), "pre", 1)

            xl_ready = emit_ln(E, "ln_", xin, xl, sq, stat, rstd_bc, mbc_sb, mid, rbf_row,
                               common["ones_col"], common["ones_row"],
                               common["psA"], common["psB"], common["psBC"], common["eps_col"])

            @E.block.vector
            def _(v):
                v.wait_ge(E.sem("pre"), E.cnt["pre"])

            emit_l1ffn(E, "f1_", xin, xout, Wd, xl_ready, xl, common, bstack)
        return E


def run_l1ffn(x, inp):
    E = build_l1ffn(shard_x(x), inp)
    res = E.run()
    return unshard_x([res.results[c]["xout"] for c in range(NCORES)])


def _l3ffn_host(inp):
    O, G, GS = 4096, 32, 128
    qwT = np.ascontiguousarray(inp["L3_ffn_qweight"].reshape(O, O).T)
    qblocks = wblocks_km(qwT, 512)
    scalesT = np.ascontiguousarray(inp["L3_ffn_scales"][:, :, 0].T)
    zT = np.ascontiguousarray(inp["L3_ffn_scaled_zeros"][:, :, 0].T)
    return qblocks, scalesT, zT, bias_cols(inp["L3_ffn_bias"])


def emit_l3ffn(E, p, xin, xout, qw, scalesT, zT, bias_sb_name, xl_ready, xl,
               common):
    MJ = 4
    nc = E.nc
    (psA, psB, psBC, mm_banks, ones_col, ones_row) = (
        common["psA"], common["psB"], common["psBC"], common["mm_banks"],
        common["ones_col"], common["ones_row"])

    NMB = 8
    qb = E.sbuf(p + "qb", [128, 2 * KT * 512], BF16)
    sexp = E.sbuf(p + "sexp", [128, 8 * 512], BF16)
    zsb = E.sbuf(p + "zsb", [32, D], BF16)
    colsum = E.sbuf(p + "cs", [32, NLOC], BF16)
    bias_sb = common[bias_sb_name]
    eye = common["eye"]
    xres = E.sbuf(p + "xres", [128, 4 * TH], F32)
    outsb = E.sbuf(p + "out", [128, 4 * TH], F32)

    dq = p + "dq"; wfree = p + "wfree"
    mm = p + "mm"; ev = p + "ev"; zl = p + "zl"; csd = p + "csd"
    for i in range(2):
        E.sem(p + "wld%d" % i)
    for i in range(8):
        E.sem(p + "sld%d" % i)
    for i in range(4):
        E.sem(p + "rld%d" % i)
        E.sem(p + "st%d" % i)

    ring = PsumRing(E, mm_banks, mm, ev)
    total_tiles = NMB * 4 * NTH
    xin_t = xin.rearrange("(mt pp) n -> mt pp n", pp=128)
    xout_t = xout.rearrange("(mt pp) n -> mt pp n", pp=128)

    def res_src(j):
        m, th = divmod(j, NTH)
        return None

    def tile_coords(j):
        mb, r = divmod(j, 4 * NTH)
        mj, th = divmod(r, NTH)
        return mb, mj, th

    @E.block.gpsimd
    def _(g):
        E.inc(g.dma_start(out=zsb[:, :], in_=zT[:, :]), zl, 16)

    @E.block.sync
    def _(s):
        for j in range(4):
            mb, mj, th = tile_coords(j)
            m = mb * 4 + mj
            E.inc(s.dma_start(out=xres[:, (j % 4) * TH:(j % 4 + 1) * TH],
                              in_=xin_t[m, :, th * TH:(th + 1) * TH]), p + "rld%d" % (j % 4), 16)

    @E.block.tensor
    def _(t):
        t.wait_ge(E.sem(xl_ready), E.cnt[xl_ready])
        t.wait_ge(E.sem("preld"), E.cnt["preld"])
        for th in range(NTH):
            ps = psA if th == 0 else psB
            for kt in range(KT):
                inst = t.matmul(ps[0:32, :], eye[:, kt * 32:(kt + 1) * 32],
                                xl[:, kt * NLOC + th * TH: kt * NLOC + th * TH + TH],
                                start=(kt == 0), stop=(kt == KT - 1))
            E.inc(inst, csd, 1)

    @E.block.vector
    def _(v):
        v.wait_ge(E.sem(csd), E.cnt[csd])
        v.tensor_copy(colsum[:, 0:TH], psA[0:32, :])
        inst = v.tensor_copy(colsum[:, TH:NLOC], psB[0:32, :])
        E.inc(inst, csd, 1)

    @E.block.gpsimd
    def _(g):
        for mb in range(NMB):
            if mb >= 2:
                g.wait_ge(E.sem(ev), (mb - 1) * 4 * NTH)
            E.inc(g.dma_start(out=qb.ap().rearrange("pp (b k) -> pp b k", b=2)[:, mb % 2, :],
                              in_=qw[mb, :, :].rearrange("(kt p) c -> p kt c", p=128)),
                  p + "wld%d" % (mb % 2), 16)
            for kt in range(KT):
                gidx = mb * KT + kt
                slot = gidx % 8
                if gidx >= 8:
                    g.wait_ge(E.sem(dq), gidx - 7)
                E.inc(g.dma_start(out=sexp[:, slot * 512:(slot + 1) * 512],
                                  in_=scalesT[kt:kt + 1, mb * 512:(mb + 1) * 512].broadcast_to([128, 512])),
                      p + "sld%d" % slot, 16)

    def emit_dq(v, mb):
        v.wait_ge(E.sem(p + "wld%d" % (mb % 2)), 16 * (mb // 2 + 1))
        for kt in range(KT):
            gidx = mb * KT + kt
            slot = gidx % 8
            v.wait_ge(E.sem(p + "sld%d" % slot), 16 * (gidx // 8 + 1))
            off = (mb % 2) * KT * 512 + kt * 512
            inst = v.tensor_tensor(qb[:, off:off + 512], qb[:, off:off + 512],
                                   sexp[:, slot * 512:(slot + 1) * 512], ALU.mult)
            E.inc(inst, dq, 1)

    def emit_main(t, mb):
        t.wait_ge(E.sem(dq), KT * (mb + 1))
        if mb == 0:
            t.wait_ge(E.sem(csd), 2)
            t.wait_ge(E.sem(zl), 16)
        for mj in range(4):
            for th in range(NTH):
                ps = ring.acquire(t)
                woff = (mb % 2) * KT * 512 + mj * 128
                for kt in range(KT):
                    minst = t.matmul(ps[:, :], qb[:, woff + kt * 512: woff + kt * 512 + 128],
                                     xl[:, kt * NLOC + th * TH: kt * NLOC + th * TH + TH],
                                     start=(kt == 0), stop=False)
                inst = t.matmul(ps[:, :], zsb[:, mb * 512 + mj * 128: mb * 512 + (mj + 1) * 128],
                                colsum[:, th * TH:(th + 1) * TH], start=False, stop=True)
                ring.finish_tile(inst)

    def emit_evac(v, mb):
        if mb == 0:
            v.wait_ge(E.sem("preld"), E.cnt["preld"])
        for mj in range(4):
            for th in range(NTH):
                j = mb * 8 + mj * NTH + th
                v.wait_ge(E.sem(mm), j + 1)
                v.wait_ge(E.sem(p + "rld%d" % (j % 4)), 16 * (j // 4 + 1))
                if j >= 4:
                    v.wait_ge(E.sem(p + "st%d" % (j % 4)), 16 * (j // 4))
                ps = ring.banks[j % len(ring.banks)]
                m = mb * 4 + mj
                inst = v.scalar_tensor_tensor(outsb[:, (j % 4) * TH:(j % 4 + 1) * TH],
                                              ps[:, :], bias_sb[:, m:m + 1],
                                              xres[:, (j % 4) * TH:(j % 4 + 1) * TH],
                                              ALU.add, ALU.add)
                E.inc(inst, ev, 1)

    def emit_store(s, mb):
        for mj in range(MJ):
            for th in range(NTH):
                j = mb * MJ * NTH + mj * NTH + th
                s.wait_ge(E.sem(ev), j + 1)
                mb2, mj2, th2 = tile_coords(j)
                m = mb2 * MJ + mj2
                E.inc(s.dma_start(out=xout_t[m, :, th2 * TH:(th2 + 1) * TH],
                                  in_=outsb[:, (j % 4) * TH:(j % 4 + 1) * TH]), p + "st%d" % (j % 4), 16)
                jn = j + 4
                if jn < total_tiles:
                    mbn, mjn, thn = tile_coords(jn)
                    mn = mbn * MJ + mjn
                    E.inc(s.dma_start(out=xres[:, (jn % 4) * TH:(jn % 4 + 1) * TH],
                                      in_=xin_t[mn, :, thn * TH:(thn + 1) * TH]), p + "rld%d" % (jn % 4), 16)

    for mb in range(NMB):
        @E.block.vector
        def _(v, mb=mb):
            if mb > 0:
                emit_evac(v, mb - 1)
            emit_dq(v, mb)

        @E.block.tensor
        def _(t, mb=mb):
            emit_main(t, mb)

        @E.block.sync
        def _(s, mb=mb):
            if mb > 0:
                emit_store(s, mb - 1)

    @E.block.vector
    def _(v):
        emit_evac(v, NMB - 1)

    @E.block.sync
    def _(s):
        emit_store(s, NMB - 1)

    @E.block.sync
    def _(s):
        for i in range(4):
            s.wait_ge(E.sem(p + "st%d" % i), E.cnt[p + "st%d" % i])

    return p + "st0"


def _common_alloc(E):
    common = {}
    common["psA"] = E.psum("psA")
    common["psB"] = E.psum("psB")
    common["psBC"] = E.psum("psBC")
    common["mm_banks"] = [E.psum("mm%d" % i) for i in range(4)]
    common["ones_col"] = E.sbuf("ones_col", [128, 1], BF16)
    common["ones_row"] = E.sbuf("ones_row", [1, 128], F32)
    common["eps_col"] = E.sbuf("eps_col", [1, 1], F32)
    common["eye"] = E.sbuf("eye", [128, KT * 32], BF16)
    return common


def emit_preamble(E, common, eye_dram):
    @E.block.vector
    def _(v):
        v.memset(common["ones_col"][:, :], 1.0)
        v.memset(common["ones_row"][:, :], 1.0)
        inst = v.memset(common["eps_col"][:, :], 1e-5)
        E.inc(inst, "pre", 1)

    @E.block.sync
    def _(s):
        E.inc(s.dma_start(out=common["eye"][:, :], in_=eye_dram[:, :]), "preld", 16)

    @E.block.tensor
    def _(t):
        t.wait_ge(E.sem("pre"), 1)


def build_l3ffn(x_arrs, inp):
    E = Env()
    qblocks, scalesT, zT, bias = _l3ffn_host(inp)
    with E.stack:
        xin = E.din("xin", x_arrs)
        qw = E.din("qw", qblocks)
        sc = E.din("sc", scalesT)
        zt = E.din("zt", zT)
        eye_d = E.din("eye_d", eye_groups())
        bias_d = E.din("bias", bias)
        xout = E.dout("xout", [D, NLOC], F32)

        common = _common_alloc(E)
        common["bias_sb"] = E.sbuf("bias_sb", [128, 32], F32)
        xl = E.sbuf("xl", [128, KT * NLOC], BF16)
        sq = E.sbuf("sq", [128, 2 * NLOC], BF16)
        stat = E.sbuf("stat", [1, 4 * NLOC], F32)
        rstd_bc = E.sbuf("rstd_bc", [128, NLOC], BF16)
        mbc_sb = E.sbuf("mbc_sb", [128, NLOC], F32)
        mid = E.sbuf("mid", [128, TH], F32)
        rbf_row = E.sbuf("rbf_row", [1, NLOC], BF16)

        with E.nc.Block() as block:
            E.block = block
            emit_preamble(E, common, eye_d)

            @E.block.sync
            def _(s):
                E.inc(s.dma_start(out=common["bias_sb"][:, :], in_=bias_d[:, :]), "preld", 16)

            xl_ready = emit_ln(E, "ln_", xin, xl, sq, stat, rstd_bc, mbc_sb, mid, rbf_row,
                               common["ones_col"], common["ones_row"],
                               common["psA"], common["psB"], common["psBC"], common["eps_col"])
            emit_l3ffn(E, "f3_", xin, xout, qw, sc, zt, "bias_sb", xl_ready, xl, common)
        return E


def run_l3ffn(x, inp):
    x_arrs = shard_x(x)
    E = build_l3ffn(x_arrs, inp)
    res = E.run()
    return unshard_x([res.results[c]["xout"] for c in range(NCORES)])



GROUPS4 = [[0, 1, 2, 3], [4, 5, 6, 7]]


def _l1attn_host(inp):
    return {
        "wq": wblocks_km(np.ascontiguousarray(np.asarray(inp["L1_attn_Wq"]).T), 128),
        "wk": wblocks_km(np.ascontiguousarray(np.asarray(inp["L1_attn_Wk"]).T), 128),
        "wv": wblocks_km(np.ascontiguousarray(np.asarray(inp["L1_attn_Wv"]).T), 128),
        "wo": wblocks_km(np.ascontiguousarray(np.asarray(inp["L1_attn_Wo"]).T), 512),
        "bq": bias_cols(inp["L1_attn_bq"]), "bk": bias_cols(inp["L1_attn_bk"]),
        "bv": bias_cols(inp["L1_attn_bv"]), "bo": bias_cols(inp["L1_attn_bo"]),
    }


def emit_l1attn(E, p, xin, xout, W, xl_ready, xl, common, astack):
    (psA, psB, psBC, mm_banks) = (common["psA"], common["psB"], common["psBC"], common["mm_banks"])
    ones_col = common["ones_col"]
    eye128b = common["eye128b"]

    bo_sb = E.sbuf(p + "bo", [128, 32], F32)
    wkb = E.sbuf(p + "wkb", [128, 2 * KT * 128], BF16, astack)
    tk0 = E.sbuf(p + "tk0", [128, NLOC], F32, astack)
    tkm = E.sbuf(p + "tkm", [128, NLOC], F32, astack)
    tka = E.sbuf(p + "tka", [128, NLOC], F32, astack)
    tk2 = E.sbuf(p + "tk2", [128, NLOC], F32, astack)
    pk = E.sbuf(p + "pk", [128, NLOC], F32, astack)
    pv = E.sbuf(p + "pv", [128, NLOC], F32, astack)
    phk_tm = E.sbuf(p + "phk", [128, NLOC], BF16, astack)
    pv_tm = E.sbuf(p + "pvt", [128, NLOC], BF16, astack)
    kvagg = E.sbuf(p + "kva", [128, 8 * 129], F32, astack)
    kvtot = E.sbuf(p + "kvt", [128, 8 * 129], F32, astack)
    nplus = E.sbuf(p + "npl", [128, 8], F32, astack)
    ncolg = E.sbuf(p + "ncg", [128, 8], F32, astack)
    rcol = E.sbuf(p + "rcl", [128, 8], F32, astack)
    rcolg = E.sbuf(p + "rcg", [128, 8], F32, astack)
    ones8 = E.sbuf(p + "on8", [128, 8], F32, astack)
    kvnf = E.sbuf(p + "kvn", [128, 128], F32, astack)
    kvn_dv = E.sbuf(p + "kdv", [128, 8 * 128], BF16, astack)
    pq = E.sbuf(p + "pq", [128, NLOC], BF16, astack)
    ost = E.sbuf(p + "ost", [128, 2 * TH], BF16, astack)
    bkc = E.sbuf(p + "bkc", [128, 8], F32, astack)
    bvc = E.sbuf(p + "bvc", [128, 8], F32, astack)
    bqc = E.sbuf(p + "bqc", [128, 32], F32, astack)

    o_dram = E.dram(p + "odr", [D, NLOC], BF16)
    cc_in = E.nc.dram_tensor(p + "ccin", [128, 8 * 129], F32)
    cc_out = E.nc.dram_tensor(p + "ccout", [128, 8 * 129], F32)

    mm = p + "mm"; ev = p + "ev"
    for nm in ("wld0", "wld1", "bld", "s1", "s2", "pkd", "pvd", "tmm", "tcv",
               "kvm", "kvv", "ccd", "ccs", "ccl", "nv", "ng", "kvtv", "kvtm",
               "pqd", "oev", "ostd0", "ostd1", "old"):
        E.sem(p + nm)
    ring = PsumRing(E, mm_banks, mm, ev)

    @E.block.sync
    def _(s):
        E.inc(s.dma_start(out=bkc[:, :], in_=W["bk"][:, :]), p + "bld", 16)
        E.inc(s.dma_start(out=bvc[:, :], in_=W["bv"][:, :]), p + "bld", 16)
        E.inc(s.dma_start(out=bqc[:, :], in_=W["bq"][:, :]), p + "bld", 16)

    @E.block.vector
    def _(v):
        v.memset(ones8[:, :], 1.0)

    @E.block.gpsimd
    def _(g):
        for i in range(16):
            kind, h = ("k", "v")[i % 2], i // 2
            src = W["wk"] if kind == "k" else W["wv"]
            if i >= 2:
                g.wait_ge(E.sem(ev), (i - 1) * 2)
            E.inc(g.dma_start(out=wkb.ap().rearrange("pp (b c) -> pp b c", b=2)[:, i % 2, :],
                              in_=src[h, :, :].rearrange("(kt pp) c -> pp kt c", pp=128)),
                  p + "wld%d" % (i % 2), 16)

    for i in range(16):
        kind, h = ("k", "v")[i % 2], i // 2

        @E.block.tensor
        def _(t, i=i):
            t.wait_ge(E.sem(p + "wld%d" % (i % 2)), 16 * (i // 2 + 1))
            if i == 0:
                t.wait_ge(E.sem(xl_ready), E.cnt[xl_ready])
            for th in range(NTH):
                ps = ring.acquire(t)
                woff = (i % 2) * KT * 128
                for kt in range(KT):
                    inst = t.matmul(ps[:, :], wkb[:, woff + kt * 128: woff + kt * 128 + 128],
                                    xl[:, kt * NLOC + th * TH: kt * NLOC + th * TH + TH],
                                    start=(kt == 0), stop=(kt == KT - 1))
                ring.finish_tile(inst)

        base_tile = 2 * i

        @E.block.vector
        def _(v, i=i, kind=kind, h=h):
            if i == 0:
                v.wait_ge(E.sem(p + "bld"), 48)
            bc = bkc if kind == "k" else bvc
            for th in range(NTH):
                v.wait_ge(E.sem(mm), base_tile + th + 1)
                ps = ring.banks[(base_tile + th) % 4]
                if kind == "k":
                    inst = v.tensor_scalar(tk0[:, th * TH:(th + 1) * TH], ps[:, :],
                                           bc[:, h:h + 1], None, ALU.add)
                else:
                    inst = v.tensor_scalar(pv[:, th * TH:(th + 1) * TH], ps[:, :],
                                           bc[:, h:h + 1], None, ALU.add)
                E.inc(inst, ev, 1)
            if kind == "k":
                inst = v.tensor_scalar(tkm[:, :], tk0[:, :], 0.0, None, ALU.min)
                E.inc(inst, p + "s1", 1)

        if kind == "k":
            @E.block.scalar
            def _(s, i=i):
                s.wait_ge(E.sem(p + "s1"), i // 2 + 1)
                inst = s.activation(tk2[:, :], tkm[:, :], AF.Exp)
                E.inc(inst, p + "s2", 1)

            @E.block.vector
            def _(v, i=i):
                v.wait_ge(E.sem(p + "s2"), i // 2 + 1)
                v.tensor_scalar(tka[:, :], tk0[:, :], 1.0, None, ALU.add)
                inst = v.tensor_tensor(pk[:, :], tka[:, :], tk2[:, :], ALU.max)
                E.inc(inst, p + "pkd", 1)
        else:
            @E.block.tensor
            def _(t, h=h):
                t.wait_ge(E.sem(p + "pkd"), h + 1)
                t.wait_ge(E.sem(ev), 4 * h + 4)
                t.wait_ge(E.sem("preld"), E.cnt["preld"])
                if h > 0:
                    t.wait_ge(E.sem(p + "kvm"), h)
                    t.wait_ge(E.sem(p + "kvv"), h)
                for tt in range(8):
                    if tt >= 2:
                        t.wait_ge(E.sem(p + "tcv"), 16 * h + 2 * (tt - 1))
                    ps = psA if tt % 2 == 0 else psB
                    t.transpose(ps[0:128, 0:128], pk[:, tt * 128:(tt + 1) * 128],
                                common["eye128"][0:128, 0:128])
                    inst = t.transpose(ps[0:128, 128:256], pv[:, tt * 128:(tt + 1) * 128],
                                       common["eye128"][0:128, 0:128])
                    E.inc(inst, p + "tmm", 1)

            @E.block.vector
            def _(v, h=h):
                for tt in range(8):
                    v.wait_ge(E.sem(p + "tmm"), 8 * h + tt + 1)
                    ps = psA if tt % 2 == 0 else psB
                    v.tensor_copy(phk_tm[:, tt * 128:(tt + 1) * 128], ps[0:128, 0:128])
                    inst = v.tensor_copy(pv_tm[:, tt * 128:(tt + 1) * 128], ps[0:128, 128:256])
                    E.inc(inst, p + "tcv", 2)

            @E.block.tensor
            def _(t, h=h):
                t.wait_ge(E.sem(p + "tcv"), 16 * (h + 1))
                for tt in range(8):
                    t.matmul(psBC[0:128, 0:128], pv_tm[:, tt * 128:(tt + 1) * 128],
                             phk_tm[:, tt * 128:(tt + 1) * 128],
                             start=(tt == 0), stop=(tt == 7))
                for tt in range(8):
                    inst = t.matmul(psBC[0:128, 128:129], phk_tm[:, tt * 128:(tt + 1) * 128],
                                    ones_col[:, 0:1], start=(tt == 0), stop=(tt == 7))
                E.inc(inst, p + "kvm", 1)

            @E.block.vector
            def _(v, h=h):
                v.wait_ge(E.sem(p + "kvm"), h + 1)
                inst = v.tensor_copy(kvagg[:, h * 129:(h + 1) * 129], psBC[0:128, 0:129])
                E.inc(inst, p + "kvv", 1)

    @E.block.sync
    def _(s):
        s.wait_ge(E.sem(p + "kvv"), 8)
        E.inc(s.dma_start(out=cc_in.ap()[:, :], in_=kvagg[:, :]), p + "ccd", 16)

    @E.block.gpsimd
    def _(g):
        g.wait_ge(E.sem(p + "ccd"), 16)
        E.inc(g.collective_compute("AllReduce", ALU.add, replica_groups=GROUPS4,
                                   ins=[cc_in.ap().opt()], outs=[cc_out.ap().opt()]),
              p + "ccs", 1)
        g.wait_ge(E.sem(p + "ccs"), 1)
        E.inc(g.dma_start(out=kvtot[:, :], in_=cc_out.ap()[:, :]), p + "ccl", 16)

    kvt_v = kvtot.ap().rearrange("p (h c) -> p h c", h=8)

    @E.block.vector
    def _(v):
        v.wait_ge(E.sem(p + "ccl"), 16)
        inst = v.tensor_scalar(nplus[:, :], kvt_v[:, :, 128], 1e-10, None, ALU.add)
        E.inc(inst, p + "nv", 1)

    @E.block.gpsimd
    def _(g):
        g.wait_ge(E.sem(p + "nv"), 1)
        E.inc(g.tensor_copy(ncolg[:, :], nplus[:, :]), p + "ng", 1)

    @E.block.scalar
    def _(s):
        s.wait_ge(E.sem(p + "ng"), 1)
        inst = s.activation(rcol[:, :], ncolg[:, :], AF.Ln)
        E.inc(inst, p + "nv", 1)

    @E.block.gpsimd
    def _(g):
        g.wait_ge(E.sem(p + "nv"), 2)
        E.inc(g.tensor_copy(ncolg[:, :], rcol[:, :]), p + "ng", 1)

    @E.block.scalar
    def _(s):
        s.wait_ge(E.sem(p + "ng"), 2)
        inst = s.activation(rcol[:, :], ncolg[:, :], AF.Exp, scale=-1.0)
        E.inc(inst, p + "nv", 1)

    @E.block.gpsimd
    def _(g):
        g.wait_ge(E.sem(p + "nv"), 3)
        E.inc(g.tensor_copy(rcolg[:, :], rcol[:, :]), p + "ng", 1)

    for h in range(8):
        @E.block.vector
        def _(v, h=h):
            if h == 0:
                v.wait_ge(E.sem(p + "ng"), 3)
            if h >= 1:
                v.wait_ge(E.sem(p + "kvtm"), h)
            inst = v.tensor_scalar(kvnf[:, :], kvtot[:, h * 129: h * 129 + 128],
                                   rcolg[:, h:h + 1], None, ALU.mult)
            E.inc(inst, p + "kvtv", 1)

        @E.block.tensor
        def _(t, h=h):
            t.wait_ge(E.sem(p + "kvtv"), h + 1)
            if h >= 2:
                t.wait_ge(E.sem(p + "tcv"), 128 + h - 1)
            ps = psA if h % 2 == 0 else psB
            inst = t.transpose(ps[0:128, 0:128], kvnf[:, :], common["eye128"][0:128, 0:128])
            E.inc(inst, p + "kvtm", 1)

        @E.block.vector
        def _(v, h=h):
            v.wait_ge(E.sem(p + "kvtm"), h + 1)
            ps = psA if h % 2 == 0 else psB
            inst = v.tensor_copy(kvn_dv[:, h * 128:(h + 1) * 128], ps[0:128, 0:128])
            E.inc(inst, p + "tcv", 1)

    @E.block.gpsimd
    def _(g):
        for qh in range(32):
            need = 32 + 4 * (qh - 2) + 2 if qh >= 2 else 32
            g.wait_ge(E.sem(ev), need)
            E.inc(g.dma_start(out=wkb.ap().rearrange("pp (b c) -> pp b c", b=2)[:, qh % 2, :],
                              in_=W["wq"][qh, :, :].rearrange("(kt pp) c -> pp kt c", pp=128)),
                  p + "wld%d" % (qh % 2), 16)

    odr_t = o_dram.rearrange("(qh pp) n -> qh pp n", pp=128)
    for qh in range(32):
        base_tile = 32 + 4 * qh
        h = qh // 4

        @E.block.tensor
        def _(t, qh=qh):
            t.wait_ge(E.sem(p + "wld%d" % (qh % 2)), 16 * (8 + qh // 2 + 1))
            for th in range(NTH):
                ps = ring.acquire(t)
                woff = (qh % 2) * KT * 128
                for kt in range(KT):
                    inst = t.matmul(ps[:, :], wkb[:, woff + kt * 128: woff + kt * 128 + 128],
                                    xl[:, kt * NLOC + th * TH: kt * NLOC + th * TH + TH],
                                    start=(kt == 0), stop=(kt == KT - 1))
                ring.finish_tile(inst)

        @E.block.vector
        def _(v, qh=qh):
            for th in range(NTH):
                v.wait_ge(E.sem(mm), base_tile + th + 1)
                ps = ring.banks[(base_tile + th) % 4]
                inst = v.tensor_scalar(tk0[:, th * TH:(th + 1) * TH], ps[:, :],
                                       bqc[:, qh:qh + 1], None, ALU.add)
                E.inc(inst, ev, 1)
            inst = v.tensor_scalar(tkm[:, :], tk0[:, :], 0.0, None, ALU.min)
            E.inc(inst, p + "s1", 1)

        @E.block.scalar
        def _(s, qh=qh):
            s.wait_ge(E.sem(p + "s1"), 8 + qh + 1)
            inst = s.activation(tk2[:, :], tkm[:, :], AF.Exp)
            E.inc(inst, p + "s2", 1)

        @E.block.vector
        def _(v, qh=qh):
            v.wait_ge(E.sem(p + "s2"), 8 + qh + 1)
            if qh > 0:
                v.wait_ge(E.sem(mm), 32 + 4 * qh)
            v.tensor_scalar(tka[:, :], tk0[:, :], 1.0, None, ALU.add)
            inst = v.tensor_tensor(pq[:, :], tka[:, :], tk2[:, :], ALU.max)
            E.inc(inst, p + "pqd", 1)

        @E.block.tensor
        def _(t, qh=qh, h=h):
            t.wait_ge(E.sem(p + "pqd"), qh + 1)
            for th in range(NTH):
                ps = ring.acquire(t)
                inst = t.matmul(ps[:, :], kvn_dv[:, h * 128:(h + 1) * 128],
                                pq[:, th * TH:(th + 1) * TH], start=True, stop=True)
                ring.finish_tile(inst)

        @E.block.vector
        def _(v, qh=qh):
            for th in range(NTH):
                j = base_tile + 2 + th
                v.wait_ge(E.sem(mm), j + 1)
                if qh > 0:
                    v.wait_ge(E.sem(p + "ostd%d" % th), 16 * qh)
                ps = ring.banks[j % 4]
                inst = v.tensor_copy(ost[:, th * TH:(th + 1) * TH], ps[:, :])
                E.inc(inst, ev, 1)

        @E.block.sync
        def _(s, qh=qh):
            for th in range(NTH):
                s.wait_ge(E.sem(ev), 32 + 4 * qh + 3 + th)
                E.inc(s.dma_start(out=odr_t[qh, :, th * TH:(th + 1) * TH],
                                  in_=ost[:, th * TH:(th + 1) * TH]), p + "ostd%d" % th, 16)

    total_mm = ring.tile

    @E.block.sync
    def _(s):
        s.wait_ge(E.sem(p + "ostd"), E.cnt[p + "ostd"])
        s.wait_ge(E.sem(mm), total_mm)
        E.inc(s.dma_start(out=xl[:, :], in_=o_dram.rearrange("(kt pp) n -> pp kt n", pp=128)),
              p + "old", 16)

    @E.block.gpsimd
    def _(g):
        g.wait_ge(E.sem(p + "ostd"), E.cnt[p + "ostd"])
        g.wait_ge(E.sem(mm), total_mm)

    @E.block.sync
    def _(s):
        E.inc(s.dma_start(out=bo_sb[:, :], in_=W["bo"][:, :]), "preld", 16)

    astack.close()
    emit_wstream_proj(E, p + "wo_", xin, xout, W["wo"], xl,
                      lambda kt, th: kt * NLOC + th * TH, KT, 8, common,
                      bias_sb=bo_sb, ready=((p + "old", 16),))


def build_l1attn(x_arrs, inp):
    E = Env()
    W = _l1attn_host(inp)
    with E.stack:
        xin = E.din("xin", x_arrs)
        Wd = {k: E.din("w_" + k, v) for k, v in W.items()}
        eye_d = E.din("eye_d", eye_groups())
        e128 = np.eye(128, dtype=np.float32)
        eye128_d = E.din("eye128_d", e128)
        eye128b_d = E.din("eye128b_d", e128.astype(BF))
        xout = E.dout("xout", [D, NLOC], F32)

        common = _common_alloc(E)
        common["eye128"] = E.sbuf("eye128", [128, 128], F32)
        common["eye128b"] = E.sbuf("eye128b", [128, 128], BF16)
        xl = E.sbuf("xl", [128, KT * NLOC], BF16)
        sq = E.sbuf("sq", [128, 2 * NLOC], BF16)
        stat = E.sbuf("stat", [1, 4 * NLOC], F32)
        rstd_bc = E.sbuf("rstd_bc", [128, NLOC], BF16)
        mbc_sb = E.sbuf("mbc_sb", [128, NLOC], F32)
        mid = E.sbuf("mid", [128, TH], F32)
        rbf_row = E.sbuf("rbf_row", [1, NLOC], BF16)
        astack = ExitStack()

        with E.nc.Block() as block:
            E.block = block
            emit_preamble(E, common, eye_d)

            @E.block.sync
            def _(s):
                E.inc(s.dma_start(out=common["eye128"][:, :], in_=eye128_d[:, :]), "preld", 16)
                E.inc(s.dma_start(out=common["eye128b"][:, :], in_=eye128b_d[:, :]), "preld", 16)

            xl_ready = emit_ln(E, "ln_", xin, xl, sq, stat, rstd_bc, mbc_sb, mid, rbf_row,
                               common["ones_col"], common["ones_row"],
                               common["psA"], common["psB"], common["psBC"], common["eps_col"])
            emit_l1attn(E, "a1_", xin, xout, Wd, xl_ready, xl, common, astack)
        return E


def run_l1attn(x, inp):
    E = build_l1attn(shard_x(x), inp)
    res = E.run()
    return unshard_x([res.results[c]["xout"] for c in range(NCORES)])



def _l2attn_host(inp):
    W = {
        "wq": wblocks_km(np.ascontiguousarray(np.asarray(inp["L2_attn_wq"]).T), 128),
        "wk": wblocks_km(np.ascontiguousarray(np.asarray(inp["L2_attn_wk"]).T), 128),
        "wv": wblocks_km(np.ascontiguousarray(np.asarray(inp["L2_attn_wv"]).T), 128),
        "wo": wblocks_km(np.ascontiguousarray(np.asarray(inp["L2_attn_wo"]).T), 512),
    }
    tri_u = (np.arange(128)[:, None] <= np.arange(128)[None, :]).astype(np.float32)
    tri_sl = (np.arange(128)[:, None] > np.arange(128)[None, :]).astype(np.float32)
    masks = []
    flags = []
    for c in range(NCORES):
        q = c % 4
        m = np.zeros((8, 5, 128, 128), np.float32)
        for tc in range(8):
            for j in range(5):
                if q * 8 + tc - 4 + j < 0:
                    continue
                m[tc, j] = tri_u if j == 4 else (tri_sl if j == 0 else 1.0)
        masks.append(np.ascontiguousarray(m.transpose(2, 0, 1, 3).reshape(128, 8 * 5 * 128)).astype(BF))
        f = np.zeros((128, 4), np.float32)
        if q > 0:
            f[:, q - 1] = 1.0
        flags.append(f.astype(BF))
    W["masks"] = masks
    W["flags"] = flags
    return W


def emit_l2attn(E, p, xin, xout, W, xl_ready, xl, common, astack):
    (psA, psB, psBC, mm_banks) = (common["psA"], common["psB"], common["psBC"], common["mm_banks"])
    ones_col = common["ones_col"]
    eye128 = common["eye128"]

    wkb = E.sbuf(p + "wkb", [128, 2 * KT * 128], BF16, astack)
    k_dm = E.sbuf(p + "kdm", [128, 8 * 1536], BF16, astack)
    v_tm = E.sbuf(p + "vtm", [128, 8 * 1536], BF16, astack)
    vtmp = E.sbuf(p + "vtp", [128, NLOC], F32, astack)
    msk_sb = E.sbuf(p + "msk", [128, 8 * 640], BF16, astack)
    flg_sb = E.sbuf(p + "flg", [128, 4], BF16, astack)
    acc = E.sbuf(p + "acc", [128, 4096], BF16, astack)
    tmg = E.sbuf(p + "tmg", [128, 4096], BF16, astack)
    pq = E.sbuf(p + "pq", [128, NLOC], BF16, astack)
    e_m = E.sbuf(p + "em", [128, 640], BF16, astack)
    rr = E.sbuf(p + "rr", [1, 128], F32, astack)
    rrg = E.sbuf(p + "rrg", [1, 128], F32, astack)
    rrec = E.sbuf(p + "rrc", [1, 128], F32, astack)
    rbc = E.sbuf(p + "rbc", [128, 128], F32, astack)
    ost = E.sbuf(p + "ost", [128, 2 * 128], BF16, astack)

    o_dram = E.dram(p + "odr", [D, NLOC], BF16)
    cc_in = E.nc.dram_tensor(p + "ccin", [128, 8192], BF16)
    cc_out = E.nc.dram_tensor(p + "ccout", [4, 128, 8192], BF16)

    mm = p + "mm"; ev = p + "ev"
    for nm in ("wld0", "wld1", "mld", "fld", "tmm", "tcv", "pkc", "ccd", "ccs",
               "mgl", "mgv", "khc", "smm", "sexp", "emsk", "csm",
               "rr1", "rr2", "rr3", "bcm", "bcs", "ostd0", "ostd1", "old"):
        E.sem(p + nm)
    ring = PsumRing(E, mm_banks, mm, ev)
    SC = float(1.0 / np.sqrt(128.0))

    @E.block.sync
    def _(s):
        mv = W["masks"]
        E.inc(s.dma_start(out=msk_sb[:, 0:4 * 640], in_=mv[:, 0:4 * 640]), p + "mld", 16)
        E.inc(s.dma_start(out=msk_sb[:, 4 * 640:8 * 640], in_=mv[:, 4 * 640:8 * 640]), p + "mld", 16)
        E.inc(s.dma_start(out=flg_sb[:, :], in_=W["flags"][:, :]), p + "fld", 16)

    @E.block.gpsimd
    def _(g):
        for i in range(16):
            kind, h = ("k", "v")[i % 2], i // 2
            src = W["wk"] if kind == "k" else W["wv"]
            if i >= 2:
                g.wait_ge(E.sem(ev), (i - 1) * 2)
            E.inc(g.dma_start(out=wkb.ap().rearrange("pp (b c) -> pp b c", b=2)[:, i % 2, :],
                              in_=src[h, :, :].rearrange("(kt pp) c -> pp kt c", pp=128)),
                  p + "wld%d" % (i % 2), 16)

    for i in range(16):
        kind, h = ("k", "v")[i % 2], i // 2
        base_tile = 2 * i

        @E.block.tensor
        def _(t, i=i):
            t.wait_ge(E.sem(p + "wld%d" % (i % 2)), 16 * (i // 2 + 1))
            if i == 0:
                t.wait_ge(E.sem(xl_ready), E.cnt[xl_ready])
            for th in range(NTH):
                ps = ring.acquire(t)
                woff = (i % 2) * KT * 128
                for kt in range(KT):
                    inst = t.matmul(ps[:, :], wkb[:, woff + kt * 128: woff + kt * 128 + 128],
                                    xl[:, kt * NLOC + th * TH: kt * NLOC + th * TH + TH],
                                    start=(kt == 0), stop=(kt == KT - 1))
                ring.finish_tile(inst)

        if kind == "k":
            @E.block.vector
            def _(v, h=h, base_tile=base_tile):
                for th in range(NTH):
                    v.wait_ge(E.sem(mm), base_tile + th + 1)
                    ps = ring.banks[(base_tile + th) % 4]
                    inst = v.tensor_copy(k_dm[:, h * 1536 + 512 + th * TH: h * 1536 + 512 + (th + 1) * TH],
                                         ps[:, :])
                    E.inc(inst, ev, 1)
        else:
            @E.block.vector
            def _(v, h=h, base_tile=base_tile):
                for th in range(NTH):
                    v.wait_ge(E.sem(mm), base_tile + th + 1)
                    if h > 0:
                        v.wait_ge(E.sem(p + "tcv"), 8 * h)
                    ps = ring.banks[(base_tile + th) % 4]
                    inst = v.tensor_copy(vtmp[:, th * TH:(th + 1) * TH], ps[:, :])
                    E.inc(inst, p + "pkc", 1)
                    E.inc(v.tensor_copy(rr[0:1, 0:1], ps[0:1, 0:1]), ev, 1)

            @E.block.tensor
            def _(t, h=h):
                t.wait_ge(E.sem(p + "pkc"), 2 * (h + 1))
                if h == 0:
                    t.wait_ge(E.sem("preld"), E.cnt["preld"])
                for tt in range(8):
                    if tt >= 2:
                        t.wait_ge(E.sem(p + "tcv"), 8 * h + tt - 1)
                    ps = psA if tt % 2 == 0 else psB
                    inst = t.transpose(ps[0:128, 0:128], vtmp[:, tt * 128:(tt + 1) * 128],
                                       eye128[0:128, 0:128])
                    E.inc(inst, p + "tmm", 1)

            @E.block.vector
            def _(v, h=h):
                for tt in range(8):
                    v.wait_ge(E.sem(p + "tmm"), 8 * h + tt + 1)
                    ps = psA if tt % 2 == 0 else psB
                    inst = v.tensor_copy(v_tm[:, h * 1536 + (4 + tt) * 128: h * 1536 + (5 + tt) * 128],
                                         ps[0:128, 0:128])
                    E.inc(inst, p + "tcv", 1)

    @E.block.sync
    def _(s):
        s.wait_ge(E.sem(ev), 32)
        s.wait_ge(E.sem(p + "tcv"), 64)
        for h in range(8):
            E.inc(s.dma_start(out=cc_in.ap()[:, h * TH:(h + 1) * TH],
                              in_=k_dm[:, h * 1536 + 1024: h * 1536 + 1536]), p + "ccd", 16)
            E.inc(s.dma_start(out=cc_in.ap()[:, 4096 + h * TH: 4096 + (h + 1) * TH],
                              in_=v_tm[:, h * 1536 + 1024: h * 1536 + 1536]), p + "ccd", 16)

    @E.block.gpsimd
    def _(g):
        g.wait_ge(E.sem(p + "ccd"), 16 * 16)
        E.inc(g.collective_compute("AllGather", ALU.bypass, replica_groups=GROUPS4,
                                   ins=[cc_in.ap().opt()], outs=[cc_out.ap().opt()]),
              p + "ccs", 1)
        g.wait_ge(E.sem(p + "ccs"), 1)
        for half in range(2):
            for c in range(4):
                idx = half * 4 + c
                if idx >= 1:
                    g.wait_ge(E.sem(p + "mgv"), idx)
                E.inc(g.dma_start(out=tmg[:, :],
                                  in_=cc_out.ap()[c, :, half * 4096:(half + 1) * 4096]),
                      p + "mgl", 16)

    for half in range(2):
        @E.block.vector
        def _(v, half=half):
            if half == 0:
                v.wait_ge(E.sem(p + "fld"), 16)
                v.memset(acc[:, :], 0.0)
            for c in range(4):
                idx = half * 4 + c
                v.wait_ge(E.sem(p + "mgl"), 16 * (idx + 1))
                inst = v.scalar_tensor_tensor(acc[:, :], tmg[:, :], flg_sb[:, c:c + 1], acc[:, :],
                                              ALU.mult, ALU.add)
                E.inc(inst, p + "mgv", 1)

        @E.block.vector
        def _(v, half=half):
            dst = k_dm if half == 0 else v_tm
            for h in range(8):
                inst = v.tensor_copy(dst[:, h * 1536: h * 1536 + 512], acc[:, h * TH:(h + 1) * TH])
                E.inc(inst, p + "khc", 1)
            if half == 0:
                inst = v.memset(acc[:, :], 0.0)

    @E.block.gpsimd
    def _(g):
        for qh in range(32):
            need = 32 + 10 * (qh - 2) + 2 if qh >= 2 else 32
            g.wait_ge(E.sem(ev), need)
            E.inc(g.dma_start(out=wkb.ap().rearrange("pp (b c) -> pp b c", b=2)[:, qh % 2, :],
                              in_=W["wq"][qh, :, :].rearrange("(kt pp) c -> pp kt c", pp=128)),
                  p + "wld%d" % (qh % 2), 16)

    odr_t = o_dram.rearrange("(qh pp) n -> qh pp n", pp=128)
    for qh in range(32):
        h = qh // 4
        base_tile = 32 + 10 * qh

        @E.block.tensor
        def _(t, qh=qh):
            t.wait_ge(E.sem(p + "wld%d" % (qh % 2)), 16 * (8 + qh // 2 + 1))
            for th in range(NTH):
                ps = ring.acquire(t)
                woff = (qh % 2) * KT * 128
                for kt in range(KT):
                    inst = t.matmul(ps[:, :], wkb[:, woff + kt * 128: woff + kt * 128 + 128],
                                    xl[:, kt * NLOC + th * TH: kt * NLOC + th * TH + TH],
                                    start=(kt == 0), stop=(kt == KT - 1))
                ring.finish_tile(inst)

        @E.block.vector
        def _(v, qh=qh, base_tile=base_tile):
            for th in range(NTH):
                v.wait_ge(E.sem(mm), base_tile + th + 1)
                if qh > 0:
                    v.wait_ge(E.sem(p + "smm"), 8 * qh)
                ps = ring.banks[(base_tile + th) % 4]
                inst = v.tensor_copy(pq[:, th * TH:(th + 1) * TH], ps[:, :])
                E.inc(inst, ev, 1)

        for tc in range(8):
            sidx = qh * 8 + tc

            @E.block.tensor
            def _(t, qh=qh, tc=tc, h=h, sidx=sidx, base_tile=base_tile):
                if tc == 0:
                    t.wait_ge(E.sem(ev), base_tile + 2)
                    if qh == 0:
                        t.wait_ge(E.sem(p + "khc"), 16)
                        t.wait_ge(E.sem(p + "mld"), 32)
                if sidx > 0:
                    t.wait_ge(E.sem(p + "sexp"), 2 * sidx)
                for j in range(5):
                    ps = psA[0:128, (j % 4) * 128:(j % 4 + 1) * 128] if j < 4 else psB[0:128, 0:128]
                    inst = t.matmul(ps, k_dm[:, h * 1536 + (tc + j) * 128: h * 1536 + (tc + j + 1) * 128],
                                    pq[:, tc * 128:(tc + 1) * 128], start=True, stop=True)
                E.inc(inst, p + "smm", 1)

            @E.block.scalar
            def _(s, sidx=sidx):
                s.wait_ge(E.sem(p + "smm"), sidx + 1)
                if sidx > 0:
                    s.wait_ge(E.sem(p + "emsk"), sidx)
                    q1, t1 = divmod(sidx - 1, 8)
                    s.wait_ge(E.sem(mm), 32 + 10 * q1 + 2 + t1 + 1)
                s.activation(e_m[:, 0:512], psA[0:128, :], AF.Exp, scale=SC)
                inst = s.activation(e_m[:, 512:640], psB[0:128, 0:128], AF.Exp, scale=SC)
                E.inc(inst, p + "sexp", 2)

            @E.block.vector
            def _(v, tc=tc, sidx=sidx):
                v.wait_ge(E.sem(p + "sexp"), 2 * (sidx + 1))
                inst = v.tensor_tensor(e_m[:, :], e_m[:, :],
                                       msk_sb[:, tc * 640:(tc + 1) * 640], ALU.mult)
                E.inc(inst, p + "emsk", 1)

            @E.block.tensor
            def _(t, sidx=sidx):
                t.wait_ge(E.sem(p + "emsk"), sidx + 1)
                if sidx > 0:
                    q1, t1 = divmod(sidx - 1, 8)
                    t.wait_ge(E.sem(ev), 32 + 10 * q1 + 2 + t1 + 1)
                for j in range(5):
                    inst = t.matmul(psBC[0:1, 0:128], ones_col[:, 0:1],
                                    e_m[:, j * 128:(j + 1) * 128], start=(j == 0), stop=(j == 4))
                E.inc(inst, p + "csm", 1)

            @E.block.scalar
            def _(s, sidx=sidx):
                s.wait_ge(E.sem(p + "csm"), sidx + 1)
                inst = s.activation(rr[0:1, :], psBC[0:1, 0:128], AF.Ln)
                E.inc(inst, p + "rr1", 1)

            @E.block.vector
            def _(v, sidx=sidx):
                v.wait_ge(E.sem(p + "rr1"), sidx + 1)
                inst = v.tensor_copy(rrg[0:1, :], rr[0:1, :])
                E.inc(inst, p + "rr2", 1)

            @E.block.scalar
            def _(s, sidx=sidx):
                s.wait_ge(E.sem(p + "rr2"), sidx + 1)
                inst = s.activation(rrec[0:1, :], rrg[0:1, :], AF.Exp, scale=-1.0)
                E.inc(inst, p + "rr3", 1)
                s.wait_ge(E.sem(p + "bcm"), sidx + 1)
                inst = s.activation(rbc[:, :], psBC[0:128, 128:256], AF.Copy)
                E.inc(inst, p + "bcs", 1)

            @E.block.tensor
            def _(t, qh=qh, tc=tc, h=h, sidx=sidx):
                t.wait_ge(E.sem(p + "rr3"), sidx + 1)
                E.inc(t.matmul(psBC[0:128, 128:256], common["ones_row"][0:1, :],
                               rrec[0:1, :], start=True, stop=True), p + "bcm", 1)
                ps = ring.acquire(t)
                for j in range(5):
                    inst = t.matmul(ps[:, 0:128], v_tm[:, h * 1536 + (tc + j) * 128: h * 1536 + (tc + j + 1) * 128],
                                    e_m[:, j * 128:(j + 1) * 128], start=(j == 0), stop=(j == 4))
                ring.finish_tile(inst)

            @E.block.vector
            def _(v, qh=qh, tc=tc, sidx=sidx, base_tile=base_tile):
                j = base_tile + 2 + tc
                v.wait_ge(E.sem(mm), j + 1)
                v.wait_ge(E.sem(p + "bcs"), sidx + 1)
                if sidx >= 2:
                    v.wait_ge(E.sem(p + "ostd%d" % (tc % 2)), 16 * (sidx // 2))
                ps = ring.banks[j % 4]
                inst = v.tensor_tensor(ost[:, (tc % 2) * 128:(tc % 2 + 1) * 128],
                                       ps[:, 0:128], rbc[:, :], ALU.mult)
                E.inc(inst, ev, 1)

            @E.block.sync
            def _(s, qh=qh, tc=tc):
                s.wait_ge(E.sem(ev), 32 + 10 * qh + 3 + tc)
                E.inc(s.dma_start(out=odr_t[qh, :, tc * 128:(tc + 1) * 128],
                                  in_=ost[:, (tc % 2) * 128:(tc % 2 + 1) * 128]),
                      p + "ostd%d" % (tc % 2), 16)

    total_mm = ring.tile

    @E.block.sync
    def _(s):
        s.wait_ge(E.sem(p + "ostd0"), E.cnt[p + "ostd0"])
        s.wait_ge(E.sem(p + "ostd1"), E.cnt[p + "ostd1"])
        s.wait_ge(E.sem(mm), total_mm)
        E.inc(s.dma_start(out=xl[:, :], in_=o_dram.rearrange("(kt pp) n -> pp kt n", pp=128)),
              p + "old", 16)

    @E.block.gpsimd
    def _(g):
        g.wait_ge(E.sem(p + "ostd0"), E.cnt[p + "ostd0"])
        g.wait_ge(E.sem(p + "ostd1"), E.cnt[p + "ostd1"])
        g.wait_ge(E.sem(mm), total_mm)

    astack.close()
    emit_wstream_proj(E, p + "wo_", xin, xout, W["wo"], xl,
                      lambda kt, th: kt * NLOC + th * TH, KT, 8, common,
                      bias_sb=None, ready=((p + "old", 16),))


def build_l2attn(x_arrs, inp):
    E = Env()
    W = _l2attn_host(inp)
    with E.stack:
        xin = E.din("xin", x_arrs)
        Wd = {k: E.din("w_" + k, v) for k, v in W.items()}
        eye_d = E.din("eye_d", eye_groups())
        e128 = np.eye(128, dtype=np.float32)
        eye128_d = E.din("eye128_d", e128)
        xout = E.dout("xout", [D, NLOC], F32)

        common = _common_alloc(E)
        common["eye128"] = E.sbuf("eye128", [128, 128], F32)
        xl = E.sbuf("xl", [128, KT * NLOC], BF16)
        sq = E.sbuf("sq", [128, 2 * NLOC], BF16)
        stat = E.sbuf("stat", [1, 4 * NLOC], F32)
        rstd_bc = E.sbuf("rstd_bc", [128, NLOC], BF16)
        mbc_sb = E.sbuf("mbc_sb", [128, NLOC], F32)
        mid = E.sbuf("mid", [128, TH], F32)
        rbf_row = E.sbuf("rbf_row", [1, NLOC], BF16)
        astack = ExitStack()

        with E.nc.Block() as block:
            E.block = block
            emit_preamble(E, common, eye_d)

            @E.block.sync
            def _(s):
                E.inc(s.dma_start(out=common["eye128"][:, :], in_=eye128_d[:, :]), "preld", 16)

            xl_ready = emit_ln(E, "ln_", xin, xl, sq, stat, rstd_bc, mbc_sb, mid, rbf_row,
                               common["ones_col"], common["ones_row"],
                               common["psA"], common["psB"], common["psBC"], common["eps_col"])
            emit_l2attn(E, "a2_", xin, xout, Wd, xl_ready, xl, common, astack)
        return E


def run_l2attn(x, inp):
    E = build_l2attn(shard_x(x), inp)
    res = E.run()
    return unshard_x([res.results[c]["xout"] for c in range(NCORES)])



DTR = 128
NST = 16


def _l3attn_host(inp):
    xpwT = np.ascontiguousarray(np.asarray(inp["L3_attn_x_proj_w"]).T)
    dtwT = np.ascontiguousarray(np.asarray(inp["L3_attn_dt_proj_w"]).T)
    alog = np.asarray(inp["L3_attn_A_log"])
    W = {
        "xpw": np.ascontiguousarray(xpwT.reshape(KT, 128, 160)),
        "dtw": dtwT,
        "alog": np.ascontiguousarray(alog.reshape(KT, 128, NST).transpose(1, 0, 2).reshape(128, KT * NST)),
        "dtb": bias_cols(inp["L3_attn_dt_proj_b"]),
        "dcol": bias_cols(inp["L3_attn_D"]),
    }
    flags = []
    for c in range(NCORES):
        q = c % 4
        f = np.zeros((128, 3), np.float32)
        f[:, :q] = 1.0
        flags.append(f)
    W["flg3"] = flags
    return W


def emit_l3attn(E, p, xin, xout, W, xl_ready, xl, common, astack):
    (psA, psB, psBC, mm_banks) = (common["psA"], common["psB"], common["psBC"], common["mm_banks"])
    ones_col = common["ones_col"]

    xpw_sb = E.sbuf(p + "xpw", [128, 2 * 160], BF16, astack)
    dtw_sb = E.sbuf(p + "dtw", [128, 2 * 128], BF16, astack)
    dti = E.sbuf(p + "dti", [128, NLOC], BF16, astack)
    bc_sb = E.sbuf(p + "bcs", [32, NLOC], F32, astack)
    alog_sb = E.sbuf(p + "alg", [128, KT * NST], F32, astack)
    asb = E.sbuf(p + "asb", [128, KT * NST], F32, astack)
    dtb_sb = E.sbuf(p + "dtb", [128, 32], F32, astack)
    dsb = E.sbuf(p + "dsb", [128, 32], F32, astack)
    flg3 = E.sbuf(p + "flg", [128, 3], F32, astack)
    tsp0 = E.sbuf(p + "t0", [128, NLOC], F32, astack)
    texp = E.sbuf(p + "te", [128, NLOC], F32, astack)
    tln = E.sbuf(p + "tl", [128, NLOC], F32, astack)
    dt_dc = E.sbuf(p + "dtc", [128, NLOC], F32, astack)
    xlnf = E.sbuf(p + "xlf", [128, NLOC], F32, astack)
    dtx = E.sbuf(p + "dtx", [128, NLOC], F32, astack)
    dasb = E.sbuf(p + "da", [128, 2 * NLOC], F32, astack)
    dbsb = E.sbuf(p + "db", [128, 2 * NLOC], F32, astack)
    bbc = E.sbuf(p + "bbc", [128, 2 * NLOC], F32, astack)
    cbc = E.sbuf(p + "cbc", [128, 2 * NLOC], F32, astack)
    hsb_v = E.sbuf(p + "hv", [128, 2 * NLOC], F32, astack)
    yprod = E.sbuf(p + "yp", [128, NLOC], F32, astack)
    yacc = E.sbuf(p + "ya", [128, NLOC], F32, astack)
    hloc = E.sbuf(p + "hl", [128, 512], F32, astack)
    ssum = E.sbuf(p + "ss", [128, 32], F32, astack)
    hin = E.sbuf(p + "hi", [128, 512], F32, astack)
    hing = E.sbuf(p + "hig", [128, 512], F32, astack)
    stil = E.sbuf(p + "st", [128, 32], F32, astack)
    ep = E.sbuf(p + "ep", [128, 512], F32, astack)
    hlt = E.sbuf(p + "hlt", [128, 512], F32, astack)
    hgb = E.sbuf(p + "hgb", [128, 544], F32, astack)
    zcol = E.sbuf(p + "zc", [128, 1], F32, astack)
    onec = E.sbuf(p + "onc", [128, 1], F32, astack)
    xres = E.sbuf(p + "xr", [128, 2 * NLOC], F32, astack)
    outsb = E.sbuf(p + "ou", [128, NLOC], F32, astack)

    bc_dram = E.dram(p + "bcd", [32, NLOC], F32)
    cc_in = E.nc.dram_tensor(p + "ccin", [128, 544], F32)
    cc_out = E.nc.dram_tensor(p + "ccout", [4, 128, 544], F32)

    mm = p + "mm"; ev = p + "ev"
    for nm in ("pld", "xwl", "dtl0", "dtl1", "asv", "bcv", "bcw", "sp1", "sp2", "dtd",
               "xlc", "dtxd", "dan", "bbl0", "bbl1", "cbl0", "cbl1", "dbd", "scv", "scg",
               "ym", "ya", "hlv", "hlg", "ssv", "ccd", "ccs", "hgl",
               "stl", "epx", "hiv", "std", "rl0", "rl1"):
        E.sem(p + nm)
    ring = PsumRing(E, mm_banks, mm, ev)
    xin_t = xin.rearrange("(mt pp) n -> mt pp n", pp=128)
    xout_t = xout.rearrange("(mt pp) n -> mt pp n", pp=128)

    @E.block.sync
    def _(s):
        E.inc(s.dma_start(out=alog_sb[:, :], in_=W["alog"][:, :]), p + "pld", 16)
        E.inc(s.dma_start(out=dtb_sb[:, :], in_=W["dtb"][:, :]), p + "pld", 16)
        E.inc(s.dma_start(out=dsb[:, :], in_=W["dcol"][:, :]), p + "pld", 16)
        E.inc(s.dma_start(out=flg3[:, :], in_=W["flg3"][:, :]), p + "pld", 16)

    @E.block.scalar
    def _(s):
        s.wait_ge(E.sem(p + "pld"), 64)
        inst = s.activation(asb[:, :], alog_sb[:, :], AF.Exp)
        E.inc(inst, p + "asv", 1)

    @E.block.vector
    def _(v):
        v.memset(zcol[:, :], 0.0)
        v.memset(onec[:, :], 1.0)
        v.wait_ge(E.sem(p + "asv"), 1)
        inst = v.tensor_scalar(asb[:, :], asb[:, :], -1.0, None, ALU.mult)
        E.inc(inst, p + "asv", 1)

    E.sem(p + "xpk")

    @E.block.gpsimd
    def _(g):
        for kt in range(KT):
            if kt >= 2:
                g.wait_ge(E.sem(p + "xpk"), kt - 1)
            E.inc(g.dma_start(out=xpw_sb.ap().rearrange("pp (b c) -> pp b c", b=2)[:, kt % 2, :],
                              in_=W["xpw"][kt, :, :]), p + "xwl", 16)

    @E.block.tensor
    def _(t):
        t.wait_ge(E.sem(xl_ready), E.cnt[xl_ready])
        ps0, ps1 = ring.banks[0], ring.banks[1]
        in0 = in1 = None
        for kt in range(KT):
            t.wait_ge(E.sem(p + "xwl"), 16 * (kt + 1))
            for th in range(NTH):
                ps = ps0 if th == 0 else ps1
                inst = t.matmul(ps[:, :], xpw_sb[:, (kt % 2) * 160:(kt % 2) * 160 + 128],
                                xl[:, kt * NLOC + th * TH: kt * NLOC + th * TH + TH],
                                start=(kt == 0), stop=(kt == KT - 1))
                if th == 0:
                    in0 = inst
                else:
                    in1 = inst
                bcp = psBC if th == 0 else psA
                inst = t.matmul(bcp[0:32, :], xpw_sb[:, (kt % 2) * 160 + 128:(kt % 2) * 160 + 160],
                                xl[:, kt * NLOC + th * TH: kt * NLOC + th * TH + TH],
                                start=(kt == 0), stop=(kt == KT - 1))
                if th == 1:
                    E.inc(inst, p + "xpk", 1)
        E.inc(in0, mm, 1)
        E.inc(in1, mm, 1)
        ring.tile += 2

    @E.block.vector
    def _(v):
        v.wait_ge(E.sem(mm), 2)
        v.tensor_copy(dti[:, 0:TH], ring.banks[0][:, :])
        inst = v.tensor_copy(dti[:, TH:NLOC], ring.banks[1][:, :])
        E.inc(inst, ev, 2)
        v.tensor_copy(bc_sb[0:32, 0:TH], psBC[0:32, :])
        inst = v.tensor_copy(bc_sb[0:32, TH:NLOC], psA[0:32, :])
        E.inc(inst, p + "bcv", 1)

    @E.block.sync
    def _(s):
        s.wait_ge(E.sem(p + "bcv"), 1)
        E.inc(s.dma_start(out=bc_dram[:, :], in_=bc_sb[:, :]), p + "bcw", 16)

    def emit_dtw_load(gidx):
        @E.block.gpsimd
        def _(g):
            if gidx >= 2:
                g.wait_ge(E.sem(p + "dtd"), gidx - 1)
            E.inc(g.dma_start(out=dtw_sb.ap().rearrange("pp (b c) -> pp b c", b=2)[:, gidx % 2, :],
                              in_=W["dtw"][:, (gidx % 32) * 128:(gidx % 32 + 1) * 128]),
                  p + "dtl%d" % (gidx % 2), 16)

    def emit_pass(pass_idx):
        want_y = pass_idx == 1
        dclim = getattr(E, "_l3_dclim", KT)
        nlim = getattr(E, "_l3_nlim", NST)
        for dc in range(dclim):
            g = pass_idx * KT + dc
            base_tile = 2 + 2 * g
            if dc == 0:
                emit_dtw_load(g)
                emit_dtw_load(g + 1)
            if dc + 2 < KT:
                emit_dtw_load(g + 2)

            @E.block.tensor
            def _(t, dc=dc, g=g):
                t.wait_ge(E.sem(p + "dtl%d" % (g % 2)), 16 * (g // 2 + 1))
                for th in range(NTH):
                    ps = ring.acquire(t)
                    inst = t.matmul(ps[:, :], dtw_sb[:, (g % 2) * 128:(g % 2) * 128 + 128],
                                    dti[:, th * TH:(th + 1) * TH], start=True, stop=True)
                    ring.finish_tile(inst)

            @E.block.vector
            def _(v, dc=dc, g=g, base_tile=base_tile):
                for th in range(NTH):
                    v.wait_ge(E.sem(mm), base_tile + th + 1)
                    ps = ring.banks[(base_tile + th) % 4]
                    inst = v.tensor_scalar(tsp0[:, th * TH:(th + 1) * TH], ps[:, :],
                                           dtb_sb[:, dc:dc + 1], None, ALU.add)
                    E.inc(inst, ev, 1)
                v.tensor_scalar(texp[:, :], tsp0[:, :], 0.0, None, ALU.min)
                inst = v.scalar_tensor_tensor(texp[:, :], texp[:, :], -2.0, tsp0[:, :],
                                              ALU.mult, ALU.add)
                E.inc(inst, p + "sp1", 1)

            @E.block.scalar
            def _(s, dc=dc, g=g):
                s.wait_ge(E.sem(p + "sp1"), g + 1)
                s.activation(tln[:, :], texp[:, :], AF.Exp, scale=-1.0)
                inst = s.activation(texp[:, :], tln[:, :], AF.Ln, bias=onec[:, 0:1])
                E.inc(inst, p + "sp2", 1)
                inst = s.activation(xlnf[:, :], xl[:, dc * NLOC:(dc + 1) * NLOC], AF.Copy)
                E.inc(inst, p + "xlc", 1)

            @E.block.vector
            def _(v, dc=dc, g=g):
                v.wait_ge(E.sem(p + "sp2"), g + 1)
                inst = v.scalar_tensor_tensor(dt_dc[:, :], tsp0[:, :], 0.0, texp[:, :],
                                              ALU.max, ALU.add)
                E.inc(inst, p + "dtd", 1)
                v.wait_ge(E.sem(p + "xlc"), g + 1)
                inst = v.tensor_tensor(dtx[:, :], dt_dc[:, :], xlnf[:, :], ALU.mult)
                E.inc(inst, p + "dtxd", 1)
                if pass_idx == 0:
                    inst = v.reduce_sum(ssum[:, dc:dc + 1], dt_dc[:, :], axis=AXX)
                    E.inc(inst, p + "ssv", 1)

            if pass_idx == 0 and dc == 0 and getattr(E, "_l3_debug", None) == "P0":
                dtd_d = E.dout("dtdd", [128, NLOC], F32)
                dti_d = E.dout("dtid", [128, NLOC], BF16)
                bcd = E.dout("bcd", [32, NLOC], F32)
                tspd = E.dout("tspd", [128, NLOC], F32)
                absd = E.dout("absd", [128, NLOC], F32)
                lnd = E.dout("lnd", [128, NLOC], F32)

                @E.block.gpsimd
                def _(gg):
                    gg.wait_ge(E.sem(p + "dtxd"), 1)
                    E.inc(gg.dma_start(out=dtd_d[:, :], in_=dt_dc[:, :]), p + "dbg2", 16)
                    E.inc(gg.dma_start(out=dti_d[:, :], in_=dti[:, :]), p + "dbg2", 16)
                    E.inc(gg.dma_start(out=bcd[:, :], in_=bc_sb[:, :]), p + "dbg2", 16)
                    E.inc(gg.dma_start(out=tspd[:, :], in_=tsp0[:, :]), p + "dbg2", 16)
                    E.inc(gg.dma_start(out=absd[:, :], in_=texp[:, :]), p + "dbg2", 16)
                    E.inc(gg.dma_start(out=lnd[:, :], in_=tln[:, :]), p + "dbg2", 16)
                    gg.wait_ge(E.sem(p + "dbg2"), 96)

            for n in range(nlim):
                gn = g * NST + n
                slot = gn % 2
                eng_v = True

                @E.block.sync
                def _(s, gn=gn, n=n, slot=slot):
                    if gn == 0:
                        s.wait_ge(E.sem(p + "bcw"), 16)
                    if gn >= 2:
                        s.wait_ge(E.sem(p + "dbd"), gn - 1)
                    E.inc(s.dma_start(out=bbc.ap().rearrange("pp (b c) -> pp b c", b=2)[:, slot, :],
                                      in_=bc_dram[n, None, :].broadcast_to([128, NLOC])),
                          p + "bbl%d" % slot, 16)
                    if want_y:
                        nid = gn - KT * NST
                        if nid >= 2:
                            s.wait_ge(E.sem(p + "ym"), nid - 1)
                        E.inc(s.dma_start(out=cbc.ap().rearrange("pp (b c) -> pp b c", b=2)[:, slot, :],
                                          in_=bc_dram[16 + n, None, :].broadcast_to([128, NLOC])),
                              p + "cbl%d" % slot, 16)

                @E.block.scalar
                def _(s, dc=dc, n=n, g=g, gn=gn, slot=slot):
                    s.wait_ge(E.sem(p + "dtd"), g + 1)
                    s.wait_ge(E.sem(p + "asv"), 2)
                    if gn >= 2:
                        s.wait_ge(E.sem(p + "scv"), gn - 1)
                    inst = s.activation(dasb[:, slot * NLOC:(slot + 1) * NLOC], dt_dc[:, :],
                                        AF.Exp, scale=asb[:, dc * NST + n: dc * NST + n + 1])
                    E.inc(inst, p + "dan", 1)

                hcol = hing[:, (g % KT) * NST + n: (g % KT) * NST + n + 1]

                @E.block.gpsimd
                def _(gg, gn=gn, slot=slot, n=n, hcol=hcol):
                    gg.wait_ge(E.sem(p + "dtxd"), g + 1)
                    gg.wait_ge(E.sem(p + "bbl%d" % slot), 16 * (gn // 2 + 1))
                    if gn >= 2:
                        gg.wait_ge(E.sem(p + "scv"), gn - 1)
                    inst = gg.tensor_tensor(dbsb[:, slot * NLOC:(slot + 1) * NLOC],
                                            dtx[:, :], bbc[:, slot * NLOC:(slot + 1) * NLOC],
                                            ALU.mult)
                    E.inc(inst, p + "dbd", 1)

                init = zcol[:, 0:1] if pass_idx == 0 else hcol

                @E.block.vector
                def _(v, gn=gn, slot=slot, init=init, dc=dc, n=n):
                    v.wait_ge(E.sem(p + "dan"), gn + 1)
                    v.wait_ge(E.sem(p + "dbd"), gn + 1)
                    if pass_idx == 1 and gn == KT * NST:
                        v.wait_ge(E.sem(p + "hiv"), E.cnt[p + "hiv"])
                    if want_y and gn >= 2 + KT * NST:
                        v.wait_ge(E.sem(p + "ym"), gn - KT * NST - 1)
                    elif pass_idx == 0 and gn >= 2:
                        v.wait_ge(E.sem(p + "hlg"), gn - 1)
                    inst = v.tensor_tensor_scan(hsb_v.ap().rearrange("pp (b c) -> pp b c", b=2)[:, gn % 2, :],
                                                dasb[:, slot * NLOC:(slot + 1) * NLOC],
                                                dbsb[:, slot * NLOC:(slot + 1) * NLOC],
                                                init, ALU.mult, ALU.add)
                    E.inc(inst, p + "scv", 1)

                if pass_idx == 0:
                    @E.block.gpsimd
                    def _(gg, gn=gn, dc=dc, n=n):
                        gg.wait_ge(E.sem(p + "scv"), gn + 1)
                        inst = gg.tensor_copy(hloc[:, dc * NST + n: dc * NST + n + 1],
                                              hsb_v[:, (gn % 2) * NLOC + NLOC - 1:(gn % 2) * NLOC + NLOC])
                        E.inc(inst, p + "hlg", 1)

                if want_y:
                    @E.block.gpsimd
                    def _(gg, gn=gn, n=n, slot=slot, dc=dc):
                        nid = gn - KT * NST
                        gg.wait_ge(E.sem(p + "scv"), gn + 1)
                        gg.wait_ge(E.sem(p + "cbl%d" % slot), 16 * (gn // 2 - KT * NST // 2 + 1))
                        inst = gg.tensor_tensor(yprod[:, 0:NLOC],
                                                hsb_v.ap().rearrange("pp (b c) -> pp b c", b=2)[:, gn % 2, :],
                                                cbc[:, slot * NLOC:(slot + 1) * NLOC],
                                                ALU.mult)
                        E.inc(inst, p + "ym", 1)

                    @E.block.vector
                    def _(v, gn=gn, n=n, dc=dc):
                        nid = gn - KT * NST
                        v.wait_ge(E.sem(p + "ym"), nid + 1)
                        if n == 0:
                            if dc > 0:
                                v.wait_ge(E.sem(p + "std"), dc)
                            inst = v.tensor_copy(yacc[:, :], yprod[:, 0:NLOC])
                        else:
                            inst = v.tensor_tensor(yacc[:, :], yacc[:, :],
                                                   yprod[:, 0:NLOC], ALU.add)
                        E.inc(inst, p + "ya", 1)

            if want_y:
                @E.block.sync
                def _(s, dc=dc):
                    if dc < 2:
                        E.inc(s.dma_start(out=xres[:, (dc % 2) * NLOC:(dc % 2 + 1) * NLOC],
                                          in_=xin_t[dc, :, :]), p + "rl%d" % (dc % 2), 16)

                @E.block.vector
                def _(v, dc=dc):
                    v.wait_ge(E.sem(p + "ya"), (dc + 1) * NST)
                    v.wait_ge(E.sem(p + "rl%d" % (dc % 2)), 16 * (dc // 2 + 1))
                    if dc > 0:
                        v.wait_ge(E.sem(p + "st0"), 16 * dc)
                    inst = v.scalar_tensor_tensor(tsp0[:, :], xlnf[:, :], dsb[:, dc:dc + 1],
                                                  yacc[:, :], ALU.mult, ALU.add)
                    inst = v.tensor_tensor(outsb[:, :], tsp0[:, :],
                                           xres[:, (dc % 2) * NLOC:(dc % 2 + 1) * NLOC], ALU.add)
                    E.inc(inst, p + "std", 1)

                @E.block.sync
                def _(s, dc=dc):
                    s.wait_ge(E.sem(p + "std"), dc + 1)
                    E.inc(s.dma_start(out=xout_t[dc, :, :], in_=outsb[:, :]), p + "st0", 16)
                    if dc + 2 < KT:
                        E.inc(s.dma_start(out=xres[:, (dc % 2) * NLOC:(dc % 2 + 1) * NLOC],
                                          in_=xin_t[dc + 2, :, :]), p + "rl%d" % (dc % 2), 16)

    E.sem(p + "st0")
    emit_pass(0)

    if getattr(E, "_l3_debug", None) == "S0":
        dmp = {}
        for nm, buf, w in (("d_da", dasb, 2 * NLOC), ("d_db", dbsb, 2 * NLOC),
                           ("d_bb", bbc, 2 * NLOC), ("d_hv", hsb_v, 2 * NLOC),
                           ("d_dt", dt_dc, NLOC), ("d_as", asb, KT * NST),
                           ("d_dx", dtx, NLOC), ("d_hl", hloc, 512),
                           ("d_t0", tsp0, NLOC), ("d_ti", dti, NLOC),
                           ("d_te", texp, NLOC), ("d_tl", tln, NLOC)):
            dmp[nm] = (E.dout(nm, [128, w], F32), buf)

        @E.block.gpsimd
        def _(gg):
            gg.wait_ge(E.sem(p + "hlg"), E.cnt[p + "hlg"])
            for nm, (od, buf) in dmp.items():
                E.inc(gg.dma_start(out=od[:, :], in_=buf[:, :]), p + "dbg3", 16)
            gg.wait_ge(E.sem(p + "dbg3"), 16 * len(dmp))
        return

    if getattr(E, "_l3_debug", None) == "P0":
        hld = E.dout("hld", [128, 512], F32)
        ssd = E.dout("ssd", [128, 32], F32)

        @E.block.sync
        def _(s):
            s.wait_ge(E.sem(p + "hlg"), E.cnt[p + "hlg"])
            s.wait_ge(E.sem(p + "ssv"), E.cnt[p + "ssv"])
            E.inc(s.dma_start(out=hld[:, :], in_=hloc[:, :]), p + "dbg", 16)
            E.inc(s.dma_start(out=ssd[:, :], in_=ssum[:, :]), p + "dbg", 16)
            s.wait_ge(E.sem(p + "dbg"), 32)
        return

    @E.block.sync
    def _(s):
        s.wait_ge(E.sem(p + "hlv"), E.cnt[p + "hlv"])
        s.wait_ge(E.sem(p + "hlg"), E.cnt[p + "hlg"])
        s.wait_ge(E.sem(p + "ssv"), 32)
        E.inc(s.dma_start(out=cc_in.ap()[:, 0:512], in_=hloc[:, :]), p + "ccd", 16)
        E.inc(s.dma_start(out=cc_in.ap()[:, 512:544], in_=ssum[:, :]), p + "ccd", 16)

    @E.block.gpsimd
    def _(g):
        g.wait_ge(E.sem(p + "ccd"), 32)
        E.inc(g.collective_compute("AllGather", ALU.bypass, replica_groups=GROUPS4,
                                   ins=[cc_in.ap().opt()], outs=[cc_out.ap().opt()]),
              p + "ccs", 1)
        g.wait_ge(E.sem(p + "ccs"), 1)
        for c in range(3):
            if c > 0:
                g.wait_ge(E.sem(p + "hiv"), 2 * c)
            E.inc(g.dma_start(out=hgb[:, :], in_=cc_out.ap()[c, :, :]), p + "hgl", 16)

    @E.block.vector
    def _(v):
        v.memset(hin[:, :], 0.0)

    for c in range(3):
        @E.block.vector
        def _(v, c=c):
            v.wait_ge(E.sem(p + "hgl"), 16 * (c + 1))
            if c > 0:
                v.wait_ge(E.sem(p + "epx"), c)
            inst = v.tensor_scalar(stil[:, :], hgb[:, 512:544], flg3[:, c:c + 1], None, ALU.mult)
            E.inc(inst, p + "stl", 1)
            inst = v.tensor_scalar(hlt[:, :], hgb[:, 0:512], flg3[:, c:c + 1], None, ALU.mult)
            E.inc(inst, p + "stl", 1)

        @E.block.scalar
        def _(s, c=c):
            s.wait_ge(E.sem(p + "stl"), 2 * (c + 1))
            for dc in range(KT):
                inst = s.activation(ep[:, dc * NST:(dc + 1) * NST], asb[:, dc * NST:(dc + 1) * NST],
                                    AF.Exp, scale=stil[:, dc:dc + 1])
            E.inc(inst, p + "epx", 1)

        @E.block.vector
        def _(v, c=c):
            v.wait_ge(E.sem(p + "epx"), c + 1)
            v.tensor_tensor(hin[:, :], hin[:, :], ep[:, :], ALU.mult)
            inst = v.tensor_tensor(hin[:, :], hin[:, :], hlt[:, :], ALU.add)
            E.inc(inst, p + "hiv", 2)

        if c == 2:
            @E.block.gpsimd
            def _(gg):
                gg.wait_ge(E.sem(p + "hiv"), E.cnt[p + "hiv"])
                E.inc(gg.tensor_copy(hing[:, :], hin[:, :]), p + "hiv", 1)

    if getattr(E, "_l3_debug", None) == "CMB":
        hid = E.dout("hid", [128, 512], F32)

        @E.block.sync
        def _(s):
            s.wait_ge(E.sem(p + "hiv"), E.cnt[p + "hiv"])
            E.inc(s.dma_start(out=hid[:, :], in_=hin[:, :]), p + "dbg", 16)
            s.wait_ge(E.sem(p + "dbg"), 16)
        return

    emit_pass(1)

    if getattr(E, "_l3_debug", None) == "Y0":
        dmp = {}
        for nm, buf, w in (("y_ya", yacc, NLOC), ("y_hv", hsb_v, 2 * NLOC),
                           ("y_da", dasb, 2 * NLOC), ("y_db", dbsb, 2 * NLOC),
                           ("y_cb", cbc, 2 * NLOC), ("y_hi", hin, 512),
                           ("y_dt", dt_dc, NLOC)):
            dmp[nm] = (E.dout(nm, [128, w], F32), buf)

        @E.block.gpsimd
        def _(gg):
            gg.wait_ge(E.sem(p + "ya"), E.cnt[p + "ya"])
            for nm, (od, buf) in dmp.items():
                E.inc(gg.dma_start(out=od[:, :], in_=buf[:, :]), p + "dbg4", 16)
            gg.wait_ge(E.sem(p + "dbg4"), 16 * len(dmp))

    @E.block.sync
    def _(s):
        s.wait_ge(E.sem(p + "st0"), E.cnt[p + "st0"])


def build_l3attn(x_arrs, inp):
    E = Env()
    W = _l3attn_host(inp)
    with E.stack:
        xin = E.din("xin", x_arrs)
        Wd = {k: E.din("w_" + k, v) for k, v in W.items()}
        eye_d = E.din("eye_d", eye_groups())
        xout = E.dout("xout", [D, NLOC], F32)

        common = _common_alloc(E)
        xl = E.sbuf("xl", [128, KT * NLOC], BF16)
        sq = E.sbuf("sq", [128, 2 * NLOC], BF16)
        stat = E.sbuf("stat", [1, 4 * NLOC], F32)
        rstd_bc = E.sbuf("rstd_bc", [128, NLOC], BF16)
        mbc_sb = E.sbuf("mbc_sb", [128, NLOC], F32)
        mid = E.sbuf("mid", [128, TH], F32)
        rbf_row = E.sbuf("rbf_row", [1, NLOC], BF16)
        astack = ExitStack()

        with E.nc.Block() as block:
            E.block = block
            emit_preamble(E, common, eye_d)
            xl_ready = emit_ln(E, "ln_", xin, xl, sq, stat, rstd_bc, mbc_sb, mid, rbf_row,
                               common["ones_col"], common["ones_row"],
                               common["psA"], common["psB"], common["psBC"], common["eps_col"])
            emit_l3attn(E, "a3_", xin, xout, Wd, xl_ready, xl, common, astack)
            astack.close()
        return E


def run_l3attn(x, inp):
    E = build_l3attn(shard_x(x), inp)
    res = E.run()
    return unshard_x([res.results[c]["xout"] for c in range(NCORES)])



def _l0attn_host(inp):
    W = {
        "wq": wblocks_km(np.ascontiguousarray(np.asarray(inp["L0_attn_Wq"])), 128),
        "wk": wblocks_km(np.ascontiguousarray(np.asarray(inp["L0_attn_Wk"])), 128),
        "wv": wblocks_km(np.ascontiguousarray(np.asarray(inp["L0_attn_Wv"])), 128),
        "wo": wblocks_km(np.ascontiguousarray(np.asarray(inp["L0_attn_Wo"])), 512),
    }
    inv = 10000.0 ** (-2.0 * np.arange(64) / 128.0)
    coss, sins, swam, flgf, flg3 = [], [], [], [], []
    a128 = np.arange(128)[:, None]
    a64 = np.arange(64)[None, :]
    band = ((a128 - a64 >= 0) & (a128 - a64 <= 64)).astype(np.float32)
    bfirst = band * (a128 >= 64)
    mreg = np.concatenate([band, band], 1)
    mj0 = np.concatenate([bfirst, band], 1)
    for c in range(NCORES):
        q = c % 4
        pos = q * 1024 + np.arange(1024)
        fr = pos[None, :] * inv[:, None]
        emb = np.concatenate([fr, fr], 0)
        coss.append(np.cos(emb).astype(np.float32))
        sins.append(np.sin(emb).astype(np.float32))
        swam.append(np.concatenate([mj0 if q == 0 else mreg, mreg], 1))
        f = np.zeros((128, 4), np.float32)
        if q > 0:
            f[:, q - 1] = 1.0
        flgf.append(f)
        f3 = np.zeros((128, 3), np.float32)
        f3[:, :q] = 1.0
        flg3.append(f3)
    W["cos"] = coss
    W["sin"] = sins
    W["swam"] = swam
    W["flgf"] = flgf
    W["flg3"] = flg3
    W["tri"] = (np.arange(128)[:, None] <= np.arange(128)[None, :]).astype(np.float32)
    return W


def emit_l0attn(E, p, xin, xout, W, xl_ready, xl, common, astack):
    (psA, psB, psBC, mm_banks) = (common["psA"], common["psB"], common["psBC"], common["mm_banks"])
    eye128b = common["eye128b"]
    nc = E.nc
    GN = 16.0

    sk_pad = E.sbuf(p + "skp", [128, 8 * 1088], BF16, astack)
    v_pad = E.sbuf(p + "vp", [128, 8 * 1088], BF16, astack)
    ktlb = E.sbuf(p + "ktb", [128, NLOC], BF16, astack)
    khb = E.sbuf(p + "khb", [128, NLOC], BF16, astack)
    S_sl = E.sbuf(p + "Ssl", [128, 8 * 128], F32, astack)
    svb = E.sbuf(p + "svb", [128, 128], F32, astack)
    totb = E.sbuf(p + "tot", [128, 64], F32, astack)
    etotb = E.sbuf(p + "eto", [128, 64], F32, astack)
    trif = E.sbuf(p + "tri", [128, 128], F32, astack)
    swam_sb = E.sbuf(p + "swm", [128, 256], F32, astack)
    flgf = E.sbuf(p + "ff", [128, 4], F32, astack)
    flg3 = E.sbuf(p + "f3", [128, 3], F32, astack)
    onesf = E.sbuf(p + "onf", [128, 128], F32, astack)
    onec = E.sbuf(p + "onc", [128, 1], F32, astack)
    onescf = E.sbuf(p + "ocf", [128, 1], F32, astack)
    ktm = E.sbuf(p + "ktm", [128, 2 * 128], BF16, astack)
    kb15 = E.sbuf(p + "k15", [128, 2 * 128], BF16, astack)
    kb4 = E.sbuf(p + "kb4", [128, 2 * 256], BF16, astack)

    psT = E.stack.enter_context(nc.psum_tensor(p + "psT", [128, 1024], BF16))

    el_dram = E.dram(p + "eld", [8, 128, NLOC], BF16)
    ktl_dram = E.dram(p + "ktd", [8, 128, NLOC], BF16)
    khat_dram = E.dram(p + "khd", [8, 128, NLOC], BF16)
    qd_dram = E.dram(p + "qdd", [D, NLOC], BF16)
    sq_dram = E.dram(p + "sqd", [D, NLOC], BF16)
    o_dram = E.dram(p + "odr", [D, NLOC], BF16)
    cc_in = E.nc.dram_tensor(p + "ccin", [128, 2056], F32)
    cc_out = E.nc.dram_tensor(p + "ccout", [4, 128, 2056], F32)

    mm = p + "mm"; ev = p + "ev"
    for nm in ("dbgh", "kbl0", "kbl1", "kcl0", "kcl1",
               "wld0", "wld1", "pld", "gs1", "gs2", "gsg", "lsc", "totg", "sle",
               "eks", "rk1", "rkv", "rk2", "bcm", "ktv", "khv", "skg", "vpv",
               "kt1", "kt2", "kvm", "sch", "scg", "eld0", "eld1",
               "ccd", "ccs", "hgl", "mgv", "hcv", "cmb",
               "qst0", "qst1", "els0", "els1", "qev",
               "qtl0", "qtl1", "tpm", "tpv", "amv", "smm", "sxp", "msk",
               "csm", "rw1", "rwv", "rw2", "rbb", "env", "sbf",
               "ost0", "ost1", "ost2", "ost3", "old"):
        E.sem(nm if nm.startswith(p) else p + nm)
    ring = PsumRing(E, mm_banks, mm, ev)

    import contextlib
    stack_b = ExitStack()
    wqb = E.sbuf(p + "wqb", [128, 2 * KT * 128], BF16, stack_b)
    cosb = E.sbuf(p + "cos", [128, NLOC], F32, stack_b)
    sinb = E.sbuf(p + "sin", [128, NLOC], F32, stack_b)
    kraw = E.sbuf(p + "kr", [128, NLOC], F32, stack_b)
    m_ = E.sbuf(p + "m_", [128, NLOC], F32, stack_b)
    a_ = E.sbuf(p + "a_", [128, NLOC], F32, stack_b)
    l1_ = E.sbuf(p + "l1", [128, NLOC], F32, stack_b)
    Lb = E.sbuf(p + "Lb", [128, NLOC], F32, stack_b)
    ekb = E.sbuf(p + "ek", [128, NLOC], F32, stack_b)
    emb_ = E.sbuf(p + "em", [128, NLOC], F32, stack_b)
    gsb = E.sbuf(p + "gsb", [128, NLOC], F32, stack_b)
    eLb = E.sbuf(p + "eL", [128, NLOC], BF16, stack_b)
    rk = E.sbuf(p + "rk", [1, NLOC], F32, stack_b)
    rkg = E.sbuf(p + "rkg", [1, NLOC], F32, stack_b)
    rkr = rk
    t1b = eLb
    elqs = E.sbuf(p + "elq", [128, 2 * NLOC], BF16, stack_b)
    qtb = E.sbuf(p + "qtb", [128, 2 * NLOC], BF16, stack_b)
    sqb = E.sbuf(p + "sqb", [128, 2 * NLOC], BF16, stack_b)

    @E.block.sync
    def _(s):
        s.wait_ge(E.sem(xl_ready), E.cnt[xl_ready])
        E.inc(s.dma_start(out=cosb[:, :], in_=W["cos"][:, :]), p + "pld", 16)
        E.inc(s.dma_start(out=sinb[:, :], in_=W["sin"][:, :]), p + "pld", 16)
        E.inc(s.dma_start(out=trif[:, :], in_=W["tri"][:, :]), p + "pld", 16)
        E.inc(s.dma_start(out=swam_sb[:, :], in_=W["swam"][:, :]), p + "pld", 16)
        E.inc(s.dma_start(out=flgf[:, :], in_=W["flgf"][:, :]), p + "pld", 16)
        E.inc(s.dma_start(out=flg3[:, :], in_=W["flg3"][:, :]), p + "pld", 16)

    @E.block.vector
    def _(v):
        v.memset(onesf[:, :], 1.0)
        v.memset(onec[:, :], 1.0)
        v.memset(onescf[:, :], 1.0)
        v.memset(S_sl[:, :], 0.0)

    def _p1_load(i):
        @E.block.gpsimd
        def _(g):
            kind2, h2 = ("k", "v")[i % 2], i // 2
            src2 = W["wk"] if kind2 == "k" else W["wv"]
            if i == 0:
                g.wait_ge(E.sem(xl_ready), E.cnt[xl_ready])
            if i >= 2:
                g.wait_ge(E.sem(ev), (i - 1) * 2)
            E.inc(g.dma_start(out=wqb.ap().rearrange("pp (b c) -> pp b c", b=2)[:, i % 2, :],
                              in_=src2[h2, :, :].rearrange("(kt pp) c -> pp kt c", pp=128)),
                  p + "wld%d" % (i % 2), 16)

    _p1_load(0)
    _p1_load(1)

    for i in range(16):
        kind, h = ("k", "v")[i % 2], i // 2
        base_tile = 2 * i

        if i + 2 < 16:
            _p1_load(i + 2)

        @E.block.tensor
        def _(t, i=i):
            t.wait_ge(E.sem(p + "wld%d" % (i % 2)), 16 * (i // 2 + 1))
            if i == 0:
                t.wait_ge(E.sem(xl_ready), E.cnt[xl_ready])
            for th in range(NTH):
                ps = ring.acquire(t)
                woff = (i % 2) * KT * 128
                for kt in range(KT):
                    inst = t.matmul(ps[:, :], wqb[:, woff + kt * 128: woff + kt * 128 + 128],
                                    xl[:, kt * NLOC + th * TH: kt * NLOC + th * TH + TH],
                                    start=(kt == 0), stop=(kt == KT - 1))
                ring.finish_tile(inst)

        if kind == "v":
            @E.block.vector
            def _(v, h=h, base_tile=base_tile):
                for th in range(NTH):
                    v.wait_ge(E.sem(mm), base_tile + th + 1)
                    ps = ring.banks[(base_tile + th) % 4]
                    inst = v.tensor_copy(v_pad[:, h * 1088 + 64 + th * TH: h * 1088 + 64 + (th + 1) * TH],
                                         ps[:, :])
                    E.inc(inst, ev, 1)
            continue

        @E.block.vector
        def _(v, h=h, base_tile=base_tile):
            if h == 0:
                v.wait_ge(E.sem(p + "pld"), 96)
            for th in range(NTH):
                v.wait_ge(E.sem(mm), base_tile + th + 1)
                ps = ring.banks[(base_tile + th) % 4]
                inst = v.tensor_copy(kraw[:, th * TH:(th + 1) * TH], ps[:, :])
                E.inc(inst, ev, 1)
            v.tensor_scalar(m_[:, :], kraw[:, :], 0.0, None, ALU.min)
            inst = v.scalar_tensor_tensor(a_[:, :], m_[:, :], -2.0, kraw[:, :],
                                          ALU.mult, ALU.add)
            E.inc(inst, p + "gs1", 1)

        @E.block.scalar
        def _(s, h=h):
            s.wait_ge(E.sem(p + "gs1"), E.cnt[p + "gs1"])
            s.activation(l1_[:, :], a_[:, :], AF.Exp, scale=-1.0)
            s.activation(a_[:, :], l1_[:, :], AF.Ln, bias=onec[:, 0:1])
            inst = s.activation(ekb[:, :], kraw[:, :], AF.Exp)
            E.inc(inst, p + "gs2", 1)

        @E.block.gpsimd
        def _(g, h=h):
            g.wait_ge(E.sem(p + "gs2"), E.cnt[p + "gs2"])
            inst = g.tensor_tensor(gsb[:, :], m_[:, :], a_[:, :], ALU.subtract)
            E.inc(inst, p + "gsg", 1)

        @E.block.vector
        def _(v, h=h):
            v.wait_ge(E.sem(p + "gsg"), E.cnt[p + "gsg"])
            for j in range(8):
                inst = v.tensor_tensor_scan(Lb[:, j * 128:(j + 1) * 128], onesf[:, :],
                                            gsb[:, j * 128:(j + 1) * 128], 0.0,
                                            ALU.mult, ALU.add)
            E.inc(inst, p + "lsc", 1)

        @E.block.gpsimd
        def _(g, h=h):
            g.wait_ge(E.sem(p + "lsc"), E.cnt[p + "lsc"])
            for j in range(8):
                inst = g.tensor_copy(totb[:, h * 8 + j: h * 8 + j + 1],
                                     Lb[:, j * 128 + 127: j * 128 + 128])
            E.inc(inst, p + "totg", 1)

        @E.block.scalar
        def _(s, h=h):
            s.wait_ge(E.sem(p + "totg"), E.cnt[p + "totg"])
            inst = s.activation(eLb[:, :], Lb[:, :], AF.Exp, scale=1.0 / GN)
            E.inc(inst, p + "sle", 1)
            inst = s.activation(emb_[:, :], Lb[:, :], AF.Exp, scale=-1.0 / GN)
            E.inc(inst, p + "eks", 1)

        @E.block.tensor
        def _(t, h=h):
            t.wait_ge(E.sem(p + "eks"), E.cnt[p + "eks"])
            t.wait_ge(E.sem(p + "khv"), E.cnt[p + "khv"])
            E.inc(t.matmul(psBC[0:1, :], onescf[:, 0:1], ekb[:, 0:TH], start=True, stop=True),
                  p + "rk1", 1)
            E.inc(t.matmul(psB[0:1, :], onescf[:, 0:1], ekb[:, TH:NLOC], start=True, stop=True),
                  p + "rk1", 1)

        @E.block.scalar
        def _(s, h=h):
            s.wait_ge(E.sem(p + "rk1"), E.cnt[p + "rk1"])
            s.activation(rk[0:1, 0:TH], psBC[0:1, :], AF.Ln)
            inst = s.activation(rk[0:1, TH:NLOC], psB[0:1, :], AF.Ln)
            E.inc(inst, p + "rkv", 1)

        @E.block.gpsimd
        def _(g, h=h):
            g.wait_ge(E.sem(p + "rkv"), E.cnt[p + "rkv"])
            E.inc(g.tensor_copy(rkg[0:1, :], rk[0:1, :]), p + "rkv", 1)

        @E.block.scalar
        def _(s, h=h):
            s.wait_ge(E.sem(p + "rkv"), E.cnt[p + "rkv"])
            inst = s.activation(rkr[0:1, :], rkg[0:1, :], AF.Exp, scale=-1.0)
            E.inc(inst, p + "rk2", 1)

        @E.block.tensor
        def _(t, h=h):
            t.wait_ge(E.sem(p + "rk2"), E.cnt[p + "rk2"])
            for th in range(NTH):
                ps = psBC if th == 0 else psB
                inst = t.matmul(ps[0:128, :], common["ones_row"][0:1, :],
                                rkr[0:1, th * TH:(th + 1) * TH], start=True, stop=True)
            E.inc(inst, p + "bcm", 1)

        @E.block.vector
        def _(v, h=h):
            v.wait_ge(E.sem(p + "bcm"), E.cnt[p + "bcm"])
            if h > 0:
                v.wait_ge(E.sem(p + "eld%d" % ((h - 1) % 2)), 16 * 3 * ((h - 1) // 2 + 1))
            for th in range(NTH):
                ps = psBC if th == 0 else psB
                sl = slice(th * TH, (th + 1) * TH)
                v.tensor_tensor(l1_[:, sl], emb_[:, sl], ekb[:, sl], ALU.mult)
                inst = v.tensor_tensor(a_[:, sl], l1_[:, sl], ps[:, :], ALU.mult)
                E.inc(inst, p + "ktv", 1)
            inst = v.tensor_copy(ktlb[:, :], a_[:, :])
            E.inc(inst, p + "ktv", 1)

        @E.block.scalar
        def _(s, h=h):
            s.wait_ge(E.sem(p + "totg"), E.cnt[p + "totg"])
            inst = s.activation(etotb[:, h * 8:(h + 1) * 8], totb[:, h * 8:(h + 1) * 8],
                                AF.Exp, scale=1.0 / GN)
            E.inc(inst, p + "sle", 1)

        @E.block.vector
        def _(v, h=h):
            v.wait_ge(E.sem(p + "ktv"), E.cnt[p + "ktv"])
            v.wait_ge(E.sem(p + "sle"), E.cnt[p + "sle"])
            for j in range(8):
                inst = v.tensor_scalar(khb[:, j * 128:(j + 1) * 128],
                                       a_[:, j * 128:(j + 1) * 128],
                                       etotb[:, h * 8 + j: h * 8 + j + 1], None, ALU.mult)
            E.inc(inst, p + "khv", 1)

        @E.block.vector
        def _(v, h=h):
            v.wait_ge(E.sem(p + "gs2"), E.cnt[p + "gs2"])
            v.wait_ge(E.sem(p + "khv"), E.cnt[p + "khv"])
            v.tensor_scalar(m_[0:64, :], kraw[64:128, :], -1.0, None, ALU.mult)
            v.tensor_copy(m_[64:128, :], kraw[0:64, :])
            v.tensor_tensor(l1_[:, :], kraw[:, :], cosb[:, :], ALU.mult)
            inst = v.tensor_tensor(m_[:, :], m_[:, :], sinb[:, :], ALU.mult)
            E.inc(inst, p + "gs1", 1)

        @E.block.gpsimd
        def _(g, h=h):
            g.wait_ge(E.sem(p + "gs1"), E.cnt[p + "gs1"])
            inst = g.tensor_tensor(sk_pad[:, h * 1088 + 64: h * 1088 + 1088],
                                   l1_[:, :], m_[:, :], ALU.add)
            E.inc(inst, p + "skg", 1)

        @E.block.sync
        def _(s, h=h):
            s.wait_ge(E.sem(p + "sle"), E.cnt[p + "sle"])
            E.inc(s.dma_start(out=el_dram[h, :, :], in_=eLb[:, :]), p + "eld%d" % (h % 2), 16)
            s.wait_ge(E.sem(p + "khv"), E.cnt[p + "khv"])
            E.inc(s.dma_start(out=ktl_dram[h, :, :], in_=ktlb[:, :]), p + "eld%d" % (h % 2), 16)
            E.inc(s.dma_start(out=khat_dram[h, :, :], in_=khb[:, :]), p + "eld%d" % (h % 2), 16)

        @E.block.scalar
        def _(s, h=h):
            if h < 7:
                s.wait_ge(E.sem(p + "eld%d" % (h % 2)), 16 * 3 * (h // 2 + 1))

    def _dbg_halt(waits):
        dd = E.dout("dbg_out", [128, 8], F32)

        @E.block.gpsimd
        def _(g):
            for snm, val in waits:
                g.wait_ge(E.sem(snm), val)
            E.inc(g.dma_start(out=dd[:, :], in_=totb[:, 0:8]), p + "dbgh", 16)
            g.wait_ge(E.sem(p + "dbgh"), 16)

    if getattr(E, "_l0_debug", None) == "P1":
        _dbg_halt([(p + "eld0", E.cnt[p + "eld0"]), (p + "eld1", E.cnt[p + "eld1"]),
                   (p + "skg", E.cnt[p + "skg"]), (ev, 32)])
        stack_b.close(); astack.close(); return

    t15 = ring.tile
    for j in range(8):
        for h in range(8):
            idx = j * 8 + h

            @E.block.sync
            def _(s, j=j, h=h, idx=idx):
                if idx == 0:
                    s.wait_ge(E.sem(p + "eld0"), E.cnt[p + "eld0"])
                    s.wait_ge(E.sem(p + "eld1"), E.cnt[p + "eld1"])
                if idx >= 2:
                    s.wait_ge(E.sem(p + "kt1"), idx - 1)
                E.inc(s.dma_start(out=kb15.ap().rearrange("pp (b c) -> pp b c", b=2)[:, idx % 2, :],
                                  in_=khat_dram[h, :, j * 128:(j + 1) * 128]),
                      p + "kbl%d" % (idx % 2), 16)

            @E.block.tensor
            def _(t, j=j, h=h, idx=idx):
                if idx == 0:
                    t.wait_ge(E.sem(p + "skg"), E.cnt[p + "skg"])
                    t.wait_ge(E.sem(ev), 32)
                    t.wait_ge(E.sem("preld"), E.cnt["preld"])
                if idx >= 2:
                    t.wait_ge(E.sem(p + "kt2"), 2 * (idx - 1))
                t.wait_ge(E.sem(p + "kbl%d" % (idx % 2)), 16 * (idx // 2 + 1))
                t.transpose(psT[0:128, (idx % 2) * 256:(idx % 2) * 256 + 128],
                            kb15.ap().rearrange("pp (b c) -> pp b c", b=2)[:, idx % 2, :],
                            eye128b[0:128, 0:128])
                inst = t.transpose(psT[0:128, (idx % 2) * 256 + 128:(idx % 2) * 256 + 256],
                                   v_pad[:, h * 1088 + 64 + j * 128: h * 1088 + 64 + (j + 1) * 128],
                                   eye128b[0:128, 0:128])
                E.inc(inst, p + "kt1", 1)

            @E.block.vector
            def _(v, idx=idx):
                v.wait_ge(E.sem(p + "kt1"), idx + 1)
                if idx > 0:
                    v.wait_ge(E.sem(mm), t15 + idx)
                v.tensor_copy(ktm[:, 0:128], psT[0:128, (idx % 2) * 256:(idx % 2) * 256 + 128])
                inst = v.tensor_copy(ktm[:, 128:256],
                                     psT[0:128, (idx % 2) * 256 + 128:(idx % 2) * 256 + 256])
                E.inc(inst, p + "kt2", 2)

            @E.block.tensor
            def _(t, idx=idx):
                t.wait_ge(E.sem(p + "kt2"), 2 * (idx + 1))
                ps = ring.acquire(t)
                inst = t.matmul(ps[:, 0:128], ktm[:, 0:128], ktm[:, 128:256],
                                start=True, stop=True)
                ring.finish_tile(inst)

            @E.block.vector
            def _(v, j=j, h=h, idx=idx):
                v.wait_ge(E.sem(mm), t15 + idx + 1)
                if idx > 0:
                    v.wait_ge(E.sem(p + "scg"), idx)
                ps = ring.banks[(t15 + idx) % 4]
                inst = v.scalar_tensor_tensor(svb[:, :], S_sl[:, h * 128:(h + 1) * 128],
                                              etotb[:, h * 8 + j: h * 8 + j + 1],
                                              ps[:, 0:128], ALU.mult, ALU.add)
                E.inc(inst, ev, 1)

            @E.block.gpsimd
            def _(g, h=h, idx=idx):
                g.wait_ge(E.sem(ev), 33 + idx)
                inst = g.tensor_copy(S_sl[:, h * 128:(h + 1) * 128], svb[:, :])
                E.inc(inst, p + "scg", 1)

    if getattr(E, "_l0_debug", None) == "P15":
        _dbg_halt([(p + "scg", 64)])
        stack_b.close(); astack.close(); return

    @E.block.vector
    def _(v):
        v.wait_ge(E.sem(p + "scg"), 64)
        for h in range(8):
            inst = v.reduce_sum(svb[:, h:h + 1], totb[:, h * 8:(h + 1) * 8], axis=AXX)
        E.inc(inst, p + "cmb", 1)

    @E.block.sync
    def _(s):
        s.wait_ge(E.sem(p + "cmb"), 1)
        for h in range(8):
            E.inc(s.dma_start(out=cc_in.ap()[:, h * 129: h * 129 + 128],
                              in_=S_sl[:, h * 128:(h + 1) * 128]), p + "ccd", 16)
            E.inc(s.dma_start(out=cc_in.ap()[:, h * 129 + 128: h * 129 + 129],
                              in_=svb[:, h:h + 1], allow_slow_non_contiguous=True), p + "ccd", 16)

    @E.block.gpsimd
    def _(g):
        g.wait_ge(E.sem(p + "vpv"), E.cnt[p + "vpv"])
        g.wait_ge(E.sem(p + "skg"), E.cnt[p + "skg"])
        for h in range(8):
            E.inc(g.dma_start(out=cc_in.ap()[:, 1032 + h * 64: 1032 + (h + 1) * 64],
                              in_=sk_pad[:, h * 1088 + 1024: h * 1088 + 1088]), p + "ccd", 16)
            E.inc(g.dma_start(out=cc_in.ap()[:, 1544 + h * 64: 1544 + (h + 1) * 64],
                              in_=v_pad[:, h * 1088 + 1024: h * 1088 + 1088]), p + "ccd", 16)

    @E.block.gpsimd
    def _(g):
        g.wait_ge(E.sem(p + "ccd"), E.cnt[p + "ccd"])
        E.inc(g.collective_compute("AllGather", ALU.bypass, replica_groups=GROUPS4,
                                   ins=[cc_in.ap().opt()], outs=[cc_out.ap().opt()]),
              p + "ccs", 1)

    if getattr(E, "_l0_debug", None) == "P3a":
        _dbg_halt([(p + "ccs", 1)])
        stack_b.close(); astack.close(); return

    t2 = ring.tile

    def _p2_load(qh):
        @E.block.gpsimd
        def _(g):
            need = t2 + 2 * (qh - 2) + 2 if qh >= 2 else 32
            g.wait_ge(E.sem(ev), need)
            E.inc(g.dma_start(out=wqb.ap().rearrange("pp (b c) -> pp b c", b=2)[:, qh % 2, :],
                              in_=W["wq"][qh, :, :].rearrange("(kt pp) c -> pp kt c", pp=128)),
                  p + "wld%d" % (qh % 2), 16)

    _p2_load(0)
    _p2_load(1)

    for qh in range(32):
        h = qh // 4
        base_tile = t2 + 2 * qh

        if qh + 2 < 32:
            _p2_load(qh + 2)

        @E.block.tensor
        def _(t, qh=qh):
            t.wait_ge(E.sem(p + "wld%d" % (qh % 2)), 16 * (8 + qh // 2 + 1))
            for th in range(NTH):
                ps = ring.acquire(t)
                woff = (qh % 2) * KT * 128
                for kt in range(KT):
                    inst = t.matmul(ps[:, :], wqb[:, woff + kt * 128: woff + kt * 128 + 128],
                                    xl[:, kt * NLOC + th * TH: kt * NLOC + th * TH + TH],
                                    start=(kt == 0), stop=(kt == KT - 1))
                ring.finish_tile(inst)

        @E.block.sync
        def _(s, qh=qh, h=h):
            if qh >= 2:
                s.wait_ge(E.sem(p + "qev"), 3 * (qh - 1))
            E.inc(s.dma_start(out=elqs.ap().rearrange("pp (b c) -> pp b c", b=2)[:, qh % 2, :],
                              in_=el_dram[h, :, :]), p + "els%d" % (qh % 2), 16)

        @E.block.vector
        def _(v, qh=qh, base_tile=base_tile):
            for th in range(NTH):
                v.wait_ge(E.sem(mm), base_tile + th + 1)
                ps = ring.banks[(base_tile + th) % 4]
                inst = v.tensor_copy(kraw[:, th * TH:(th + 1) * TH], ps[:, :])
                E.inc(inst, ev, 1)

        @E.block.scalar
        def _(s, qh=qh):
            s.wait_ge(E.sem(ev), E.cnt[ev])
            inst = s.activation(ekb[:, :], kraw[:, :], AF.Exp)
            E.inc(inst, p + "eks", 1)

        @E.block.tensor
        def _(t, qh=qh):
            t.wait_ge(E.sem(p + "eks"), E.cnt[p + "eks"])
            t.wait_ge(E.sem(p + "qev"), E.cnt[p + "qev"])
            E.inc(t.matmul(psBC[0:1, :], onescf[:, 0:1], ekb[:, 0:TH], start=True, stop=True),
                  p + "rk1", 1)
            E.inc(t.matmul(psB[0:1, :], onescf[:, 0:1], ekb[:, TH:NLOC], start=True, stop=True),
                  p + "rk1", 1)

        @E.block.scalar
        def _(s, qh=qh):
            s.wait_ge(E.sem(p + "rk1"), E.cnt[p + "rk1"])
            s.activation(rk[0:1, 0:TH], psBC[0:1, :], AF.Ln)
            inst = s.activation(rk[0:1, TH:NLOC], psB[0:1, :], AF.Ln)
            E.inc(inst, p + "rkv", 1)

        @E.block.gpsimd
        def _(g, qh=qh):
            g.wait_ge(E.sem(p + "rkv"), E.cnt[p + "rkv"])
            E.inc(g.tensor_copy(rkg[0:1, :], rk[0:1, :]), p + "rkv", 1)

        @E.block.scalar
        def _(s, qh=qh):
            s.wait_ge(E.sem(p + "rkv"), E.cnt[p + "rkv"])
            inst = s.activation(rkr[0:1, :], rkg[0:1, :], AF.Exp, scale=-1.0)
            E.inc(inst, p + "rk2", 1)

        @E.block.tensor
        def _(t, qh=qh):
            t.wait_ge(E.sem(p + "rk2"), E.cnt[p + "rk2"])
            for th in range(NTH):
                ps = psBC if th == 0 else psB
                inst = t.matmul(ps[0:128, :], common["ones_row"][0:1, :],
                                rkr[0:1, th * TH:(th + 1) * TH], start=True, stop=True)
            E.inc(inst, p + "bcm", 1)

        @E.block.vector
        def _(v, qh=qh):
            v.wait_ge(E.sem(p + "bcm"), E.cnt[p + "bcm"])
            if qh == 0:
                v.wait_ge(E.sem(p + "eld0"), E.cnt[p + "eld0"])
                v.wait_ge(E.sem(p + "eld1"), E.cnt[p + "eld1"])
            v.wait_ge(E.sem(p + "els%d" % (qh % 2)), 16 * (qh // 2 + 1))
            for th in range(NTH):
                ps = psBC if th == 0 else psB
                sl = slice(th * TH, (th + 1) * TH)
                inst = v.tensor_tensor(t1b[:, sl], ekb[:, sl], ps[:, :], ALU.mult)
            qsl = elqs.ap().rearrange("pp (b c) -> pp b c", b=2)[:, qh % 2, :]
            inst = v.tensor_tensor(qtb.ap().rearrange("pp (b c) -> pp b c", b=2)[:, qh % 2, :],
                                   t1b[:, :], qsl, ALU.mult)
            E.inc(inst, p + "qev", 1)
            v.tensor_scalar(m_[0:64, :], kraw[64:128, :], -1.0, None, ALU.mult)
            v.tensor_copy(m_[64:128, :], kraw[0:64, :])
            v.tensor_tensor(l1_[:, :], kraw[:, :], cosb[:, :], ALU.mult)
            inst = v.tensor_tensor(m_[:, :], m_[:, :], sinb[:, :], ALU.mult)
            E.inc(inst, p + "qev", 1)

        @E.block.gpsimd
        def _(g, qh=qh):
            g.wait_ge(E.sem(p + "qev"), E.cnt[p + "qev"])
            inst = g.tensor_tensor(sqb.ap().rearrange("pp (b c) -> pp b c", b=2)[:, qh % 2, :],
                                   l1_[:, :], m_[:, :], ALU.add)
            E.inc(inst, p + "qev", 1)

        @E.block.sync
        def _(s, qh=qh):
            s.wait_ge(E.sem(p + "qev"), E.cnt[p + "qev"])
            E.inc(s.dma_start(out=qd_dram.rearrange("(qh pp) n -> qh pp n", pp=128)[qh, :, :],
                              in_=qtb.ap().rearrange("pp (b c) -> pp b c", b=2)[:, qh % 2, :]),
                  p + "qst%d" % (qh % 2), 16)
            E.inc(s.dma_start(out=sq_dram.rearrange("(qh pp) n -> qh pp n", pp=128)[qh, :, :],
                              in_=sqb.ap().rearrange("pp (b c) -> pp b c", b=2)[:, qh % 2, :]),
                  p + "qst%d" % (qh % 2), 16)

        @E.block.vector
        def _(v, qh=qh):
            v.wait_ge(E.sem(p + "qst%d" % (qh % 2)), 32 * (qh // 2 + 1))

    if getattr(E, "_l0_debug", None) == "P2":
        _dbg_halt([(p + "qst0", E.cnt[p + "qst0"]), (p + "qst1", E.cnt[p + "qst1"]),
                   (p + "ccs", 1)])
        stack_b.close(); astack.close(); return

    stack_b.close()
    stack_c = ExitStack()
    acc = E.sbuf(p + "acc", [128, 1024], F32, stack_c)
    tmg = E.sbuf(p + "tmg", [128, 1024], F32, stack_c)
    hgb2 = E.sbuf(p + "hgb", [128, 1032], F32, stack_c)
    slt = E.sbuf(p + "slt", [128, 128], F32, stack_c)
    t8 = E.sbuf(p + "t8", [128, 8], F32, stack_c)
    et8 = E.sbuf(p + "et8", [128, 8], F32, stack_c)
    qt_sb = E.sbuf(p + "qts", [128, 2 * KT * 128], BF16, stack_c)
    sqt_sb = E.sbuf(p + "sqs", [128, 2 * KT * 128], BF16, stack_c)
    vtm = E.sbuf(p + "vtm", [128, 2 * 384], BF16, stack_c)
    s_bf = E.sbuf(p + "sbf", [128, 128], BF16, stack_c)
    Amb = E.sbuf(p + "Amb", [128, 512], BF16, stack_c)
    esw = E.sbuf(p + "esw", [128, 512], F32, stack_c)
    e_m = E.sbuf(p + "e_m", [128, 512], F32, stack_c)
    e_n = E.sbuf(p + "e_n", [128, 512], BF16, stack_c)
    rw = E.sbuf(p + "rw", [1, 512], F32, stack_c)
    rwg = E.sbuf(p + "rwg", [1, 512], F32, stack_c)
    rwr = E.sbuf(p + "rwr", [1, 512], F32, stack_c)
    tchn = E.sbuf(p + "tch", [128, 128], F32, stack_c)
    ost = E.sbuf(p + "ost", [128, 4 * 128], BF16, stack_c)

    @E.block.gpsimd
    def _(g):
        g.wait_ge(E.sem(p + "ccs"), 1)
        g.wait_ge(E.sem(p + "qst0"), E.cnt[p + "qst0"])
        g.wait_ge(E.sem(p + "qst1"), E.cnt[p + "qst1"])
        for c in range(4):
            if c > 0:
                g.wait_ge(E.sem(p + "mgv"), c)
            E.inc(g.dma_start(out=tmg[:, :], in_=cc_out.ap()[c, :, 1032:2056]), p + "hgl", 16)

    @E.block.vector
    def _(v):
        v.memset(acc[:, :], 0.0)
        for c in range(4):
            v.wait_ge(E.sem(p + "hgl"), 16 * (c + 1))
            inst = v.scalar_tensor_tensor(acc[:, :], tmg[:, :], flgf[:, c:c + 1], acc[:, :],
                                          ALU.mult, ALU.add)
            E.inc(inst, p + "mgv", 1)

    @E.block.vector
    def _(v):
        for h in range(8):
            v.tensor_copy(sk_pad[:, h * 1088: h * 1088 + 64], acc[:, h * 64:(h + 1) * 64])
            inst = v.tensor_copy(v_pad[:, h * 1088: h * 1088 + 64], acc[:, 512 + h * 64: 512 + (h + 1) * 64])
            E.inc(inst, p + "hcv", 2)

    for c in range(3):
        @E.block.gpsimd
        def _(g, c=c):
            if c == 0:
                g.wait_ge(E.sem(p + "qst0"), E.cnt[p + "qst0"])
                g.wait_ge(E.sem(p + "qst1"), E.cnt[p + "qst1"])
            g.wait_ge(E.sem(p + "cmb"), 1 + 8 * c)
            E.inc(g.dma_start(out=hgb2[:, :], in_=cc_out.ap()[c, :, 0:1032]), p + "hgl", 16)

        @E.block.vector
        def _(v, c=c):
            v.wait_ge(E.sem(p + "hgl"), 64 + 16 * (c + 1))
            hv = hgb2.ap().rearrange("pp (h c2) -> pp h c2", h=8)
            inst = v.tensor_scalar(t8[:, :], hv[:, :, 128], flg3[:, c:c + 1], None, ALU.mult)
            E.inc(inst, p + "cmb", 1)

        @E.block.scalar
        def _(s, c=c):
            s.wait_ge(E.sem(p + "cmb"), 2 + 8 * c)
            inst = s.activation(et8[:, :], t8[:, :], AF.Exp, scale=1.0 / GN)
            E.inc(inst, p + "cmb", 1)

        for h in range(8):
            @E.block.vector
            def _(v, c=c, h=h):
                v.wait_ge(E.sem(p + "cmb"), 3 + 8 * c)
                if h > 0:
                    v.wait_ge(E.sem(p + "scg"), 64 + 8 * c + h)
                inst = v.tensor_scalar(slt[:, :], hgb2[:, h * 129: h * 129 + 128],
                                       flg3[:, c:c + 1], None, ALU.mult)
                inst = v.scalar_tensor_tensor(svb[:, :], S_sl[:, h * 128:(h + 1) * 128],
                                              et8[:, h:h + 1], slt[:, :], ALU.mult, ALU.add)
                E.cnt[p + "sch"] += 1
                inst.then_inc(E.sem(p + "sch"), 1)

            @E.block.gpsimd
            def _(g, c=c, h=h):
                g.wait_ge(E.sem(p + "sch"), 8 * c + h + 1)
                inst = g.tensor_copy(S_sl[:, h * 128:(h + 1) * 128], svb[:, :])
                E.inc(inst, p + "scg", 1)

        @E.block.vector
        def _(v, c=c):
            v.wait_ge(E.sem(p + "scg"), 64 + 8 * (c + 1))
            inst = v.tensor_copy(svb[0:1, 0:1], svb[0:1, 0:1])
            E.inc(inst, p + "cmb", 6)

    if getattr(E, "_l0_debug", None) == "P3b":
        _dbg_halt([(p + "hcv", 16), (p + "scg", 88)])
        stack_c.close(); astack.close(); return
    if getattr(E, "_l0_debug", None) == "P3b_h":
        _dbg_halt([(p + "hcv", 16)])
        stack_c.close(); astack.close(); return
    if getattr(E, "_l0_debug", None) == "P3b_s":
        _dbg_halt([(p + "scg", 88)])
        stack_c.close(); astack.close(); return
    if getattr(E, "_l0_debug", None) == "P3b_g":
        _dbg_halt([(p + "hgl", 80)])
        stack_c.close(); astack.close(); return

    t4 = ring.tile
    ev_p4 = E.cnt[ev]
    odr_t = o_dram.rearrange("(qh pp) n -> qh pp n", pp=128)

    @E.block.sync
    def _(s):
        for j in range(2):
            s.wait_ge(E.sem(p + "qst%d" % (j % 2)), E.cnt[p + "qst%d" % (j % 2)])
            E.inc(s.dma_start(out=qt_sb.ap().rearrange("pp (b qh c) -> pp b qh c", b=2, qh=32)[:, j % 2, :, :],
                              in_=qd_dram.rearrange("(qh pp) n -> pp qh n", pp=128)[:, :, j * 128:(j + 1) * 128]),
                  p + "qtl%d" % (j % 2), 16)
            E.inc(s.dma_start(out=sqt_sb.ap().rearrange("pp (b qh c) -> pp b qh c", b=2, qh=32)[:, j % 2, :, :],
                              in_=sq_dram.rearrange("(qh pp) n -> pp qh n", pp=128)[:, :, j * 128:(j + 1) * 128]),
                  p + "qtl%d" % (j % 2), 16)

    for j in range(8):
        for h in range(8):
            idx = j * 8 + h
            tb = t4 + idx * 6

            @E.block.sync
            def _(s, j=j, h=h, idx=idx):
                if idx >= 2:
                    s.wait_ge(E.sem(p + "tpm"), idx - 1)
                    s.wait_ge(E.sem(p + "smm"), 2 * (idx - 1))
                E.inc(s.dma_start(out=kb4.ap().rearrange("pp (b c) -> pp b c", b=2)[:, idx % 2, 0:128],
                                  in_=ktl_dram[h, :, j * 128:(j + 1) * 128]),
                      p + "kcl%d" % (idx % 2), 16)
                E.inc(s.dma_start(out=kb4.ap().rearrange("pp (b c) -> pp b c", b=2)[:, idx % 2, 128:256],
                                  in_=khat_dram[h, :, j * 128:(j + 1) * 128]),
                      p + "kcl%d" % (idx % 2), 16)

            @E.block.tensor
            def _(t, j=j, h=h, idx=idx):
                if idx == 0:
                    t.wait_ge(E.sem(p + "hcv"), 16)
                    t.wait_ge(E.sem(p + "scg"), 88)
                t.wait_ge(E.sem(p + "qtl%d" % (j % 2)), 32 * (j // 2 + 1))
                t.wait_ge(E.sem(p + "kcl%d" % (idx % 2)), 32 * (idx // 2 + 1))
                if idx >= 2:
                    t.wait_ge(E.sem(p + "tpv"), 4 * (idx - 1))
                    t.wait_ge(E.sem(mm), t4 + 6 * (idx - 2) + 6)
                offs = [64 + j * 128, j * 128, j * 128 + 64]
                for z, off in enumerate(offs):
                    t.transpose(psT[0:128, (idx % 2) * 512 + z * 128:(idx % 2) * 512 + (z + 1) * 128],
                                v_pad[:, h * 1088 + off: h * 1088 + off + 128],
                                eye128b[0:128, 0:128])
                inst = t.transpose(psT[0:128, (idx % 2) * 512 + 384:(idx % 2) * 512 + 512],
                                   kb4.ap().rearrange("pp (b c) -> pp b c", b=2)[:, idx % 2, 128:256],
                                   eye128b[0:128, 0:128])
                E.inc(inst, p + "tpm", 1)

            @E.block.vector
            def _(v, idx=idx):
                v.wait_ge(E.sem(p + "tpm"), idx + 1)
                for z in range(3):
                    v.tensor_copy(vtm[:, (idx % 2) * 384 + z * 128:(idx % 2) * 384 + (z + 1) * 128],
                                  psT[0:128, (idx % 2) * 512 + z * 128:(idx % 2) * 512 + (z + 1) * 128])
                inst = v.tensor_copy(ktm[:, (idx % 2) * 128:(idx % 2) * 128 + 128],
                                     psT[0:128, (idx % 2) * 512 + 384:(idx % 2) * 512 + 512])
                E.inc(inst, p + "tpv", 4)
                if idx > 0:
                    v.wait_ge(E.sem(mm), t4 + 6 * (idx - 1) + 5)
                if idx >= 8:
                    v.wait_ge(E.sem(p + "scg"), 88 + idx - 7)
                inst = v.tensor_copy(s_bf[:, :], S_sl[:, h * 128:(h + 1) * 128])
                E.inc(inst, p + "sbf", 1)

            @E.block.tensor
            def _(t, j=j, h=h, idx=idx):
                if idx > 0:
                    t.wait_ge(E.sem(p + "amv"), 4 * idx)
                for qi in range(4):
                    qh = h * 4 + qi
                    inst = t.matmul(psA[0:128, qi * 128:(qi + 1) * 128],
                                    kb4.ap().rearrange("pp (b c) -> pp b c", b=2)[:, idx % 2, 0:128],
                                    qt_sb[:, (j % 2) * 4096 + qh * 128:(j % 2) * 4096 + (qh + 1) * 128],
                                    start=True, stop=True)
                E.inc(inst, p + "smm", 1)

            @E.block.vector
            def _(v, idx=idx):
                v.wait_ge(E.sem(p + "smm"), 2 * idx + 1)
                if idx > 0:
                    v.wait_ge(E.sem(mm), t4 + 6 * (idx - 1) + 5)
                for qi in range(4):
                    inst = v.tensor_tensor(Amb[:, qi * 128:(qi + 1) * 128],
                                           psA[0:128, qi * 128:(qi + 1) * 128],
                                           trif[:, :], ALU.mult)
                E.inc(inst, p + "amv", 4)

            @E.block.tensor
            def _(t, j=j, h=h, idx=idx):
                if idx > 0:
                    t.wait_ge(E.sem(p + "sxp"), idx)
                for qi in range(4):
                    qh = h * 4 + qi
                    for w in range(2):
                        inst = t.matmul(psB[0:128, qi * 128 + w * 64: qi * 128 + (w + 1) * 64],
                                        sk_pad[:, h * 1088 + j * 128 + w * 64: h * 1088 + j * 128 + w * 64 + 128],
                                        sqt_sb[:, (j % 2) * 4096 + qh * 128 + w * 64:(j % 2) * 4096 + qh * 128 + (w + 1) * 64],
                                        start=True, stop=True)
                E.inc(inst, p + "smm", 1)

            @E.block.scalar
            def _(s, idx=idx):
                s.wait_ge(E.sem(p + "smm"), E.cnt[p + "smm"])
                if idx > 0:
                    s.wait_ge(E.sem(p + "msk"), idx)
                inst = s.activation(esw[:, :], psB[0:128, :], AF.Exp, scale=float(1.0 / np.sqrt(128.0)))
                E.inc(inst, p + "sxp", 1)

            @E.block.vector
            def _(v, j=j, idx=idx):
                v.wait_ge(E.sem(p + "sxp"), idx + 1)
                if idx > 0:
                    v.wait_ge(E.sem(p + "rw1"), 9 * (idx - 1) + 1)
                sel = 0 if j == 0 else 128
                for qi in range(4):
                    inst = v.tensor_tensor(e_m[:, qi * 128:(qi + 1) * 128],
                                           esw[:, qi * 128:(qi + 1) * 128],
                                           swam_sb[:, sel:sel + 128], ALU.mult)
                E.inc(inst, p + "msk", 1)

            @E.block.tensor
            def _(t, idx=idx):
                t.wait_ge(E.sem(p + "msk"), idx + 1)
                if idx > 0:
                    t.wait_ge(E.sem(p + "rw1"), 9 * idx)
                for qi in range(4):
                    for w in range(2):
                        c0 = qi * 128 + w * 64
                        inst = t.matmul(psBC[0:1, c0:c0 + 64], onescf[:, 0:1],
                                        e_m[:, c0:c0 + 64], start=True, stop=True)
                E.inc(inst, p + "rw1", 1)

            @E.block.scalar
            def _(s, idx=idx):
                s.wait_ge(E.sem(p + "rw1"), 9 * idx + 1)
                inst = s.activation(rw[0:1, :], psBC[0:1, :], AF.Ln)
                E.inc(inst, p + "rw1", 8)

            @E.block.vector
            def _(v, idx=idx):
                v.wait_ge(E.sem(p + "rw1"), 9 * (idx + 1))
                inst = v.tensor_copy(rwg[0:1, :], rw[0:1, :])
                E.inc(inst, p + "rwv", 1)

            @E.block.scalar
            def _(s, idx=idx):
                s.wait_ge(E.sem(p + "rwv"), idx + 1)
                inst = s.activation(rwr[0:1, :], rwg[0:1, :], AF.Exp, scale=-1.0)
                E.inc(inst, p + "rw2", 1)

            @E.block.tensor
            def _(t, idx=idx):
                t.wait_ge(E.sem(p + "rw2"), idx + 1)
                ps = ring.acquire(t)
                inst = t.matmul(ps[:, :], common["ones_row"][0:1, :], rwr[0:1, :],
                                start=True, stop=True)
                ring.finish_tile(inst)

            @E.block.vector
            def _(v, idx=idx, tb=tb):
                v.wait_ge(E.sem(mm), tb + 1)
                ps = ring.banks[tb % 4]
                inst = v.tensor_tensor(e_n[:, :], e_m[:, :], ps[:, :], ALU.mult)
                E.inc(inst, ev, 1)

            for qi in range(4):
                qh = h * 4 + qi

                @E.block.tensor
                def _(t, j=j, h=h, qi=qi, qh=qh, idx=idx, tb=tb):
                    if qi == 0:
                        t.wait_ge(E.sem(ev), ev_p4 + 6 * idx + 1)
                        t.wait_ge(E.sem(p + "amv"), 4 * (idx + 1))
                        t.wait_ge(E.sem(p + "tpv"), 4 * (idx + 1))
                        t.wait_ge(E.sem(p + "sbf"), idx + 1)
                    ps = ring.acquire(t)
                    b = (idx % 2) * 384
                    t.matmul(ps[:, 0:128], vtm[:, b:b + 128], Amb[:, qi * 128:(qi + 1) * 128],
                             start=True, stop=False)
                    t.matmul(ps[:, 0:128], s_bf[:, :],
                             qt_sb[:, (j % 2) * 4096 + qh * 128:(j % 2) * 4096 + (qh + 1) * 128],
                             start=False, stop=False)
                    t.matmul(ps[:, 0:64], vtm[:, b + 128:b + 256], e_n[:, qi * 128: qi * 128 + 64],
                             start=False, stop=True)
                    inst = t.matmul(ps[:, 64:128], vtm[:, b + 256:b + 384],
                                    e_n[:, qi * 128 + 64:(qi + 1) * 128], start=False, stop=True)
                    ring.finish_tile(inst)

                @E.block.vector
                def _(v, j=j, qh=qh, qi=qi, idx=idx, tb=tb):
                    v.wait_ge(E.sem(mm), tb + 2 + qi)
                    if idx > 0:
                        v.wait_ge(E.sem(p + "ost%d" % qi), 16 * idx)
                    ps = ring.banks[(tb + 1 + qi) % 4]
                    inst = v.tensor_scalar(ost[:, qi * 128:(qi + 1) * 128], ps[:, 0:128],
                                           0.5, None, ALU.mult)
                    E.inc(inst, ev, 1)

                @E.block.sync
                def _(s, j=j, qh=qh, qi=qi, idx=idx, tb=tb):
                    s.wait_ge(E.sem(ev), tb + 2 + qi)
                    E.inc(s.dma_start(out=odr_t[qh, :, j * 128:(j + 1) * 128],
                                      in_=ost[:, qi * 128:(qi + 1) * 128]), p + "ost%d" % qi, 16)

            @E.block.tensor
            def _(t, j=j, h=h, idx=idx, tb=tb):
                ps = ring.acquire(t)
                inst = t.matmul(ps[:, 0:128], ktm[:, (idx % 2) * 128:(idx % 2) * 128 + 128],
                                vtm[:, (idx % 2) * 384:(idx % 2) * 384 + 128], start=True, stop=True)
                ring.finish_tile(inst)

            @E.block.vector
            def _(v, j=j, h=h, idx=idx, tb=tb):
                v.wait_ge(E.sem(mm), tb + 6)
                ps = ring.banks[(tb + 5) % 4]
                inst = v.scalar_tensor_tensor(svb[:, :], S_sl[:, h * 128:(h + 1) * 128],
                                              etotb[:, h * 8 + j: h * 8 + j + 1],
                                              ps[:, 0:128], ALU.mult, ALU.add)
                E.inc(inst, ev, 1)

            @E.block.gpsimd
            def _(g, h=h, idx=idx, tb=tb):
                g.wait_ge(E.sem(ev), tb + 6 - t4 + ev_p4)
                inst = g.tensor_copy(S_sl[:, h * 128:(h + 1) * 128], svb[:, :])
                E.inc(inst, p + "scg", 1)

        @E.block.sync
        def _(s, j=j):
            jn = j + 2
            if jn < 8:
                s.wait_ge(E.sem(ev), t4 + (j + 1) * 48)
                E.inc(s.dma_start(out=qt_sb.ap().rearrange("pp (b qh c) -> pp b qh c", b=2, qh=32)[:, jn % 2, :, :],
                                  in_=qd_dram.rearrange("(qh pp) n -> pp qh n", pp=128)[:, :, jn * 128:(jn + 1) * 128]),
                      p + "qtl%d" % (jn % 2), 16)
                E.inc(s.dma_start(out=sqt_sb.ap().rearrange("pp (b qh c) -> pp b qh c", b=2, qh=32)[:, jn % 2, :, :],
                                  in_=sq_dram.rearrange("(qh pp) n -> pp qh n", pp=128)[:, :, jn * 128:(jn + 1) * 128]),
                      p + "qtl%d" % (jn % 2), 16)

    if getattr(E, "_l0_debug", None) == "P4":
        _dbg_halt([(p + "ost0", E.cnt[p + "ost0"]), (p + "ost1", E.cnt[p + "ost1"]),
                   (p + "ost2", E.cnt[p + "ost2"]), (p + "ost3", E.cnt[p + "ost3"])])
        stack_c.close(); astack.close(); return

    total_mm = ring.tile

    @E.block.sync
    def _(s):
        for qi in range(4):
            s.wait_ge(E.sem(p + "ost%d" % qi), E.cnt[p + "ost%d" % qi])
        s.wait_ge(E.sem(mm), total_mm)
        E.inc(s.dma_start(out=xl[:, :], in_=o_dram.rearrange("(kt pp) n -> pp kt n", pp=128)),
              p + "old", 16)

    @E.block.gpsimd
    def _(g):
        for qi in range(4):
            g.wait_ge(E.sem(p + "ost%d" % qi), E.cnt[p + "ost%d" % qi])
        g.wait_ge(E.sem(mm), total_mm)

    stack_c.close()
    astack.close()
    emit_wstream_proj(E, p + "wo_", xin, xout, W["wo"], xl,
                      lambda kt, th: kt * NLOC + th * TH, KT, 8, common,
                      bias_sb=None, ready=((p + "old", 16),))


def build_l0attn(x_arrs, inp):
    E = Env()
    W = _l0attn_host(inp)
    with E.stack:
        xin = E.din("xin", x_arrs)
        Wd = {k: E.din("w_" + k, v) for k, v in W.items()}
        eye_d = E.din("eye_d", eye_groups())
        e128 = np.eye(128, dtype=np.float32)
        eye128b_d = E.din("eye128b_d", e128.astype(BF))
        xout = E.dout("xout", [D, NLOC], F32)

        common = _common_alloc(E)
        common["eye128b"] = E.sbuf("eye128b", [128, 128], BF16)
        xl = E.sbuf("xl", [128, KT * NLOC], BF16)
        lnstack = ExitStack()
        sq = E.sbuf("sq", [128, 2 * NLOC], BF16, lnstack)
        stat = E.sbuf("stat", [1, 4 * NLOC], F32, lnstack)
        rstd_bc = E.sbuf("rstd_bc", [128, NLOC], BF16, lnstack)
        mbc_sb = E.sbuf("mbc_sb", [128, NLOC], F32, lnstack)
        mid = E.sbuf("mid", [128, TH], F32, lnstack)
        rbf_row = E.sbuf("rbf_row", [1, NLOC], BF16, lnstack)
        astack = ExitStack()

        with E.nc.Block() as block:
            E.block = block
            emit_preamble(E, common, eye_d)

            @E.block.sync
            def _(s):
                E.inc(s.dma_start(out=common["eye128b"][:, :], in_=eye128b_d[:, :]), "preld", 16)

            xl_ready = emit_ln(E, "ln_", xin, xl, sq, stat, rstd_bc, mbc_sb, mid, rbf_row,
                               common["ones_col"], common["ones_row"],
                               common["psA"], common["psB"], common["psBC"], common["eps_col"])
            lnstack.close()
            emit_l0attn(E, "a0_", xin, xout, Wd, xl_ready, xl, common, astack)
        return E


def run_l0attn(x, inp):
    E = build_l0attn(shard_x(x), inp)
    res = E.run()
    return unshard_x([res.results[c]["xout"] for c in range(NCORES)])



def kernel(**inputs):
    inputs = {k: np.asarray(v) for k, v in inputs.items()}
    arrs = shard_x(np.asarray(inputs["L0_attn_hidden_states"], dtype=np.float32))
    sel_arrs = shard_sel(threefry_mask_sel())

    builders = [
        lambda a: build_l0attn(a, inputs),
        lambda a: build_l0ffn(a, inputs),
        lambda a: build_l1attn(a, inputs),
        lambda a: build_l1ffn(a, inputs),
        lambda a: build_l2attn(a, inputs),
        lambda a: build_l2ffn(a, inputs, sel_arrs),
        lambda a: build_l3attn(a, inputs),
        lambda a: build_l3ffn(a, inputs),
    ]
    for bld in builders:
        E = bld(arrs)
        res = E.run()
        arrs = [np.ascontiguousarray(np.asarray(res.results[c]["xout"], dtype=np.float32))
                for c in range(NCORES)]
    return unshard_x(arrs)
